# version 11
# speedup vs baseline: 1.3425x; 1.0914x over previous
import numpy as np
from contextlib import ExitStack

import concourse.bass as bass
import concourse.mybir as mybir
from concourse.bass_utils import run_bass_kernel_spmd
from concourse.alu_op_type import AluOpType as OP

F32 = mybir.dt.float32
BF16 = mybir.dt.bfloat16
AF = mybir.ActivationFunctionType
AX = mybir.AxisListType

NCORES = 8
T = 8192
S = T // NCORES
DM = 2048
NEG = -30000.0
ISC = 0.08838834764831845

ENGS = ("tensor", "vector", "scalar", "gpsimd", "sync")


class Sem:
    __slots__ = ("name", "v", "h")

    def __init__(self, name):
        self.name, self.v, self.h = name, 0, None

    def bump(self, n):
        self.v += n
        return self.v


class Ctx:
    def __init__(self):
        self.ops = {e: [] for e in ENGS}
        self.sems = []
        self.dma_sems = {}

    def sem(self, name):
        s = Sem(name)
        self.sems.append(s)
        return s

    def emit(self, eng, fn, fusable=True):
        fn._fusable = fusable
        self.ops[eng].append(fn)

    def wait(self, eng, sem, val=None):
        v = sem.v if val is None else val
        if v <= 0:
            return
        self.emit(eng, lambda E, s=sem, v=v: E.wait_ge(s.h, v), fusable=False)

    def inc(self, eng, sem, n=1):
        v = sem.bump(n)
        ops = self.ops[eng]
        if ops and getattr(ops[-1], "_fusable", False):
            prev = ops[-1]

            def fused(E, prev=prev, s=sem, n=n):
                r = prev(E)
                r.then_inc(s.h, n)
            fused._fusable = False
            ops[-1] = fused
        else:
            def fn(E, s=sem, n=n):
                E.drain()
                E.sem_inc(s.h, n)
            fn._fusable = False
            self.emit(eng, fn, fusable=False)
        return v

    def dma(self, q, out, in_, sem):
        self.dma_sems[sem] = q
        if sem.v > 0:
            self.wait(q, sem, sem.v)
        v = sem.bump(16)
        self.emit(q, lambda E, o=out, i=in_, s=sem: E.dma_start(out=o, in_=i).then_inc(s.h, 16),
                  fusable=False)
        return v

    def barrier(self, name):
        for ds, q in list(self.dma_sems.items()):
            if ds.v > 0:
                self.wait(q, ds, ds.v)
        s = self.sem("bar_" + name)

        def fn(E, s=s):
            E.drain()
            E.sem_inc(s.h, 1)
        for e in ENGS:
            self.emit(e, fn)
        s.v = len(ENGS)
        for e in ENGS:
            self.wait(e, s, len(ENGS))


def replay(nc, ctx):
    with ExitStack() as st:
        for s in ctx.sems:
            s.h = st.enter_context(nc.semaphore(s.name))
        block = st.enter_context(nc.Block())

        def mk(e):
            def body(E):
                for fn in ctx.ops[e]:
                    fn(E)
            return body

        for e in ENGS:
            getattr(block, e)(mk(e))


class WStream:
    def __init__(self, ctx, nc, st_, name, cols_max, nbuf=3):
        self.ctx, self.name, self.nbuf = ctx, name, nbuf
        self.bufs = [st_.enter_context(nc.sbuf_tensor(f"{name}{i}", [128, cols_max], BF16))
                     for i in range(nbuf)]
        self.ld = ctx.sem(name + "_ld")
        self.free = ctx.sem(name + "_free")
        self.n_loaded = 0
        self.n_used = 0
        self.n_emitted = 0
        self.pending = []

    def load(self, dram_ap):
        if self.n_loaded - len(self.pending) < self.nbuf and not self.pending:
            self._emit_load(dram_ap)
        else:
            self.pending.append(dram_ap)
        self.n_loaded += 1

    def _emit_load(self, dram_ap):
        buf = self.bufs[self.n_emitted % self.nbuf]
        cols = dram_ap.shape[-1]
        self.ctx.dma("gpsimd", buf[:, 0:cols], dram_ap, self.ld)
        self.n_emitted += 1

    def advance(self, k, gate_sem, gate_val):
        self.n_used += k
        if self.pending:
            self.ctx.wait("gpsimd", gate_sem, gate_val)
            for _ in range(min(k, len(self.pending))):
                self._emit_load(self.pending.pop(0))

    def use(self, eng):
        i = self.n_used
        self.ctx.wait(eng, self.ld, 16 * (i + 1))
        return self.bufs[i % self.nbuf]

    def done(self, eng):
        self.n_used += 1
        self.ctx.inc(eng, self.free, 1)

    def peek(self, eng, k):
        i = self.n_used + k
        self.ctx.wait(eng, self.ld, 16 * (i + 1))
        return self.bufs[i % self.nbuf]

    def release(self, eng, k):
        self.n_used += k
        self.ctx.inc(eng, self.free, k)

    def free_via(self, eng, k, gate_sem, gate_val):
        self.n_used += k
        self.ctx.wait(eng, gate_sem, gate_val)
        self.ctx.inc(eng, self.free, k)


def _prep_weights(w):
    C = np.ascontiguousarray
    p = {}
    p["inpw"] = C(w["L0_attn_in_proj_w"].reshape(64, 128, 16, 128)
                  .transpose(0, 3, 2, 1)).reshape(64, 128, 2048)
    p["outw"] = C(w["L0_attn_out_proj_w"].reshape(16, 128, 32, 128)
                  .transpose(0, 3, 2, 1)).reshape(16, 128, 4096)
    p["xpw"] = C(w["L0_attn_x_proj_w"].reshape(160, 32, 128)
                 .transpose(2, 1, 0)).reshape(128, 32 * 160)
    p["dtw"] = C(w["L0_attn_dt_proj_w"].reshape(32, 128, 128)
                 .transpose(2, 0, 1)).reshape(128, 32 * 128)
    p["convw"] = C(w["L0_attn_conv_w"].reshape(32, 128, 4)
                   .transpose(1, 0, 2)).reshape(128, 128)
    p["convb"] = C(w["L0_attn_conv_b"].reshape(32, 128).T)
    p["dtb"] = C(w["L0_attn_dt_proj_b"].reshape(32, 128).T)
    p["dvec"] = C(w["L0_attn_D"].reshape(32, 128).T)
    p["alog"] = C(w["L0_attn_A_log"].reshape(32, 128, 16)
                  .transpose(1, 0, 2)).reshape(128, 512)
    p["nsc"] = w["L0_attn_norm_scale"].reshape(1, 2048)
    p["ffnw"] = C(w["L0_ffn_weight"].T.reshape(16, 128, 2048)
                  .transpose(1, 0, 2)).reshape(128, 16 * 2048)
    for nm, key in (("wq", "L1_attn_Wq"), ("wk", "L1_attn_Wk"), ("wv", "L1_attn_Wv")):
        p[nm] = C(w[key].reshape(16, 16, 128, 128).transpose(0, 2, 1, 3)).reshape(16, 128, 2048)
    p["wo"] = C(w["L1_attn_Wo"].reshape(16, 128, 16, 128)
                .transpose(2, 1, 0, 3)).reshape(16, 128, 2048)
    p["bq"] = C(w["L1_attn_bq"].T)
    p["bk"] = C(w["L1_attn_bk"].T)
    p["bv"] = C(w["L1_attn_bv"]).reshape(1, 2048)
    p["bo"] = w["L1_attn_bo"].reshape(1, 2048)
    p["w1"] = C(w["L1_ffn_W1"].reshape(8, 16, 128, 11, 128)
                .transpose(0, 3, 2, 1, 4)).reshape(8, 11, 128, 2048)
    p["w3"] = C(w["L1_ffn_W3"].reshape(8, 16, 128, 11, 128)
                .transpose(0, 3, 2, 1, 4)).reshape(8, 11, 128, 2048)
    p["w2"] = C(w["L1_ffn_W2"].reshape(8, 11, 128, 16, 128)
                .transpose(0, 3, 2, 1, 4)).reshape(8, 16, 128, 1408)
    p["sw1"] = C(w["L1_ffn_sw1"].reshape(22, 128, 16, 128)
                 .transpose(0, 3, 2, 1)).reshape(22, 128, 2048)
    p["sw3"] = C(w["L1_ffn_sw3"].reshape(22, 128, 16, 128)
                 .transpose(0, 3, 2, 1)).reshape(22, 128, 2048)
    p["sw2"] = C(w["L1_ffn_sw2"].reshape(16, 128, 22, 128)
                 .transpose(0, 3, 2, 1)).reshape(16, 128, 2816)
    p["gatew"] = C(w["L1_ffn_gate_w"].T.reshape(16, 128, 8)
                   .transpose(1, 0, 2)).reshape(128, 128)
    p["rbias"] = w["L1_ffn_router_bias"].reshape(1, 8)
    return p


def _prep_core(c, x):
    d = {}
    t0 = c * S
    d["x_sh"] = np.ascontiguousarray(x[0, t0:t0 + S, :])
    d["xprev"] = (np.ascontiguousarray(x[0, t0 - 128:t0, :]) if c > 0
                  else np.zeros((128, DM), np.float32))
    sel = np.zeros((128, 8), np.float32)
    sel[:, c] = 1.0
    d["selmask"] = sel
    bt = np.zeros((128, 64), np.float32)
    for jg in range(64):
        if jg // 8 >= c:
            bt[:, jg] = NEG
    d["bias_tab"] = bt
    return d


def _structural():
    s = {"eye": np.eye(128, dtype=np.float32)}
    rsel = np.zeros((32, 32 * 128), np.float32)
    for n in range(32):
        rsel[n, n * 128:(n + 1) * 128] = 1.0
    s["rsel"] = rsel
    esel = np.zeros((8, 8 * 128), np.float32)
    for e in range(8):
        esel[e, e * 128:(e + 1) * 128] = 1.0
    s["esel"] = esel
    m = np.zeros((128, 8 * 1024), np.float32)
    pp = np.arange(128)[:, None]
    tt = np.arange(1024)[None, :]
    for l in range(8):
        m[:, l * 1024:(l + 1) * 1024] = (l * 128 + pp <= tt).astype(np.float32)
    s["lmask"] = m
    return s


def emit_ln(ctx, b, xin, xout, scr, stat, d=2048, eps=1e-5):
    mean, rstd, msq, bias = stat[:, 0:1], stat[:, 1:2], stat[:, 2:3], stat[:, 3:4]
    ctx.emit("scalar", lambda E: E.activation(scr[:, 0:d], xin, AF.Square))
    vS = ctx.inc("scalar", b.sS, 1)
    ctx.wait("vector", b.sS, vS)
    ctx.emit("vector", lambda E: E.tensor_scalar(scr[:, 0:d], scr[:, 0:d], 1.0, None,
                                                 op0=OP.mult, op1=OP.add, accum_out=rstd))
    ctx.emit("vector", lambda E: E.tensor_copy(stat[:, 4:8], stat[:, 0:4]))
    ctx.emit("vector", lambda E: E.tensor_scalar(scr[:, 0:d], xin, 1.0, None,
                                                 op0=OP.mult, op1=OP.add, accum_out=mean))
    ctx.emit("vector", lambda E: E.tensor_copy(stat[:, 4:8], stat[:, 0:4]))
    ctx.emit("vector", lambda E: E.tensor_scalar(rstd, rstd, 1.0 / d, None, op0=OP.mult))
    ctx.emit("vector", lambda E: E.tensor_scalar(mean, mean, 1.0 / d, None, op0=OP.mult))
    ctx.emit("vector", lambda E: E.tensor_tensor(msq, mean, mean, op=OP.mult))
    ctx.emit("vector", lambda E: E.tensor_tensor(rstd, rstd, msq, op=OP.subtract))
    ctx.emit("vector", lambda E: E.tensor_scalar(rstd, rstd, 1.0, eps, op0=OP.mult, op1=OP.add))
    vV = ctx.inc("vector", b.sV, 1)
    ctx.wait("scalar", b.sV, vV)
    ctx.emit("scalar", lambda E: E.activation(rstd, rstd, AF.Ln))
    ctx.emit("scalar", lambda E: E.activation(rstd, rstd, AF.Exp, scale=-0.5))
    vS2 = ctx.inc("scalar", b.sS, 1)
    ctx.wait("vector", b.sS, vS2)
    ctx.emit("vector", lambda E: E.tensor_copy(stat[:, 4:8], stat[:, 0:4]))
    ctx.emit("vector", lambda E: E.tensor_scalar(bias, mean, rstd[:, 0:1], -1.0,
                                                 op0=OP.mult, op1=OP.mult))
    vV2 = ctx.inc("vector", b.sV, 1)
    ctx.wait("scalar", b.sV, vV2)
    ctx.emit("scalar", lambda E: E.activation(xout, xin, AF.Identity, scale=rstd, bias=bias))
    return ctx.inc("scalar", b.sS, 1)


def emit_tr_strip(ctx, b, src_bf, nblk, col_fn, dst, psum):
    for i in range(nblk):
        q = i % 4
        ctx.emit("tensor", lambda E, i=i, q=q: E.matmul(
            psum[:, q * 128:(q + 1) * 128], src_bf[:, i * 128:(i + 1) * 128],
            b.eye_b[:, :], is_transpose=True, start=True, stop=True))
        if q == 3 or i == nblk - 1:
            vt = ctx.inc("tensor", b.sT, 1)
            ctx.wait("scalar", b.sT, vt)
            for qq in range(q + 1):
                i0 = i - q + qq
                ctx.emit("scalar", lambda E, i0=i0, qq=qq: E.activation(
                    dst[:, col_fn(i0):col_fn(i0) + 128],
                    psum[:, qq * 128:(qq + 1) * 128], AF.Identity))
            ss = ctx.inc("scalar", b.sS, 1)
            ctx.wait("tensor", b.sS, ss)
    return b.sS.v


def build():
    nc = bass.Bass("TRN2", target_bir_lowering=False, debug=False,
                   detect_race_conditions=False)
    ctx = Ctx()
    dram = {}

    def din(n, sh, dt=F32):
        dram[n] = nc.dram_tensor(n, list(sh), dt, kind="ExternalInput").ap()

    def dout(n, sh, dt=F32):
        dram[n] = nc.dram_tensor(n, list(sh), dt, kind="ExternalOutput").ap()

    import os as _os
    _dbg = set((_os.environ.get("MK_DBG") or "").split(",")) - {""}

    def dint(n, sh, dt=F32):
        if n in _dbg:
            dram[n] = nc.dram_tensor(n, list(sh), dt, kind="ExternalOutput").ap()
        else:
            dram[n] = nc.dram_tensor(n, list(sh), dt).ap()

    din("x_sh", (S, DM)); din("xprev", (128, DM))
    din("selmask", (128, 8)); din("bias_tab", (128, 64))
    din("eye", (128, 128)); din("lmask", (128, 8 * 1024))
    din("rsel", (32, 32 * 128)); din("esel", (8, 8 * 128))
    din("inpw", (64, 128, 2048)); din("outw", (16, 128, 4096))
    din("xpw", (128, 32 * 160)); din("dtw", (128, 32 * 128))
    din("convw", (128, 128)); din("convb", (128, 32)); din("dtb", (128, 32))
    din("dvec", (128, 32)); din("alog", (128, 512)); din("nsc", (1, 2048))
    din("ffnw", (128, 16 * 2048))
    din("wq", (16, 128, 2048)); din("wk", (16, 128, 2048)); din("wv", (16, 128, 2048))
    din("wo", (16, 128, 2048))
    din("bq", (128, 16)); din("bk", (128, 16)); din("bv", (1, 2048)); din("bo", (1, 2048))
    din("w1", (8, 11, 128, 2048)); din("w3", (8, 11, 128, 2048)); din("w2", (8, 16, 128, 1408))
    din("sw1", (22, 128, 2048)); din("sw3", (22, 128, 2048)); din("sw2", (16, 128, 2816))
    din("gatew", (128, 128)); din("rbias", (1, 8))
    dout("out", (S, DM))
    dint("hx1", (S, DM)); dint("hx2", (S, DM)); dint("hx3", (S, DM))
    dint("hxn0", (S, DM))
    dint("hsz", (128, 32 * S), BF16); dint("hxc", (128, 32 * S), BF16)
    dint("hy0", (128, 32 * S), BF16); dint("hcumd", (128, 32 * S), BF16)
    dint("hy2", (128, 32 * S), BF16)
    dint("hwqt", (128, 16 * 2048), BF16)
    dint("cc_in", (1, 128 * 1024)); dint("cc_out", (8, 128 * 1024))
    if "hha" in _dbg:
        dram["hha"] = nc.dram_tensor("hha", [128, 1], F32, kind="ExternalOutput").ap()
    if "hxin" in _dbg:
        dram["hxin"] = nc.dram_tensor("hxin", [128, 32 * 1027], BF16, kind="ExternalOutput").ap()
    if "hxq" in _dbg:
        dram["hxq"] = nc.dram_tensor("hxq", [S, DM], BF16, kind="ExternalOutput").ap()
    if "hffn" in _dbg:
        dram["hffn"] = nc.dram_tensor("hffn", [S, DM], F32, kind="ExternalOutput").ap()
    if "hbst" in _dbg:
        dram["hbst"] = nc.dram_tensor("hbst", [128, 8], F32, kind="ExternalOutput").ap()
        dram["hvq"] = nc.dram_tensor("hvq", [128, 2048], F32, kind="ExternalOutput").ap()
        dram["hcur"] = nc.dram_tensor("hcur", [128, 2048], F32, kind="ExternalOutput").ap()
    dint("kv_in", (1, 128 * 32768), BF16); dint("kv_out", (8, 128 * 32768), BF16)

    class B:
        pass
    b = B()
    with ExitStack() as top:
        sb = lambda n, sh, dt: top.enter_context(nc.sbuf_tensor(n, sh, dt))
        b.eye_b = sb("eye_b", [128, 128], BF16)
        b.eye_f = sb("eye_f", [128, 128], F32)
        b.ones_b = sb("ones_b", [1, 128], BF16)
        b.ones_f = sb("ones_f", [1, 128], F32)
        b.onescol_b = sb("onescol_b", [128, 1], BF16)
        b.onescol_f = sb("onescol_f", [128, 1], F32)
        b.ha = sb("ha", [128, 1], F32)
        b.stat = sb("stat", [128, 8], F32)
        b.scr = sb("scr", [128, 2052], F32)
        b.sT = ctx.sem("sT"); b.sV = ctx.sem("sV"); b.sS = ctx.sem("sS")
        b.sG = ctx.sem("sG"); b.stx = ctx.sem("stx")
        ld0 = ctx.sem("ld0")
        ld0g = ctx.sem("ld0g")
        ctx.dma("sync", b.eye_f[:, :], dram["eye"], ld0)
        ctx.dma("gpsimd", b.eye_b[:, :], dram["eye"], ld0g)
        ctx.emit("gpsimd", lambda E: E.memset(b.ones_b[:, :], 1.0))
        ctx.emit("gpsimd", lambda E: E.memset(b.ones_f[:, :], 1.0))
        ctx.emit("gpsimd", lambda E: E.memset(b.onescol_b[:, :], 1.0))
        ctx.emit("gpsimd", lambda E: E.memset(b.onescol_f[:, :], 1.0))
        ctx.emit("gpsimd", lambda E: E.memset(b.stat[:, :], 0.0))
        ctx.emit("gpsimd", lambda E: E.memset(b.scr[:, :], 0.0))
        gi = ctx.inc("gpsimd", b.sG, 1)
        for e in ("tensor", "vector", "scalar"):
            ctx.wait(e, ld0, 16)
            ctx.wait(e, ld0g, 16)
            ctx.wait(e, b.sG, gi)

        import os
        nph = int(os.environ.get("MK_PHASES", "5"))
        emit_p0(ctx, nc, b, dram)
        ctx.barrier("p0")
        if nph >= 2:
            emit_p1(ctx, nc, b, dram)
            ctx.barrier("p1")
        if nph >= 3:
            emit_p2(ctx, nc, b, dram)
            ctx.barrier("p2")
        if nph >= 4:
            emit_p3(ctx, nc, b, dram)
            ctx.barrier("p3")
        if nph >= 5:
            emit_p4(ctx, nc, b, dram)

        replay(nc, ctx)
    return nc


def emit_p0(ctx, nc, b, dram):
    with ExitStack() as st_:
        sbl = lambda n, sh, dt: st_.enter_context(nc.sbuf_tensor(n, sh, dt))
        slab = [sbl(f"q_slab{i}", [128, 2048], F32) for i in range(2)]
        absw = sbl("q_absw", [128, 2048], F32)
        sgn = sbl("q_sgn", [128, 2048], F32)
        ind = sbl("q_ind", [128, 2048], F32)
        wqb = sbl("q_wqb", [128, 2048], BF16)
        junk = sbl("q_junk", [128, 2048], BF16)
        acc = sbl("q_acc", [128, 18], F32)
        ld = ctx.sem("q_ld"); fr = ctx.sem("q_fr")
        for i in range(16):
            if i >= 2:
                ctx.wait("sync", fr, i - 1)
            ctx.dma("sync", slab[i % 2][:, :], dram["ffnw"][:, i * 2048:(i + 1) * 2048], ld)
        for i in range(16):
            ctx.wait("vector", ld, 16 * (i + 1))
            ctx.emit("vector", lambda E, i=i: E.scalar_tensor_tensor(
                absw[:, :], slab[i % 2][:, :], -1.0, slab[i % 2][:, :], op0=OP.mult, op1=OP.max))
            ctx.inc("vector", fr, 1)
            ctx.emit("vector", lambda E, i=i: E.tensor_scalar(
                junk[:, :], absw[:, :], 1.0, None, op0=OP.mult, op1=OP.add,
                accum_out=acc[:, i:i + 1]))
        ctx.emit("vector", lambda E: E.tensor_copy(junk[:, 0:16], absw[:, 0:16]))
        ctx.emit("vector", lambda E: E.tensor_reduce(acc[:, 16:17], acc[:, 0:16], AX.X, OP.add))
        vv = ctx.inc("vector", b.sV, 1)
        with nc.psum_tensor("q_ps", [128, 8], F32) as ps:
            ctx.wait("tensor", b.sV, vv)
            ctx.emit("tensor", lambda E: E.matmul(ps[0:1, 0:1], b.onescol_f[:, :],
                                                  acc[:, 16:17], start=True, stop=True))
            vt = ctx.inc("tensor", b.sT, 1)
            ctx.wait("vector", b.sT, vt)
            ctx.emit("vector", lambda E: E.tensor_scalar(
                acc[0:1, 17:18], ps[0:1, 0:1], 1.0 / (2048.0 * 2048.0), 1e-8,
                op0=OP.mult, op1=OP.max))
            vv2 = ctx.inc("vector", b.sV, 1)
            ctx.wait("tensor", b.sV, vv2)
            ctx.emit("tensor", lambda E: E.matmul(ps[:, 1:2], b.ones_f[0:1, :],
                                                  acc[0:1, 17:18], start=True, stop=True))
            vt2 = ctx.inc("tensor", b.sT, 1)
            ctx.wait("vector", b.sT, vt2)
            ctx.emit("vector", lambda E: E.tensor_scalar(b.ha[:, :], ps[:, 1:2], 0.5, None,
                                                         op0=OP.mult))
            ctx.inc("vector", b.sV, 1)
        ld2 = ctx.sem("q_ld2"); fr2 = ctx.sem("q_fr2"); stq = ctx.sem("q_st")
        def load2(i):
            if i >= 2:
                ctx.wait("sync", fr2, i - 1)
            ctx.dma("sync", slab[i % 2][:, :], dram["ffnw"][:, i * 2048:(i + 1) * 2048], ld2)
        load2(0)
        load2(1)
        pv_vv = []
        for i in range(16):
            ctx.wait("scalar", ld2, 16 * (i + 1))
            if i >= 1:
                ctx.wait("scalar", b.sV, pv_vv[i - 1])
            ctx.emit("scalar", lambda E, i=i: E.activation(sgn[:, :], slab[i % 2][:, :], AF.Sign))
            ss = ctx.inc("scalar", b.sS, 1)
            ctx.wait("vector", ld2, 16 * (i + 1))
            ctx.emit("vector", lambda E, i=i: E.scalar_tensor_tensor(
                absw[:, :], slab[i % 2][:, :], -1.0, slab[i % 2][:, :], op0=OP.mult, op1=OP.max))
            ctx.emit("vector", lambda E: E.tensor_scalar(
                ind[:, :], absw[:, :], b.ha[:, 0:1], None, op0=OP.is_ge))
            ctx.wait("vector", b.sS, ss)
            ctx.emit("vector", lambda E: E.tensor_tensor(wqb[:, :], sgn[:, :], ind[:, :],
                                                         op=OP.mult))
            ctx.inc("vector", fr2, 1)
            vv3 = ctx.inc("vector", b.sV, 1)
            pv_vv.append(vv3)
            ctx.wait("sync", b.sV, vv3)
            ctx.dma("sync", dram["hwqt"][:, i * 2048:(i + 1) * 2048], wqb[:, :], stq)
            if i + 2 < 16:
                load2(i + 2)
            ctx.wait("vector", stq, stq.v)
        ctx.wait("sync", stq, 256)
        if "hha" in dram:
            ctx.wait("sync", b.sV, b.sV.v)
            ctx.dma("sync", dram["hha"], b.ha[:, :], stq)


def emit_p1(ctx, nc, b, dram):
    import os as _os
    p1s = int(_os.environ.get("MK_P1S", "9"))
    hsz, hxc, hy0 = dram["hsz"], dram["hxc"], dram["hy0"]
    hcumd, hy2 = dram["hcumd"], dram["hy2"]
    with ExitStack() as st_:
        sbl = lambda n, sh, dt: st_.enter_context(nc.sbuf_tensor(n, sh, dt))
        ps = lambda n, sh, dt: st_.enter_context(nc.psum_tensor(n, sh, dt))
        convw_s = sbl("m_convw", [128, 128], F32)
        convb_s = sbl("m_convb", [128, 32], F32)
        dtb_s = sbl("m_dtb", [128, 32], F32)
        dvec_s = sbl("m_dvec", [128, 32], F32)
        a_s = sbl("m_a", [128, 512], F32)
        etot = sbl("m_etot", [128, 512], F32)
        hlast = sbl("m_hlast", [128, 512], F32)
        sdelta = sbl("m_sdelta", [128, 32], F32)
        bcrows_bf = sbl("m_bcbf", [32, 1024], BF16)
        hin = sbl("m_hin", [128, 512], F32)
        ldp = ctx.sem("m_ldp")
        ldpg = ctx.sem("m_ldpg")
        ctx.dma("sync", convw_s[:, :], dram["convw"], ldp)
        ctx.dma("sync", convb_s[:, :], dram["convb"], ldp)
        ctx.dma("sync", dtb_s[:, :], dram["dtb"], ldp)
        ctx.dma("sync", dvec_s[:, :], dram["dvec"], ldp)
        ctx.dma("sync", a_s[:, :], dram["alog"], ldp)
        ldp_all = ldp.v
        ctx.wait("scalar", ldp, 80)
        ctx.emit("scalar", lambda E: E.activation(a_s[:, :], a_s[:, :], AF.Exp))
        sA = ctx.inc("scalar", b.sS, 1)
        ctx.wait("vector", b.sS, sA)
        ctx.emit("vector", lambda E: E.tensor_scalar(a_s[:, :], a_s[:, :], -1.0, None,
                                                     op0=OP.mult))
        ctx.inc("vector", b.sV, 1)

        with ExitStack() as sc:
            sb2 = lambda n, sh, dt: sc.enter_context(nc.sbuf_tensor(n, sh, dt))
            ps1 = lambda n, sh, dt: sc.enter_context(nc.psum_tensor(n, sh, dt))
            xt = [sb2(f"m_xt{i}", [128, 2048], F32) for i in range(2)]
            xbf = sb2("m_xbf", [128, 2048], BF16)
            nsc_s = sb2("m_nsc", [128, 2048], BF16)
            nscrow = sb2("m_nscrow", [1, 2048], BF16)
            ldn = ctx.sem("m_ldn")
            ctx.dma("gpsimd", nscrow[:, :], dram["nsc"], ldn)
            xnmT = sb2("m_xnmT", [128, 16 * 1152], BF16)
            xin = sb2("m_xin", [128, 32 * 1027], BF16)
            szt = sb2("m_szt", [128, 1024], BF16)
            trp = ps1("m_trp", [128, 512], BF16)
            pse = [ps1(f"m_pse{i}", [128, 512], F32) for i in range(4)]
            psh = ps1("m_psh", [128, 128], F32)
            ctx.wait("tensor", ldn, 16)
            for j in range(4):
                ctx.emit("tensor", lambda E, j=j: E.matmul(
                    pse[0][:, :], b.ones_b[0:1, :], nscrow[0:1, j * 512:(j + 1) * 512],
                    start=True, stop=True))
                vt = ctx.inc("tensor", b.sT, 1)
                ctx.wait("scalar", b.sT, vt)
                ctx.emit("scalar", lambda E, j=j: E.activation(
                    nsc_s[:, j * 512:(j + 1) * 512], pse[0][:, :], AF.Identity))
                ss = ctx.inc("scalar", b.sS, 1)
                ctx.wait("tensor", b.sS, ss)
            ldx = ctx.sem("m_ldx")
            tdone = []
            for tt in range(9):
                i = tt % 2
                src = dram["xprev"] if tt == 0 else dram["x_sh"][(tt - 1) * 128:tt * 128, :]
                if tt >= 2:
                    ctx.wait("sync", b.sV, tdone[tt - 2])
                ctx.dma("sync", xt[i][:, :], src, ldx)
                ctx.wait("scalar", ldx, 16 * (tt + 1))
                if tt >= 1:
                    ctx.wait("scalar", b.sV, tdone[tt - 1])
                    ctx.wait("scalar", b.stx, b.stx.v)
                sEnd = emit_ln(ctx, b, xt[i][:, :], b.scr[:, 0:2048], b.scr, b.stat)
                if tt >= 1:
                    ctx.wait("sync", b.sS, sEnd)
                    ctx.dma("sync", dram["hxn0"][(tt - 1) * 128:tt * 128, :],
                            b.scr[:, 0:2048], b.stx)
                ctx.emit("scalar", lambda E, i=i: E.activation(
                    xt[i][:, :], b.scr[:, 0:2048], AF.Square))
                s2 = ctx.inc("scalar", b.sS, 1)
                ctx.wait("vector", b.sS, s2)
                ctx.emit("vector", lambda E, i=i: E.tensor_scalar(
                    xt[i][:, :], xt[i][:, :], 1.0, None, op0=OP.mult, op1=OP.add,
                    accum_out=b.stat[:, 2:3]))
                ctx.emit("vector", lambda E: E.tensor_copy(b.stat[:, 4:8], b.stat[:, 0:4]))
                ctx.emit("vector", lambda E: E.tensor_scalar(
                    b.stat[:, 2:3], b.stat[:, 2:3], 1.0 / 2048.0, 1e-5,
                    op0=OP.mult, op1=OP.add))
                v2 = ctx.inc("vector", b.sV, 1)
                ctx.wait("scalar", b.sV, v2)
                ctx.emit("scalar", lambda E: E.activation(b.stat[:, 2:3], b.stat[:, 2:3], AF.Ln))
                ctx.emit("scalar", lambda E: E.activation(b.stat[:, 2:3], b.stat[:, 2:3],
                                                          AF.Exp, scale=-0.5))
                s3 = ctx.inc("scalar", b.sS, 1)
                ctx.wait("vector", b.sS, s3)
                ctx.emit("vector", lambda E: E.tensor_copy(b.stat[:, 4:8], b.stat[:, 0:4]))
                ctx.emit("vector", lambda E, i=i: E.tensor_scalar(
                    xt[i][:, :], b.scr[:, 0:2048], b.stat[:, 2:3], None, op0=OP.mult))
                ctx.emit("vector", lambda E, i=i: E.tensor_tensor(
                    xbf[:, :], xt[i][:, :], nsc_s[:, :], op=OP.mult))
                v3 = ctx.inc("vector", b.sV, 1)
                tdone.append(v3)
                ctx.wait("tensor", b.sV, v3)
                emit_tr_strip(ctx, b, xbf, 16,
                              (lambda tt: (lambda dm: dm * 1152 + tt * 128))(tt), xnmT, trp)
                ctx.wait("vector", b.sS, b.sS.v)
            if int(_os.environ.get("MK_P1SUB", "9")) < 2:
                ctx.barrier("m_sub1")
                return
            ws_in = WStream(ctx, nc, sc, "m_win", 2048, 3)
            for eb in range(64):
                ws_in.load(dram["inpw"][eb])
            g_evac = {}
            g_idx = 0
            for eb in range(64):
                wbuf = ws_in.use("tensor")
                is_x = eb < 32
                if is_x:
                    for dm in range(16):
                        ctx.emit("tensor", lambda E, dm=dm, w=wbuf: E.matmul(
                            psh[:, :], w[:, dm * 128:(dm + 1) * 128],
                            xnmT[:, dm * 1152:dm * 1152 + 128],
                            start=(dm == 0), stop=(dm == 15)))
                    vt = ctx.inc("tensor", b.sT, 1)
                    ctx.wait("scalar", b.sT, vt)
                    ctx.emit("scalar", lambda E, eb=eb: E.activation(
                        xin[:, eb * 1027:eb * 1027 + 3], psh[:, 125:128], AF.Identity))
                    hv = ctx.inc("scalar", b.sS, 1)
                    ctx.wait("tensor", b.sS, hv)
                for half in range(2):
                    pp = pse[g_idx % 4]
                    if g_idx >= 4:
                        ctx.wait("tensor", b.sS, g_evac[g_idx - 4])
                    for dm in range(16):
                        ctx.emit("tensor", lambda E, dm=dm, half=half, w=wbuf, pp=pp: E.matmul(
                            pp[:, :], w[:, dm * 128:(dm + 1) * 128],
                            xnmT[:, dm * 1152 + 128 + half * 512:
                                 dm * 1152 + 128 + (half + 1) * 512],
                            start=(dm == 0), stop=(dm == 15)))
                    vt = ctx.inc("tensor", b.sT, 1)
                    ctx.wait("scalar", b.sT, vt)
                    if is_x:
                        ctx.emit("scalar", lambda E, eb=eb, half=half, pp=pp: E.activation(
                            xin[:, eb * 1027 + 3 + half * 512:
                                eb * 1027 + 3 + (half + 1) * 512], pp[:, :], AF.Identity))
                    else:
                        ctx.emit("scalar", lambda E, half=half, pp=pp: E.activation(
                            szt[:, half * 512:(half + 1) * 512], pp[:, :], AF.Silu))
                    g_evac[g_idx] = ctx.inc("scalar", b.sS, 1)
                    g_idx += 1
                ws_in.advance(1, b.sT, b.sT.v)
                if not is_x:
                    ctx.wait("sync", b.sS, b.sS.v)
                    ctx.dma("sync", hsz[:, (eb - 32) * 1024:(eb - 31) * 1024], szt[:, :], b.stx)
                    ctx.wait("scalar", b.stx, b.stx.v)
            xct = sb2("m_xct", [128, 1024], BF16)
            cacc = sb2("m_cacc", [128, 1024], F32)
            if "hxin" in dram:
                ctx.wait("sync", b.sS, b.sS.v)
                ctx.dma("sync", dram["hxin"], xin[:, :], b.stx)
            ctx.wait("vector", b.sS, b.sS.v)
            ctx.wait("vector", ldp, 32)
            for db in range(32):
                base = db * 1027
                ctx.emit("vector", lambda E, base=base, db=db: E.tensor_scalar(
                    cacc[:, :], xin[:, base:base + 1024], convw_s[:, db * 4:db * 4 + 1],
                    None, op0=OP.mult))
                for k in range(1, 4):
                    ctx.emit("vector", lambda E, base=base, db=db, k=k: E.scalar_tensor_tensor(
                        cacc[:, :], xin[:, base + k:base + k + 1024],
                        convw_s[:, db * 4 + k:db * 4 + k + 1], cacc[:, :],
                        op0=OP.mult, op1=OP.add))
                vv = ctx.inc("vector", b.sV, 1)
                ctx.wait("scalar", b.sV, vv)
                ctx.emit("scalar", lambda E, db=db: E.activation(
                    xct[:, :], cacc[:, :], AF.Silu, bias=convb_s[:, db:db + 1]))
                ss = ctx.inc("scalar", b.sS, 1)
                ctx.wait("sync", b.sS, ss)
                ctx.dma("sync", hxc[:, db * 1024:(db + 1) * 1024], xct[:, :], b.stx)
                ctx.wait("scalar", b.stx, b.stx.v)
                ctx.wait("vector", b.sS, ss)
        ctx.barrier("m_ab")
        if p1s < 2:
            return

        with ExitStack() as sc:
            sb2 = lambda n, sh, dt: sc.enter_context(nc.sbuf_tensor(n, sh, dt))
            cb_t = sb2("m_cb", [128, 16 * 1024], BF16)
            tmb = sb2("m_tmb", [128, 1024], BF16)
            xcr = [sb2(f"m_xcr{i}", [128, 1024], BF16) for i in range(2)]
            with ExitStack() as sd:
                sb3 = lambda n, sh, dt: sd.enter_context(nc.sbuf_tensor(n, sh, dt))
                delta = sb3("m_delta", [128, 32 * 1024], BF16)
                bb = sb3("m_bb", [128, 16 * 1024], BF16)
                du = sb3("m_du", [128, 1024], BF16)
                wt = sb3("m_wt", [128, 1024], BF16)
                ht = sb3("m_ht", [128, 1024], BF16)
                y0 = [sb3(f"m_y0{i}", [128, 1024], BF16) for i in range(2)]
                sdd = sd.enter_context(ExitStack())
                sb4 = lambda n, sh, dt: sdd.enter_context(nc.sbuf_tensor(n, sh, dt))
                xpw_s = sb4("m_xpw", [128, 32 * 160], BF16)
                dtw_s = sb4("m_dtw", [128, 32 * 128], BF16)
                ctx.dma("gpsimd", xpw_s[:, :], dram["xpw"], ldpg)
                ctx.dma("gpsimd", dtw_s[:, :], dram["dtw"], ldpg)
                ldp_all = ldpg.v
                dtlow = sb4("m_dtlow", [128, 1024], BF16)
                sp1 = sb4("m_sp1", [128, 1024], BF16)
                sp2 = sb4("m_sp2", [128, 1024], BF16)
                cumd_t = [sb4(f"m_cumd{i}", [128, 1024], BF16) for i in range(2)]
                onesf_t = sb4("m_ones_t", [128, 1024], BF16)
                ctx.emit("gpsimd", lambda E: E.memset(onesf_t[:, :], 1.0))
                gv = ctx.inc("gpsimd", b.sG, 1)
                ps2 = lambda n, sh, dt: sdd.enter_context(nc.psum_tensor(n, sh, dt))
                psd = [ps2(f"m_psd{i}", [128, 512], F32) for i in range(2)]
                psbc = [ps2(f"m_psbc{i}", [32, 512], F32) for i in range(2)]
                psq = [ps2(f"m_psq{i}", [128, 512], F32) for i in range(2)]
                ldc = ctx.sem("m_ldc")
                db_done = []
                for db in range(32):
                    if db >= 2:
                        ctx.wait("sync", b.sT, db_done[db - 2])
                    ctx.dma("sync", xcr[db % 2][:, :], hxc[:, db * 1024:(db + 1) * 1024], ldc)
                    ctx.wait("tensor", ldc, 16 * (db + 1))
                    ctx.wait("tensor", ldpg, ldp_all)
                    for half in range(2):
                        ctx.emit("tensor", lambda E, db=db, half=half: E.matmul(
                            psd[half][:, :], xpw_s[:, db * 160:db * 160 + 128],
                            xcr[db % 2][:, half * 512:(half + 1) * 512],
                            start=(db == 0), stop=(db == 31)))
                        ctx.emit("tensor", lambda E, db=db, half=half: E.matmul(
                            psbc[half][:, :], xpw_s[:, db * 160 + 128:db * 160 + 160],
                            xcr[db % 2][:, half * 512:(half + 1) * 512],
                            start=(db == 0), stop=(db == 31)))
                    db_done.append(ctx.inc("tensor", b.sT, 1))
                ctx.wait("scalar", b.sT, b.sT.v)
                ctx.emit("scalar", lambda E: E.activation(dtlow[:, 0:512], psd[0][:, :],
                                                          AF.Identity))
                ctx.emit("scalar", lambda E: E.activation(dtlow[:, 512:1024], psd[1][:, :],
                                                          AF.Identity))
                ctx.emit("scalar", lambda E: E.activation(bcrows_bf[:, 0:512], psbc[0][:, :],
                                                          AF.Identity))
                ctx.emit("scalar", lambda E: E.activation(bcrows_bf[:, 512:1024], psbc[1][:, :],
                                                          AF.Identity))
                ssx = ctx.inc("scalar", b.sS, 1)
                ctx.wait("tensor", b.sS, ssx)
                dvv = []
                for db in range(32):
                    for half in range(2):
                        ctx.emit("tensor", lambda E, db=db, half=half: E.matmul(
                            psq[half][:, :], dtw_s[:, db * 128:(db + 1) * 128],
                            dtlow[:, half * 512:(half + 1) * 512], start=True, stop=True))
                    vt = ctx.inc("tensor", b.sT, 1)
                    ctx.wait("scalar", b.sT, vt)
                    if db >= 1:
                        ctx.wait("scalar", b.sV, dvv[db - 1])
                    for half in range(2):
                        ctx.emit("scalar", lambda E, db=db, half=half: E.activation(
                            sp1[:, half * 512:(half + 1) * 512], psq[half][:, :], AF.Abs,
                            bias=dtb_s[:, db:db + 1]))
                        ctx.emit("scalar", lambda E, db=db, half=half: E.activation(
                            sp2[:, half * 512:(half + 1) * 512], psq[half][:, :], AF.Relu,
                            bias=dtb_s[:, db:db + 1]))
                    ctx.emit("scalar", lambda E: E.activation(sp1[:, :], sp1[:, :], AF.Exp,
                                                              scale=-1.0))
                    ctx.emit("scalar", lambda E: E.activation(sp1[:, :], sp1[:, :], AF.Ln,
                                                              bias=1.0))
                    ss = ctx.inc("scalar", b.sS, 1)
                    ctx.wait("tensor", b.sS, ss)
                    ctx.wait("vector", b.sS, ss)
                    ctx.emit("vector", lambda E, db=db: E.scalar_tensor_tensor(
                        delta[:, db * 1024:(db + 1) * 1024], sp1[:, :], 1.0, sp2[:, :],
                        op0=OP.mult, op1=OP.add, accum_out=sdelta[:, db:db + 1]))
                    ctx.wait("vector", b.sG, gv)
                    if db >= 2:
                        ctx.wait("vector", b.stx, b.stx.v)
                    ctx.emit("vector", lambda E, db=db: E.tensor_tensor_scan(
                        cumd_t[db % 2][:, :], onesf_t[:, :],
                        delta[:, db * 1024:(db + 1) * 1024], 0.0, op0=OP.mult, op1=OP.add))
                    vv = ctx.inc("vector", b.sV, 1)
                    dvv.append(vv)
                    ctx.wait("sync", b.sV, vv)
                    ctx.dma("sync", hcumd[:, db * 1024:(db + 1) * 1024],
                            cumd_t[db % 2][:, :], b.stx)
                    ctx.wait("scalar", b.sV, vv)
                    for n in range(16):
                        ctx.emit("scalar", lambda E, db=db, n=n: E.activation(
                            etot[:, db * 16 + n:db * 16 + n + 1], sdelta[:, db:db + 1],
                            AF.Exp, scale=a_s[:, db * 16 + n:db * 16 + n + 1]))
                    ctx.inc("scalar", b.sS, 1)
                ctx.barrier("m_dt")
                sdd.close()
                if p1s < 3:
                    return

                psw = [sd.enter_context(nc.psum_tensor(f"m_psw{i}", [128, 512], F32))
                       for i in range(2)]
                rsel_s = sb3("m_rsel", [32, 32 * 128], BF16)
                dA = [sb3(f"m_dA{i}", [128, 1024], BF16) for i in range(8)]
                ldrs = ctx.sem("m_ldrs")
                ctx.dma("gpsimd", rsel_s[:, :], dram["rsel"], ldrs)
                ctx.wait("tensor", ldrs, 16)
                for n in range(32):
                    dst = bb if n < 16 else cb_t
                    nn = n % 16
                    for half in range(2):
                        ctx.emit("tensor", lambda E, n=n, half=half: E.matmul(
                            psw[half][:, :], rsel_s[:, n * 128:(n + 1) * 128],
                            bcrows_bf[:, half * 512:(half + 1) * 512],
                            start=True, stop=True))
                    vt = ctx.inc("tensor", b.sT, 1)
                    ctx.wait("scalar", b.sT, vt)
                    for half in range(2):
                        ctx.emit("scalar", lambda E, nn=nn, dst=dst, half=half: E.activation(
                            dst[:, nn * 1024 + half * 512:nn * 1024 + (half + 1) * 512],
                            psw[half][:, :], AF.Identity))
                    ss = ctx.inc("scalar", b.sS, 1)
                    ctx.wait("tensor", b.sS, ss)
                ldc2 = ctx.sem("m_ldc2")
                sty = ctx.sem("m_sty")
                sdone = {}
                ctx.wait("vector", b.sS, b.sS.v)
                ctx.wait("vector", ldp, 64)
                for db in range(32):
                    if db >= 2:
                        ctx.wait("sync", b.sV, sdone[(db - 2, 3)])
                    ctx.dma("sync", xcr[db % 2][:, :], hxc[:, db * 1024:(db + 1) * 1024], ldc2)
                    ctx.wait("vector", ldc2, 16 * (db + 1))
                    ctx.emit("vector", lambda E, db=db: E.tensor_tensor(
                        du[:, :], delta[:, db * 1024:(db + 1) * 1024], xcr[db % 2][:, :],
                        op=OP.mult))
                    if db >= 2:
                        ctx.wait("vector", sty, 16 * (db - 1))
                    ctx.emit("vector", lambda E, db=db: E.tensor_scalar(
                        y0[db % 2][:, :], xcr[db % 2][:, :], dvec_s[:, db:db + 1],
                        None, op0=OP.mult))
                    for nb in range(4):
                        if db * 4 + nb >= 2:
                            pb = (db, nb - 2) if nb >= 2 else (db - 1, nb + 2)
                            ctx.wait("scalar", b.sV, sdone[pb])
                        for j in range(4):
                            n = nb * 4 + j
                            ctx.emit("scalar", lambda E, db=db, n=n, nb=nb, j=j: E.activation(
                                dA[(nb % 2) * 4 + j][:, :],
                                delta[:, db * 1024:(db + 1) * 1024], AF.Exp,
                                scale=a_s[:, db * 16 + n:db * 16 + n + 1]))
                        ss = ctx.inc("scalar", b.sS, 1)
                        ctx.wait("vector", b.sS, ss)
                        for j in range(4):
                            n = nb * 4 + j
                            ctx.emit("vector", lambda E, n=n: E.tensor_tensor(
                                wt[:, :], du[:, :], bb[:, n * 1024:(n + 1) * 1024], op=OP.mult))
                            ctx.emit("vector", lambda E, nb=nb, j=j: E.tensor_tensor_scan(
                                ht[:, :], dA[(nb % 2) * 4 + j][:, :], wt[:, :], 0.0,
                                op0=OP.mult, op1=OP.add))
                            ctx.emit("vector", lambda E, db=db, n=n: E.tensor_copy(
                                hlast[:, db * 16 + n:db * 16 + n + 1], ht[:, 1023:1024]))
                            ctx.emit("vector", lambda E, n=n: E.tensor_tensor(
                                tmb[:, :], ht[:, :], cb_t[:, n * 1024:(n + 1) * 1024],
                                op=OP.mult))
                            ctx.emit("vector", lambda E, db=db: E.tensor_tensor(
                                y0[db % 2][:, :], y0[db % 2][:, :], tmb[:, :], op=OP.add))
                        sdone[(db, nb)] = ctx.inc("vector", b.sV, 1)
                    ctx.wait("sync", b.sV, sdone[(db, 3)])
                    ctx.dma("sync", hy0[:, db * 1024:(db + 1) * 1024], y0[db % 2][:, :], sty)
                ctx.barrier("m_scan")
                if p1s < 4:
                    return

            civ = dram["cc_in"].rearrange("o (p f) -> (o p) f", p=128)
            ctx.dma("sync", civ[:, 0:512], etot[:, :], b.stx)
            ctx.dma("sync", civ[:, 512:1024], hlast[:, :], b.stx)
            ctx.wait("gpsimd", b.stx, b.stx.v)
            ccs = ctx.sem("m_ccs")
            ctx.emit("gpsimd", lambda E: E.collective_compute(
                "AllGather", OP.bypass, replica_groups=[list(range(NCORES))],
                ins=[dram["cc_in"].opt()], outs=[dram["cc_out"].opt()]).then_inc(ccs.h, 1),
                fusable=False)
            ccs.v += 1
            ctx.wait("sync", ccs, 1)
            er = [sb2(f"m_er{i}", [128, 512], F32) for i in range(8)]
            hr = [sb2(f"m_hr{i}", [128, 512], F32) for i in range(8)]
            cur = [sb2(f"m_cur{i}", [128, 512], F32) for i in range(2)]
            selm = sb2("m_selm", [128, 8], F32)
            ldcc = ctx.sem("m_ldcc")
            ctx.dma("sync", selm[:, :], dram["selmask"], ldcc)
            for r in range(NCORES):
                view = dram["cc_out"][r:r + 1, :].rearrange("o (p f) -> (o p) f", p=128)
                ctx.dma("sync", er[r][:, :], view[:, 0:512], ldcc)
                ctx.dma("sync", hr[r][:, :], view[:, 512:1024], ldcc)
            ctx.emit("gpsimd", lambda E: E.memset(hin[:, :], 0.0))
            ctx.emit("gpsimd", lambda E: E.memset(cur[0][:, :], 0.0))
            gv2 = ctx.inc("gpsimd", b.sG, 1)
            ctx.wait("vector", b.sG, gv2)
            ctx.wait("vector", ldcc, ldcc.v)
            for r in range(NCORES):
                a, o = cur[r % 2], cur[(r + 1) % 2]
                ctx.emit("vector", lambda E, r=r, a=a: E.scalar_tensor_tensor(
                    hin[:, :], a[:, :], selm[:, r:r + 1], hin[:, :], op0=OP.mult, op1=OP.add))
                ctx.emit("vector", lambda E, r=r, a=a, o=o: E.tensor_tensor(
                    o[:, :], a[:, :], er[r][:, :], op=OP.mult))
                ctx.emit("vector", lambda E, r=r, o=o: E.tensor_tensor(
                    o[:, :], o[:, :], hr[r][:, :], op=OP.add))
            ctx.inc("vector", b.sV, 1)
            ctx.barrier("m_carry")
            if p1s < 5:
                return

            szr = [sb2(f"m_szr{i}", [128, 1024], BF16) for i in range(2)]
            cdr = [sb2(f"m_cdr{i}", [128, 1024], BF16) for i in range(2)]
            y0r = [sb2(f"m_y0r{i}", [128, 1024], BF16) for i in range(2)]
            El = [sb2(f"m_El{i}", [128, 1024], BF16) for i in range(8)]
            y2t = [sb2(f"m_y2t{i}", [128, 1024], BF16) for i in range(2)]
            ldg = ctx.sem("m_ldg")
            sty2 = ctx.sem("m_sty2")
            gdone = {}
            for db in range(32):
                i = db % 2
                if db >= 2:
                    ctx.wait("sync", b.sV, gdone[db - 2])
                ctx.dma("sync", y0r[i][:, :], hy0[:, db * 1024:(db + 1) * 1024], ldg)
                ctx.dma("sync", cdr[i][:, :], hcumd[:, db * 1024:(db + 1) * 1024], ldg)
                ctx.dma("sync", szr[i][:, :], hsz[:, db * 1024:(db + 1) * 1024], ldg)
                ctx.wait("vector", ldg, 48 * (db + 1))
                for nb in range(4):
                    if db * 4 + nb >= 2:
                        pb = (db, nb - 2) if nb >= 2 else (db - 1, nb + 2)
                        ctx.wait("scalar", b.sV, gdone[pb])
                    ctx.wait("scalar", ldg, 48 * (db + 1))
                    for j in range(4):
                        n = nb * 4 + j
                        ctx.emit("scalar", lambda E, db=db, n=n, nb=nb, j=j, i=i: E.activation(
                            El[(nb % 2) * 4 + j][:, :], cdr[i][:, :], AF.Exp,
                            scale=a_s[:, db * 16 + n:db * 16 + n + 1]))
                    ss = ctx.inc("scalar", b.sS, 1)
                    ctx.wait("vector", b.sS, ss)
                    for j in range(4):
                        n = nb * 4 + j
                        sl = (nb % 2) * 4 + j
                        ctx.emit("vector", lambda E, db=db, n=n, sl=sl: E.tensor_scalar(
                            El[sl][:, :], El[sl][:, :],
                            hin[:, db * 16 + n:db * 16 + n + 1], None, op0=OP.mult))
                        ctx.emit("vector", lambda E, n=n, sl=sl: E.tensor_tensor(
                            tmb[:, :], El[sl][:, :], cb_t[:, n * 1024:(n + 1) * 1024],
                            op=OP.mult))
                        ctx.emit("vector", lambda E, db=db, i=i: E.tensor_tensor(
                            y0r[i][:, :], y0r[i][:, :], tmb[:, :], op=OP.add))
                    gdone[(db, nb)] = ctx.inc("vector", b.sV, 1)
                if db >= 2:
                    ctx.wait("vector", sty2, 16 * (db - 1))
                ctx.emit("vector", lambda E, db=db, i=i: E.tensor_tensor(
                    y2t[i][:, :], y0r[i][:, :], szr[i][:, :], op=OP.mult))
                gdone[db] = ctx.inc("vector", b.sV, 1)
                ctx.wait("sync", b.sV, gdone[db])
                ctx.dma("sync", hy2[:, db * 1024:(db + 1) * 1024], y2t[i][:, :], sty2)
        ctx.barrier("m_y2")
        if p1s < 6:
            return

        with ExitStack() as sc:
            sb2 = lambda n, sh, dt: sc.enter_context(nc.sbuf_tensor(n, sh, dt))
            ps1 = lambda n, sh, dt: sc.enter_context(nc.psum_tensor(n, sh, dt))
            y2r = [sb2(f"m_y2r{i}", [128, 1024], BF16) for i in range(2)]
            mo = sb2("m_mo", [128, 16 * 1024], BF16)
            owb = [sb2(f"m_owb{i}", [128, 4096], BF16) for i in range(4)]
            sop = sc.enter_context(ExitStack())
            pso = [sop.enter_context(nc.psum_tensor(f"m_pso{i}", [128, 512], F32))
                   for i in range(8)]
            ldy = ctx.sem("m_ldy")
            ldo = ctx.sem("m_ldo")
            ogdone = ctx.sem("m_ogdone")
            nload = 0
            tdb = []
            for og in range(4):
                for j in range(4):
                    if og >= 1:
                        ctx.wait("gpsimd", ogdone, og)
                    ctx.dma("gpsimd", owb[j][:, :], dram["outw"][og * 4 + j], ldo)
                ctx.wait("tensor", ldo, 16 * 4 * (og + 1))
                for db in range(32):
                    if nload >= 2:
                        ctx.wait("sync", b.sT, tdb[nload - 2])
                    ctx.dma("sync", y2r[db % 2][:, :], dram["hy2"][:, db * 1024:(db + 1) * 1024], ldy)
                    nload += 1
                    ctx.wait("tensor", ldy, 16 * nload)
                    for ob in range(4):
                        for half in range(2):
                            ctx.emit("tensor", lambda E, ob=ob, half=half, db=db, w=owb[ob]:
                                     E.matmul(pso[ob * 2 + half][:, :],
                                              w[:, db * 128:(db + 1) * 128],
                                              y2r[db % 2][:, half * 512:(half + 1) * 512],
                                              start=(db == 0), stop=(db == 31)))
                    tdb.append(ctx.inc("tensor", b.sT, 1))
                vt = b.sT.v
                ctx.wait("scalar", b.sT, vt)
                for ob in range(4):
                    obg = og * 4 + ob
                    for half in range(2):
                        ctx.emit("scalar", lambda E, obg=obg, ob=ob, half=half: E.activation(
                            mo[:, obg * 1024 + half * 512:obg * 1024 + (half + 1) * 512],
                            pso[ob * 2 + half][:, :], AF.Identity))
                ssg = ctx.inc("scalar", b.sS, 1)
                ctx.wait("tensor", b.sS, ssg)
                ctx.inc("scalar", ogdone, 1)
            ctx.barrier("m_op")
            sop.close()

            xn0r = [sb2(f"m_xn0r{i}", [128, 2048], F32) for i in range(2)]
            xr = [sb2(f"m_xr{i}", [128, 2048], F32) for i in range(2)]
            pstr = ps1("m_pstr", [128, 512], BF16)
            ldh = ctx.sem("m_ldh")
            hdone = []
            for tt in range(8):
                i = tt % 2
                if tt >= 2:
                    ctx.wait("sync", b.sV, hdone[tt - 2])
                ctx.dma("sync", xn0r[i][:, :], dram["hxn0"][tt * 128:(tt + 1) * 128, :], ldh)
                ctx.dma("sync", xr[i][:, :], dram["x_sh"][tt * 128:(tt + 1) * 128, :], ldh)
                for ob in range(16):
                    q = ob % 4
                    ctx.emit("tensor", lambda E, ob=ob, tt=tt, q=q: E.matmul(
                        pstr[:, q * 128:(q + 1) * 128],
                        mo[:, ob * 1024 + tt * 128:ob * 1024 + (tt + 1) * 128],
                        b.eye_b[:, :], is_transpose=True, start=True, stop=True))
                    if q == 3:
                        vt = ctx.inc("tensor", b.sT, 1)
                        ctx.wait("vector", b.sT, vt)
                        ctx.wait("vector", ldh, 32 * (tt + 1))
                        for qq in range(4):
                            obq = ob - 3 + qq
                            ctx.emit("vector", lambda E, obq=obq, qq=qq, i=i: E.tensor_tensor(
                                xn0r[i][:, obq * 128:(obq + 1) * 128],
                                xn0r[i][:, obq * 128:(obq + 1) * 128],
                                pstr[:, qq * 128:(qq + 1) * 128], op=OP.add))
                        vv = ctx.inc("vector", b.sV, 1)
                        ctx.wait("tensor", b.sV, vv)
                ctx.wait("scalar", b.sV, b.sV.v)
                if tt >= 1:
                    ctx.wait("scalar", b.stx, b.stx.v)
                sEnd = emit_ln(ctx, b, xn0r[i][:, :], b.scr[:, 0:2048], b.scr, b.stat)
                ctx.wait("vector", b.sS, sEnd)
                ctx.emit("vector", lambda E, i=i: E.tensor_tensor(
                    xr[i][:, :], xr[i][:, :], b.scr[:, 0:2048], op=OP.add))
                vv = ctx.inc("vector", b.sV, 1)
                hdone.append(vv)
                ctx.wait("sync", b.sV, vv)
                ctx.dma("sync", dram["hx1"][tt * 128:(tt + 1) * 128, :], xr[i][:, :], b.stx)


def emit_p2(ctx, nc, b, dram):
    with ExitStack() as st_:
        sbl = lambda n, sh, dt: st_.enter_context(nc.sbuf_tensor(n, sh, dt))
        psl = lambda n, sh, dt: st_.enter_context(nc.psum_tensor(n, sh, dt))
        wqt = sbl("f_wqt", [128, 16 * 2048], BF16)
        x1t = [sbl(f"f_x1t{i}", [128, 2048], F32) for i in range(2)]
        xn2 = sbl("f_xn2", [128, 2048], F32)
        vq = sbl("f_vq", [128, 2048], F32)
        vkeep = sbl("f_vkeep", [128, 2048], F32)
        lad0 = sbl("f_lad0", [128, 2048], F32)
        indt = sbl("f_ind", [128, 2048], F32)
        sgn = sbl("f_sgn", [128, 2048], F32)
        xqb = sbl("f_xqb", [128, 2048], BF16)
        xqT = sbl("f_xqT", [128, 16 * 128], BF16)
        ot = sbl("f_ot", [128, 2048], F32)
        bst = sbl("f_bst", [128, 4], F32)
        ptr = psl("f_ptr", [128, 512], BF16)
        pmm = [psl(f"f_pmm{i}", [128, 512], F32) for i in range(4)]
        ldw = ctx.sem("f_ldw")
        ctx.dma("sync", wqt[:, :], dram["hwqt"], ldw)
        ldx = ctx.sem("f_ldx")
        fdone = []
        for tt in range(8):
            i = tt % 2
            if tt >= 2:
                ctx.wait("sync", b.sV, fdone[tt - 2])
            ctx.dma("sync", x1t[i][:, :], dram["hx1"][tt * 128:(tt + 1) * 128, :], ldx)
            ctx.wait("scalar", ldx, 16 * (tt + 1))
            if tt >= 1:
                ctx.wait("scalar", b.sV, fdone[tt - 1])
                ctx.wait("scalar", b.stx, b.stx.v)
            sEnd = emit_ln(ctx, b, x1t[i][:, :], xn2[:, :], b.scr, b.stat)
            ctx.emit("scalar", lambda E: E.activation(sgn[:, :], xn2[:, :], AF.Sign))
            sSg = ctx.inc("scalar", b.sS, 1)
            ctx.wait("vector", b.sS, sSg)
            ctx.emit("vector", lambda E: E.scalar_tensor_tensor(
                vq[:, :], xn2[:, :], -1.0, xn2[:, :], op0=OP.mult, op1=OP.max))
            ctx.emit("vector", lambda E: E.tensor_scalar(
                lad0[:, :], vq[:, :], 1.0, None, op0=OP.mult, op1=OP.max,
                accum_out=bst[:, 0:1]))
            ctx.emit("vector", lambda E: E.tensor_copy(vkeep[:, 0:16], vq[:, 0:16]))
            ctx.emit("vector", lambda E: E.tensor_scalar(
                bst[:, 1:2], bst[:, 0:1], 1.0 / 127.0, 1e-8, op0=OP.mult, op1=OP.max))
            ctx.emit("vector", lambda E: E.tensor_copy(vkeep[:, 32:48], vq[:, 32:48]))
            ctx.emit("vector", lambda E: E.tensor_copy(vkeep[:, 48:64], vq[:, 48:64]))
            ctx.emit("vector", lambda E: E.reciprocal(bst[:, 2:3], bst[:, 1:2]))
            ctx.emit("vector", lambda E: E.tensor_scalar(
                bst[:, 3:4], bst[:, 1:2], b.ha[:, 0:1], 2.0, op0=OP.mult, op1=OP.mult))
            ctx.emit("vector", lambda E: E.tensor_copy(lad0[:, 0:16], vq[:, 0:16]))
            ctx.emit("vector", lambda E: E.tensor_copy(lad0[:, 16:32], vq[:, 16:32]))
            ctx.emit("vector", lambda E: E.tensor_scalar(
                vq[:, :], vq[:, :], bst[:, 2:3], None, op0=OP.mult))
            ctx.emit("vector", lambda E: E.tensor_copy(vkeep[:, :], vq[:, :]))
            if "hbst" in dram and tt == 0:
                vdmp = ctx.inc("vector", b.sV, 1)
                ctx.wait("sync", b.sV, vdmp)
                ctx.dma("sync", dram["hbst"][:, 0:4], bst[:, 0:4], b.stx)
                ctx.dma("sync", dram["hvq"], vq[:, :], b.stx)
            cur, other = vq, lad0
            for k in (64.0, 32.0, 16.0, 8.0, 4.0, 2.0, 1.0):
                ctx.emit("vector", lambda E, cur=cur, k=k: E.tensor_scalar(
                    indt[:, :], cur[:, :], k, None, op0=OP.is_ge))
                ctx.emit("vector", lambda E, cur=cur, other=other, k=k: E.scalar_tensor_tensor(
                    other[:, :], indt[:, :], -k, cur[:, :], op0=OP.mult, op1=OP.add))
                cur, other = other, cur
            if "hbst" in dram and tt == 0:
                vdmp2 = ctx.inc("vector", b.sV, 1)
                ctx.wait("sync", b.sV, vdmp2)
                ctx.dma("sync", dram["hcur"], cur[:, :], b.stx)
                ctx.wait("vector", b.stx, b.stx.v)
            ctx.emit("vector", lambda E, cur=cur: E.tensor_tensor(
                vkeep[:, :], vkeep[:, :], cur[:, :], op=OP.subtract))
            ctx.emit("vector", lambda E, cur=cur: E.tensor_scalar(
                indt[:, :], cur[:, :], 0.5, None, op0=OP.is_ge))
            ctx.emit("vector", lambda E: E.tensor_tensor(
                vkeep[:, :], vkeep[:, :], indt[:, :], op=OP.add))
            ctx.emit("vector", lambda E: E.tensor_scalar(
                vkeep[:, :], vkeep[:, :], 127.0, None, op0=OP.min))
            ctx.emit("vector", lambda E: E.tensor_tensor(
                xqb[:, :], vkeep[:, :], sgn[:, :], op=OP.mult))
            vq_done = ctx.inc("vector", b.sV, 1)
            if "hxq" in dram:
                ctx.wait("sync", b.sV, vq_done)
                ctx.dma("sync", dram["hxq"][tt * 128:(tt + 1) * 128, :], xqb[:, :], b.stx)
            ctx.wait("tensor", b.sV, vq_done)
            emit_tr_strip(ctx, b, xqb, 16, lambda j: j * 128, xqT, ptr)
            ctx.wait("tensor", ldw, 16)
            for ib in range(16):
                for oc in range(4):
                    ctx.emit("tensor", lambda E, ib=ib, oc=oc: E.matmul(
                        pmm[oc][:, :], xqT[:, ib * 128:(ib + 1) * 128],
                        wqt[:, ib * 2048 + oc * 512:ib * 2048 + (oc + 1) * 512],
                        start=(ib == 0), stop=(ib == 15)))
            vt = ctx.inc("tensor", b.sT, 1)
            ctx.wait("scalar", b.sT, vt)
            for oc in range(4):
                ctx.emit("scalar", lambda E, oc=oc: E.activation(
                    ot[:, oc * 512:(oc + 1) * 512], pmm[oc][:, :], AF.Identity,
                    scale=bst[:, 3:4]))
            sEv = ctx.inc("scalar", b.sS, 1)
            if "hffn" in dram:
                ctx.wait("sync", b.sS, sEv)
                ctx.dma("sync", dram["hffn"][tt * 128:(tt + 1) * 128, :], ot[:, :], b.stx)
                ctx.wait("scalar", b.stx, b.stx.v)
            ctx.wait("tensor", b.sS, sEv)
            sEnd2 = emit_ln(ctx, b, ot[:, :], b.scr[:, 0:2048], b.scr, b.stat)
            ctx.wait("vector", b.sS, sEnd2)
            ctx.emit("vector", lambda E, i=i: E.tensor_tensor(
                x1t[i][:, :], x1t[i][:, :], b.scr[:, 0:2048], op=OP.add))
            vv = ctx.inc("vector", b.sV, 1)
            fdone.append(vv)
            ctx.wait("sync", b.sV, vv)
            ctx.dma("sync", dram["hx2"][tt * 128:(tt + 1) * 128, :], x1t[i][:, :], b.stx)


def emit_p3(ctx, nc, b, dram):
    with ExitStack() as st_:
        sbl = lambda n, sh, dt: st_.enter_context(nc.sbuf_tensor(n, sh, dt))
        psl = lambda n, sh, dt: st_.enter_context(nc.psum_tensor(n, sh, dt))
        yhb = sbl("a_yhb", [128, 16 * 1024], BF16)
        bt_s = sbl("a_bt", [128, 64], F32)
        bq_s = sbl("a_bq", [128, 16], F32)
        bk_s = sbl("a_bk", [128, 16], F32)
        bvrow = sbl("a_bvrow", [1, 2048], BF16)
        bvb = sbl("a_bvb", [128, 16 * 128], BF16)
        borow = sbl("a_borow", [1, 2048], BF16)
        bob = sbl("a_bob", [128, 2048], BF16)
        ldp = ctx.sem("a_ldp")
        ldpg = ctx.sem("a_ldpg")
        ctx.dma("gpsimd", bvrow[:, :], dram["bv"], ldpg)
        ctx.dma("gpsimd", borow[:, :], dram["bo"], ldpg)
        ctx.dma("sync", bt_s[:, :], dram["bias_tab"], ldp)
        ctx.dma("sync", bq_s[:, :], dram["bq"], ldp)
        ctx.dma("sync", bk_s[:, :], dram["bk"], ldp)
        ldp_all = ldp.v
        ctx.wait("scalar", ldp, ldp_all)
        with nc.psum_tensor("a_psb", [128, 512], F32) as psb:
            ctx.wait("tensor", ldp, ldp_all)
            ctx.wait("tensor", ldpg, 32)
            for h in range(16):
                ctx.emit("tensor", lambda E, h=h: E.matmul(
                    psb[:, 0:128], b.ones_b[0:1, :], bvrow[0:1, h * 128:(h + 1) * 128],
                    start=True, stop=True))
                vt = ctx.inc("tensor", b.sT, 1)
                ctx.wait("scalar", b.sT, vt)
                ctx.emit("scalar", lambda E, h=h: E.activation(
                    bvb[:, h * 128:(h + 1) * 128], psb[:, 0:128], AF.Identity))
                ss = ctx.inc("scalar", b.sS, 1)
                ctx.wait("tensor", b.sS, ss)
            for j in range(4):
                ctx.emit("tensor", lambda E, j=j: E.matmul(
                    psb[:, :], b.ones_b[0:1, :], borow[0:1, j * 512:(j + 1) * 512],
                    start=True, stop=True))
                vt = ctx.inc("tensor", b.sT, 1)
                ctx.wait("scalar", b.sT, vt)
                ctx.emit("scalar", lambda E, j=j: E.activation(
                    bob[:, j * 512:(j + 1) * 512], psb[:, :], AF.Identity))
                ss = ctx.inc("scalar", b.sS, 1)
                ctx.wait("tensor", b.sS, ss)

        skv = st_.enter_context(ExitStack())
        qb = skv.enter_context(nc.sbuf_tensor("a_qb", [128, 16 * 1024], BF16))
        kb = skv.enter_context(nc.sbuf_tensor("a_kb", [128, 16 * 1024], BF16))
        vb = skv.enter_context(nc.sbuf_tensor("a_vb", [128, 16 * 1024], BF16))
        with ExitStack() as sc:
            sb2 = lambda n, sh, dt: sc.enter_context(nc.sbuf_tensor(n, sh, dt))
            ps1 = lambda n, sh, dt: sc.enter_context(nc.psum_tensor(n, sh, dt))
            xn3T = sb2("a_xn3T", [128, 16 * 1024], BF16)
            x2t = [sb2("a_x2t0", [128, 2048], F32)]
            xn3 = sb2("a_xn3", [128, 2048], BF16)
            trp = ps1("a_trp", [128, 512], BF16)
            ldx = ctx.sem("a_ldx")
            adone = []
            for tt in range(8):
                i = 0
                if tt >= 1:
                    ctx.wait("sync", b.sS, adone[tt - 1])
                ctx.dma("sync", x2t[i][:, :], dram["hx2"][tt * 128:(tt + 1) * 128, :], ldx)
                ctx.wait("scalar", ldx, 16 * (tt + 1))
                sEnd = emit_ln(ctx, b, x2t[i][:, :], b.scr[:, 0:2048], b.scr, b.stat)
                ctx.emit("scalar", lambda E: E.activation(xn3[:, :], b.scr[:, 0:2048],
                                                          AF.Identity))
                s2 = ctx.inc("scalar", b.sS, 1)
                ctx.wait("tensor", b.sS, s2)
                s3 = emit_tr_strip(ctx, b, xn3, 16,
                                   (lambda tt: (lambda dm: dm * 1024 + tt * 128))(tt),
                                   xn3T, trp)
                adone.append(s3)
            ws_qkv = WStream(ctx, nc, sc, "a_wqkv", 2048, 2)
            for h in range(16):
                ws_qkv.load(dram["wq"][h])
            for h in range(16):
                ws_qkv.load(dram["wk"][h])
            for h in range(16):
                ws_qkv.load(dram["wv"][h])
            psq = [ps1(f"a_psq{i}", [128, 512], F32) for i in range(4)]
            qg_evac = {}
            qg = 0
            for which, (dstb, bias_s) in enumerate(((qb, bq_s), (kb, bk_s))):
                for h in range(16):
                    wb = ws_qkv.use("tensor")
                    for half in range(2):
                        pp = psq[qg % 4]
                        if qg >= 4:
                            ctx.wait("tensor", b.sS, qg_evac[qg - 4])
                        for dm in range(16):
                            ctx.emit("tensor", lambda E, h=h, half=half, dm=dm, w=wb, pp=pp:
                                     E.matmul(
                                         pp[:, :], w[:, dm * 128:(dm + 1) * 128],
                                         xn3T[:, dm * 1024 + half * 512:
                                              dm * 1024 + (half + 1) * 512],
                                         start=(dm == 0), stop=(dm == 15)))
                        vt = ctx.inc("tensor", b.sT, 1)
                        ctx.wait("scalar", b.sT, vt)
                        ctx.emit("scalar", lambda E, h=h, half=half, dstb=dstb,
                                 bias_s=bias_s, pp=pp: E.activation(
                                     dstb[:, h * 1024 + half * 512:h * 1024 + (half + 1) * 512],
                                     pp[:, :], AF.Identity, bias=bias_s[:, h:h + 1]))
                        qg_evac[qg] = ctx.inc("scalar", b.sS, 1)
                        qg += 1
                    ws_qkv.advance(1, b.sT, b.sT.v)
            ctx.wait("tensor", b.sS, b.sS.v)
            psv = [ps1(f"a_psv{i}", [128, 128], F32) for i in range(2)]
            vg_evac = {}
            vgi = 0
            for h in range(16):
                wb = ws_qkv.use("tensor")
                for tt in range(8):
                    pv = psv[vgi % 2]
                    if vgi >= 2:
                        ctx.wait("tensor", b.sV, vg_evac[vgi - 2])
                    for dm in range(16):
                        ctx.emit("tensor", lambda E, h=h, tt=tt, dm=dm, w=wb, pv=pv: E.matmul(
                            pv[:, :], xn3T[:, dm * 1024 + tt * 128:dm * 1024 + (tt + 1) * 128],
                            w[:, dm * 128:(dm + 1) * 128],
                            start=(dm == 0), stop=(dm == 15)))
                    vt = ctx.inc("tensor", b.sT, 1)
                    ctx.wait("vector", b.sT, vt)
                    ctx.emit("vector", lambda E, h=h, tt=tt, pv=pv: E.tensor_tensor(
                        vb[:, h * 1024 + tt * 128:h * 1024 + (tt + 1) * 128],
                        pv[:, :], bvb[:, h * 128:(h + 1) * 128], op=OP.add))
                    vg_evac[vgi] = ctx.inc("vector", b.sV, 1)
                    vgi += 1
                ws_qkv.advance(1, b.sT, b.sT.v)
            ctx.wait("tensor", b.sV, b.sV.v)
        ctx.barrier("a_qkv")

        kvi = dram["kv_in"].rearrange("o (p f) -> (o p) f", p=128)
        ctx.dma("sync", kvi[:, 0:16384], kb[:, :], b.stx)
        ctx.dma("sync", kvi[:, 16384:32768], vb[:, :], b.stx)
        ctx.wait("gpsimd", b.stx, b.stx.v)
        ccs = ctx.sem("a_ccs")
        ctx.emit("gpsimd", lambda E: E.collective_compute(
            "AllGather", OP.bypass, replica_groups=[list(range(NCORES))],
            ins=[dram["kv_in"].opt()], outs=[dram["kv_out"].opt()]).then_inc(ccs.h, 1),
            fusable=False)
        ccs.v += 1
        ctx.wait("sync", ccs, 1)

        with ExitStack() as sc:
            sb2 = lambda n, sh, dt: sc.enter_context(nc.sbuf_tensor(n, sh, dt))
            ps1 = lambda n, sh, dt: sc.enter_context(nc.psum_tensor(n, sh, dt))
            lm_s = sb2("a_lm", [128, 8 * 1024], BF16)
            ldlm = ctx.sem("a_ldlm")
            ctx.dma("gpsimd", lm_s[:, :], dram["lmask"], ldlm)
            ctx.wait("vector", ldlm, 16)
            kg = [sb2(f"a_kg{i}", [128, 1024], BF16) for i in range(2)]
            vg = [sb2(f"a_vg{i}", [128, 1024], BF16) for i in range(2)]
            expst = [sb2(f"a_ex{i}", [128, 1024], BF16) for i in range(4)]
            dnacc = sb2("a_dn", [128, 1024], F32)
            dnb = sb2("a_dnb", [128, 1024], BF16)
            rd = sb2("a_rd", [1, 1024], F32)
            rd2 = sb2("a_rd2", [1, 1024], F32)
            rdb_row = sb2("a_rdbrow", [1, 1024], BF16)
            rdb = sb2("a_rdb", [128, 1024], BF16)
            psc = [ps1(f"a_psc{i}", [128, 1024], F32) for i in range(2)]
            pyh = ps1("a_pyh", [128, 1024], F32)
            pdn = ps1("a_pdn", [1, 1024], F32)
            ldkv = ctx.sem("a_ldkv")
            nkv = 0
            kv_gate = []
            exp_done = {}
            pv_done = {}
            va_done = {}

            def load_rank(g):
                nonlocal nkv
                h, r = g // 8, g % 8
                view = dram["kv_out"][r:r + 1, :].rearrange("o (p f) -> (o p) f", p=128)
                if g >= 2:
                    ctx.wait("sync", b.sT, kv_gate[g - 2])
                ctx.dma("sync", kg[g % 2][:, :], view[:, h * 1024:(h + 1) * 1024], ldkv)
                ctx.dma("sync", vg[g % 2][:, :],
                        view[:, 16384 + h * 1024:16384 + (h + 1) * 1024], ldkv)
                nkv += 1

            load_rank(0)
            load_rank(1)

            def tile_info(h, idx):
                if idx < 8:
                    ksl = kb[:, h * 1024 + idx * 128:h * 1024 + (idx + 1) * 128]
                    vsl = vb[:, h * 1024 + idx * 128:h * 1024 + (idx + 1) * 128]
                    return "L", idx, ksl, vsl, 0.0, None
                g = h * 8 + (idx - 8) // 8
                j2 = (idx - 8) % 8
                ksl = kg[g % 2][:, j2 * 128:(j2 + 1) * 128]
                vsl = vg[g % 2][:, j2 * 128:(j2 + 1) * 128]
                jg = g % 8 * 8 + j2
                return "G", j2, ksl, vsl, bt_s[:, jg:jg + 1], g

            mv_done = {}

            def emit_pv(h, idx):
                kind, j2, ksl, vsl, bias, g = tile_info(h, idx)
                first, last = idx == 0, idx == 71
                if kind == "L":
                    ctx.wait("tensor", b.sV, mv_done[(h, idx)])
                else:
                    ctx.wait("tensor", b.sS, exp_done[(h, idx)])
                tb = idx % 4
                for half in range(2):
                    ctx.emit("tensor", lambda E, tb=tb, half=half, vsl=vsl,
                             first=first, last=last: E.matmul(
                                 pyh[:, half * 512:(half + 1) * 512], vsl,
                                 expst[tb][:, half * 512:(half + 1) * 512],
                                 start=first, stop=last))
                pv_done[(h, idx)] = ctx.inc("tensor", b.sT, 1)
                if kind == "G" and j2 == 7:
                    kv_gate.append(pv_done[(h, idx)])
                    if nkv < 128:
                        load_rank(nkv)

            for h in range(16):
                for idx in range(72):
                    kind, j2, ksl, vsl, bias, g = tile_info(h, idx)
                    tb, pb = idx % 4, idx % 2
                    if kind == "G" and j2 == 0:
                        ctx.wait("tensor", ldkv, 32 * (g + 1))
                    if idx >= 2:
                        ctx.wait("tensor", b.sS, exp_done[(h, idx - 2)])
                    for half in range(2):
                        ctx.emit("tensor", lambda E, pb=pb, half=half, ksl=ksl, h=h: E.matmul(
                            psc[pb][:, half * 512:(half + 1) * 512], ksl,
                            qb[:, h * 1024 + half * 512:h * 1024 + (half + 1) * 512],
                            start=True, stop=True))
                    sct = ctx.inc("tensor", b.sT, 1)
                    ctx.wait("scalar", b.sT, sct)
                    pu = (h, idx - 4) if idx >= 4 else ((h - 1, 68 + idx) if h > 0 else None)
                    if pu is not None:
                        ctx.wait("scalar", b.sT, pv_done[pu])
                        ctx.wait("scalar", b.sV, va_done[pu])
                    ctx.emit("scalar", lambda E, tb=tb, pb=pb, bias=bias: E.activation(
                        expst[tb][:, :], psc[pb][:, :], AF.Exp, scale=ISC, bias=bias))
                    exp_done[(h, idx)] = ctx.inc("scalar", b.sS, 1)
                    if kind == "L":
                        ctx.wait("vector", b.sS, exp_done[(h, idx)])
                        ctx.emit("vector", lambda E, tb=tb, j2=j2: E.tensor_tensor(
                            expst[tb][:, :], expst[tb][:, :],
                            lm_s[:, j2 * 1024:(j2 + 1) * 1024], op=OP.mult))
                        mv_done[(h, idx)] = ctx.inc("vector", b.sV, 1)
                    else:
                        ctx.wait("vector", b.sS, exp_done[(h, idx)])
                    if idx == 0:
                        ctx.emit("vector", lambda E, tb=tb: E.tensor_copy(
                            dnacc[:, :], expst[tb][:, :]))
                    else:
                        ctx.emit("vector", lambda E, tb=tb: E.tensor_tensor(
                            dnacc[:, :], dnacc[:, :], expst[tb][:, :], op=OP.add))
                    va_done[(h, idx)] = ctx.inc("vector", b.sV, 1)
                    if idx >= 1:
                        emit_pv(h, idx - 1)
                emit_pv(h, 71)
                ctx.emit("vector", lambda E: E.tensor_copy(dnb[:, :], dnacc[:, :]))
                dnv = ctx.inc("vector", b.sV, 1)
                ctx.wait("tensor", b.sV, dnv)
                for half in range(2):
                    ctx.emit("tensor", lambda E, half=half: E.matmul(
                        pdn[0:1, half * 512:(half + 1) * 512], b.onescol_b[:, :],
                        dnb[:, half * 512:(half + 1) * 512], start=True, stop=True))
                vtd = ctx.inc("tensor", b.sT, 1)
                ctx.wait("vector", b.sT, vtd)
                vt = pv_done[(h, 71)]
                ctx.wait("vector", b.sT, vt)
                ctx.emit("vector", lambda E: E.tensor_copy(rd[:, :], pdn[0:1, :]))
                ctx.emit("vector", lambda E: E.tensor_copy(rdb_row[:, 0:16], rd[:, 0:16]))
                ctx.emit("vector", lambda E: E.tensor_copy(rdb_row[:, 16:32], rd[:, 16:32]))
                ctx.emit("vector", lambda E: E.reciprocal(rd2[:, :], rd[:, :]))
                ctx.emit("vector", lambda E: E.tensor_copy(rd[:, 0:16], rd[:, 0:16]))
                ctx.emit("vector", lambda E: E.tensor_copy(rd[:, 16:32], rd[:, 16:32]))
                ctx.emit("vector", lambda E: E.tensor_copy(rdb_row[:, :], rd2[:, :]))
                bcv = ctx.inc("vector", b.sV, 1)
                ctx.wait("tensor", b.sV, bcv)
                for half in range(2):
                    ctx.emit("tensor", lambda E, half=half: E.matmul(
                        psc[0][:, half * 512:(half + 1) * 512], b.ones_b[0:1, :],
                        rdb_row[0:1, half * 512:(half + 1) * 512], start=True, stop=True))
                vt2 = ctx.inc("tensor", b.sT, 1)
                ctx.wait("scalar", b.sT, vt2)
                ctx.emit("scalar", lambda E: E.activation(rdb[:, :], psc[0][:, :], AF.Identity))
                ssn = ctx.inc("scalar", b.sS, 1)
                ctx.wait("vector", b.sS, ssn)
                for half in range(2):
                    ctx.emit("vector", lambda E, h=h, half=half: E.tensor_tensor(
                        yhb[:, h * 1024 + half * 512:h * 1024 + (half + 1) * 512],
                        pyh[:, half * 512:(half + 1) * 512],
                        rdb[:, half * 512:(half + 1) * 512], op=OP.mult))
                yv = ctx.inc("vector", b.sV, 1)
                ctx.wait("tensor", b.sV, yv)
        ctx.barrier("a_attn")
        skv.close()

        with ExitStack() as sc:
            sb2 = lambda n, sh, dt: sc.enter_context(nc.sbuf_tensor(n, sh, dt))
            ps1 = lambda n, sh, dt: sc.enter_context(nc.psum_tensor(n, sh, dt))
            ao = sb2("a_ao", [128, 16 * 1024], BF16)
            ws_wo = WStream(ctx, nc, sc, "a_wwo", 2048, 3)
            for ob in range(16):
                ws_wo.load(dram["wo"][ob])
            pwo = [ps1(f"a_pwo{i}", [128, 512], F32) for i in range(2)]
            for ob in range(16):
                wb = ws_wo.use("tensor")
                for half in range(2):
                    for hh in range(16):
                        ctx.emit("tensor", lambda E, ob=ob, half=half, hh=hh, w=wb: E.matmul(
                            pwo[half][:, :], w[:, hh * 128:(hh + 1) * 128],
                            yhb[:, hh * 1024 + half * 512:hh * 1024 + (half + 1) * 512],
                            start=(hh == 0), stop=(hh == 15)))
                vt = ctx.inc("tensor", b.sT, 1)
                ctx.wait("scalar", b.sT, vt)
                for half in range(2):
                    ctx.emit("scalar", lambda E, ob=ob, half=half: E.activation(
                        ao[:, ob * 1024 + half * 512:ob * 1024 + (half + 1) * 512],
                        pwo[half][:, :], AF.Identity))
                ss = ctx.inc("scalar", b.sS, 1)
                ctx.wait("tensor", b.sS, ss)
                ws_wo.advance(1, b.sT, vt)
            x2r = [sb2(f"a_x2r{i}", [128, 2048], F32) for i in range(2)]
            att = sb2("a_att", [128, 2048], F32)
            prev_ln = [0]
            pstr = ps1("a_pstr", [128, 512], BF16)
            ldh = ctx.sem("a_ldh")
            hdone = []
            for tt in range(8):
                i = tt % 2
                if tt >= 2:
                    ctx.wait("sync", b.sV, hdone[tt - 2])
                ctx.dma("sync", x2r[i][:, :], dram["hx2"][tt * 128:(tt + 1) * 128, :], ldh)
                ctx.wait("vector", b.sS, prev_ln[0])
                for ob in range(16):
                    q = ob % 4
                    ctx.emit("tensor", lambda E, ob=ob, tt=tt, q=q: E.matmul(
                        pstr[:, q * 128:(q + 1) * 128],
                        ao[:, ob * 1024 + tt * 128:ob * 1024 + (tt + 1) * 128],
                        b.eye_b[:, :], is_transpose=True, start=True, stop=True))
                    if q == 3:
                        vt = ctx.inc("tensor", b.sT, 1)
                        ctx.wait("vector", b.sT, vt)
                        for qq in range(4):
                            obq = ob - 3 + qq
                            ctx.emit("vector", lambda E, obq=obq, qq=qq: E.scalar_tensor_tensor(
                                att[:, obq * 128:(obq + 1) * 128],
                                pstr[:, qq * 128:(qq + 1) * 128], 1.0,
                                bob[:, obq * 128:(obq + 1) * 128], op0=OP.mult, op1=OP.add))
                        vv = ctx.inc("vector", b.sV, 1)
                        ctx.wait("tensor", b.sV, vv)
                ctx.wait("scalar", b.sV, b.sV.v)
                if tt >= 1:
                    ctx.wait("scalar", b.stx, b.stx.v)
                sEnd = emit_ln(ctx, b, att[:, :], b.scr[:, 0:2048], b.scr, b.stat)
                prev_ln[0] = sEnd
                ctx.wait("vector", b.sS, sEnd)
                ctx.wait("vector", ldh, 16 * (tt + 1))
                ctx.emit("vector", lambda E, i=i: E.tensor_tensor(
                    x2r[i][:, :], x2r[i][:, :], b.scr[:, 0:2048], op=OP.add))
                vv = ctx.inc("vector", b.sV, 1)
                hdone.append(vv)
                ctx.wait("sync", b.sV, vv)
                ctx.dma("sync", dram["hx3"][tt * 128:(tt + 1) * 128, :], x2r[i][:, :], b.stx)


def emit_p4(ctx, nc, b, dram):
    with ExitStack() as st_:
        sbl = lambda n, sh, dt: st_.enter_context(nc.sbuf_tensor(n, sh, dt))
        psl = lambda n, sh, dt: st_.enter_context(nc.psum_tensor(n, sh, dt))
        xn4T = sbl("e_xn4T", [128, 16 * 1024], BF16)
        gb = sbl("e_gb", [128, 8 * 1024], BF16)
        macc = sbl("e_macc", [128, 16 * 1024], F32)
        gstr = sbl("e_gstr", [128, 22 * 1024], BF16)
        gatew_s = sbl("e_gatew", [128, 128], BF16)
        gates = sbl("e_gates", [128, 64], F32)
        gstat = sbl("e_gstat", [128, 48], F32)
        gatesT = sbl("e_gatesT", [8, 1024], F32)
        gatesTb = sbl("e_gatesTb", [8, 1024], BF16)
        rbb = sbl("e_rbb", [128, 8], F32)
        rbrow = sbl("e_rbrow", [1, 8], F32)
        ldp = ctx.sem("e_ldp")
        ldpg = ctx.sem("e_ldpg")
        ctx.dma("gpsimd", gatew_s[:, :], dram["gatew"], ldpg)
        ctx.dma("sync", rbrow[:, :], dram["rbias"], ldp)
        ldp_all = ldp.v

        with ExitStack() as sc:
            sb2 = lambda n, sh, dt: sc.enter_context(nc.sbuf_tensor(n, sh, dt))
            ps1 = lambda n, sh, dt: sc.enter_context(nc.psum_tensor(n, sh, dt))
            x3t = [sb2(f"e_x3t{i}", [128, 2048], F32) for i in range(2)]
            xn4 = sb2("e_xn4", [128, 2048], BF16)
            trp = ps1("e_trp", [128, 512], BF16)
            psr = ps1("e_psr", [128, 8], F32)
            psg = ps1("e_psg", [8, 128], F32)
            psbc2 = ps1("e_psbc2", [128, 512], F32)
            ldx = ctx.sem("e_ldx")
            edone = []
            ctx.wait("tensor", ldp, ldp_all)
            ctx.wait("tensor", ldpg, 16)
            ctx.emit("tensor", lambda E: E.matmul(psbc2[:, 0:8], b.ones_f[0:1, :],
                                                  rbrow[0:1, :], start=True, stop=True))
            vt = ctx.inc("tensor", b.sT, 1)
            ctx.wait("scalar", b.sT, vt)
            ctx.emit("scalar", lambda E: E.activation(rbb[:, :], psbc2[:, 0:8], AF.Identity))
            ssr = ctx.inc("scalar", b.sS, 1)
            ctx.wait("tensor", b.sS, ssr)
            for tt in range(8):
                i = tt % 2
                if tt >= 2:
                    ctx.wait("sync", b.sS, edone[tt - 2])
                ctx.dma("sync", x3t[i][:, :], dram["hx3"][tt * 128:(tt + 1) * 128, :], ldx)
                ctx.wait("scalar", ldx, 16 * (tt + 1))
                if tt >= 1:
                    ctx.wait("scalar", b.stx, b.stx.v)
                sEnd = emit_ln(ctx, b, x3t[i][:, :], b.scr[:, 0:2048], b.scr, b.stat)
                ctx.emit("scalar", lambda E: E.activation(xn4[:, :], b.scr[:, 0:2048],
                                                          AF.Identity))
                s2 = ctx.inc("scalar", b.sS, 1)
                ctx.wait("tensor", b.sS, s2)
                emit_tr_strip(ctx, b, xn4, 16,
                              (lambda tt: (lambda dm: dm * 1024 + tt * 128))(tt),
                              xn4T, trp)
                for dm in range(16):
                    ctx.emit("tensor", lambda E, tt=tt, dm=dm: E.matmul(
                        psr[:, :], xn4T[:, dm * 1024 + tt * 128:dm * 1024 + (tt + 1) * 128],
                        gatew_s[:, dm * 8:(dm + 1) * 8], start=(dm == 0), stop=(dm == 15)))
                vt = ctx.inc("tensor", b.sT, 1)
                ctx.wait("scalar", b.sT, vt)
                ctx.emit("scalar", lambda E, tt=tt: E.activation(
                    gates[:, tt * 8:(tt + 1) * 8], psr[:, :], AF.Tanh, scale=0.5))
                ss = ctx.inc("scalar", b.sS, 1)
                edone.append(ss)
                ctx.wait("tensor", b.sS, ss)
            sg_all = ctx.inc("scalar", b.sS, 0) if False else b.sS.v
            ctx.wait("vector", b.sS, sg_all)
            G = gstat
            for tt in range(8):
                ctx.emit("vector", lambda E, tt=tt: E.tensor_scalar(
                    gates[:, tt * 8:(tt + 1) * 8], gates[:, tt * 8:(tt + 1) * 8],
                    0.5, 0.5, op0=OP.mult, op1=OP.add))
            for tt in range(8):
                ctx.emit("vector", lambda E, tt=tt: E.tensor_tensor(
                    b.scr[:, tt * 8:tt * 8 + 8], gates[:, tt * 8:(tt + 1) * 8],
                    rbb[:, :], op=OP.add))
            for tt in range(8):
                ctx.emit("vector", lambda E, tt=tt: E.tensor_reduce(
                    G[:, tt:tt + 1], b.scr[:, tt * 8:tt * 8 + 8], AX.X, OP.max))
            for tt in range(8):
                ctx.emit("vector", lambda E, tt=tt: E.tensor_scalar(
                    b.scr[:, 64 + tt * 8:64 + tt * 8 + 8], b.scr[:, tt * 8:tt * 8 + 8],
                    G[:, tt:tt + 1], None, op0=OP.is_ge))
            for tt in range(8):
                ctx.emit("vector", lambda E, tt=tt: E.scalar_tensor_tensor(
                    b.scr[:, 64 + tt * 8:64 + tt * 8 + 8],
                    b.scr[:, 64 + tt * 8:64 + tt * 8 + 8], NEG,
                    b.scr[:, tt * 8:tt * 8 + 8], op0=OP.mult, op1=OP.add))
            for tt in range(8):
                ctx.emit("vector", lambda E, tt=tt: E.tensor_reduce(
                    G[:, 8 + tt:9 + tt], b.scr[:, 64 + tt * 8:64 + tt * 8 + 8], AX.X, OP.max))
            for tt in range(8):
                ctx.emit("vector", lambda E, tt=tt: E.tensor_scalar(
                    b.scr[:, tt * 8:tt * 8 + 8], b.scr[:, tt * 8:tt * 8 + 8],
                    G[:, 8 + tt:9 + tt], None, op0=OP.is_ge))
            for tt in range(8):
                ctx.emit("vector", lambda E, tt=tt: E.tensor_tensor(
                    gates[:, tt * 8:(tt + 1) * 8], gates[:, tt * 8:(tt + 1) * 8],
                    b.scr[:, tt * 8:tt * 8 + 8], op=OP.mult))
            for tt in range(8):
                ctx.emit("vector", lambda E, tt=tt: E.tensor_reduce(
                    G[:, 16 + tt:17 + tt], gates[:, tt * 8:(tt + 1) * 8], AX.X, OP.add))
            for tt in range(8):
                ctx.emit("vector", lambda E, tt=tt: E.tensor_scalar(
                    G[:, 24 + tt:25 + tt], G[:, 16 + tt:17 + tt], 1e-12, None, op0=OP.max))
            for tt in range(8):
                ctx.emit("vector", lambda E, tt=tt: E.reciprocal(
                    G[:, 32 + tt:33 + tt], G[:, 24 + tt:25 + tt]))
            for tt in range(8):
                ctx.emit("vector", lambda E, tt=tt: E.tensor_scalar(
                    gates[:, tt * 8:(tt + 1) * 8], gates[:, tt * 8:(tt + 1) * 8],
                    G[:, 32 + tt:33 + tt], None, op0=OP.mult))
            gv = ctx.inc("vector", b.sV, 1)
            ctx.wait("tensor", b.sV, gv)
            for tt in range(8):
                ctx.emit("tensor", lambda E, tt=tt: E.matmul(
                    psg[:, :], gates[:, tt * 8:(tt + 1) * 8], b.eye_f[:, :],
                    is_transpose=True, start=True, stop=True))
                vt = ctx.inc("tensor", b.sT, 1)
                ctx.wait("scalar", b.sT, vt)
                ctx.emit("scalar", lambda E, tt=tt: E.activation(
                    gatesT[:, tt * 128:(tt + 1) * 128], psg[:, :], AF.Identity))
                ss = ctx.inc("scalar", b.sS, 1)
                ctx.wait("tensor", b.sS, ss)
            ctx.wait("vector", b.sS, b.sS.v)
            ctx.emit("vector", lambda E: E.tensor_copy(gatesTb[:, :], gatesT[:, :]))
            gtv = ctx.inc("vector", b.sV, 1)
            ctx.wait("tensor", b.sV, gtv)
            esel_s = sb2("e_esel", [8, 8 * 128], BF16)
            ldes = ctx.sem("e_ldes")
            ctx.dma("gpsimd", esel_s[:, :], dram["esel"], ldes)
            ctx.wait("tensor", ldes, 16)
            for e in range(8):
                for half in range(2):
                    ctx.emit("tensor", lambda E, e=e, half=half: E.matmul(
                        psbc2[:, :], esel_s[:, e * 128:(e + 1) * 128],
                        gatesTb[:, half * 512:(half + 1) * 512], start=True, stop=True))
                    vt = ctx.inc("tensor", b.sT, 1)
                    ctx.wait("scalar", b.sT, vt)
                    ctx.emit("scalar", lambda E, e=e, half=half: E.activation(
                        gb[:, e * 1024 + half * 512:e * 1024 + (half + 1) * 512],
                        psbc2[:, :], AF.Identity))
                    ss = ctx.inc("scalar", b.sS, 1)
                    ctx.wait("tensor", b.sS, ss)

        ctx.barrier("e_router")
        with ExitStack() as sc:
            sb2 = lambda n, sh, dt: sc.enter_context(nc.sbuf_tensor(n, sh, dt))
            ps1 = lambda n, sh, dt: sc.enter_context(nc.psum_tensor(n, sh, dt))
            s1b = [sb2(f"e_s1b{i}", [128, 1024], BF16) for i in range(2)]
            p4 = [ps1(f"e_p4_{i}", [128, 512], F32) for i in range(8)]
            ws = WStream(ctx, nc, sc, "e_ws", 2816, 4)
            for i2 in range(22):
                ws.load(dram["sw1"][i2])
                ws.load(dram["sw3"][i2])
            for ob in range(16):
                ws.load(dram["sw2"][ob])
            for e in range(8):
                for ib in range(11):
                    ws.load(dram["w1"][e, ib])
                    ws.load(dram["w3"][e, ib])
                for ob in range(16):
                    ws.load(dram["w2"][e, ob])

            def ffn_block(nblk, gate_e):
                vdone = {}
                for ib in range(nblk):
                    pr = ib % 2
                    ph1 = [p4[pr * 4 + 0], p4[pr * 4 + 1]]
                    ph3 = [p4[pr * 4 + 2], p4[pr * 4 + 3]]
                    w1b = ws.peek("tensor", 0)
                    w3b = ws.peek("tensor", 1)
                    if ib >= 2:
                        ctx.wait("tensor", b.sV, vdone[ib - 2])
                    for half in range(2):
                        for dm in range(16):
                            ctx.emit("tensor", lambda E, half=half, dm=dm, w=w1b, p=ph1:
                                     E.matmul(p[half][:, :], w[:, dm * 128:(dm + 1) * 128],
                                              xn4T[:, dm * 1024 + half * 512:
                                                   dm * 1024 + (half + 1) * 512],
                                              start=(dm == 0), stop=(dm == 15)))
                        for dm in range(16):
                            ctx.emit("tensor", lambda E, half=half, dm=dm, w=w3b, p=ph3:
                                     E.matmul(p[half][:, :], w[:, dm * 128:(dm + 1) * 128],
                                              xn4T[:, dm * 1024 + half * 512:
                                                   dm * 1024 + (half + 1) * 512],
                                              start=(dm == 0), stop=(dm == 15)))
                    vt = ctx.inc("tensor", b.sT, 1)
                    ws.advance(2, b.sT, vt)
                    ctx.wait("scalar", b.sT, vt)
                    if ib >= 2:
                        ctx.wait("scalar", b.sV, vdone[ib - 2])
                    for half in range(2):
                        ctx.emit("scalar", lambda E, half=half, pr=pr, p=ph1: E.activation(
                            s1b[pr][:, half * 512:(half + 1) * 512], p[half][:, :], AF.Silu))
                    ss = ctx.inc("scalar", b.sS, 1)
                    ctx.wait("vector", b.sS, ss)
                    for half in range(2):
                        ctx.emit("vector", lambda E, ib=ib, half=half, pr=pr, p=ph3:
                                 E.tensor_tensor(
                                     gstr[:, ib * 1024 + half * 512:ib * 1024 + (half + 1) * 512],
                                     s1b[pr][:, half * 512:(half + 1) * 512],
                                     p[half][:, :], op=OP.mult))
                    if gate_e is not None:
                        ctx.emit("vector", lambda E, ib=ib, e=gate_e: E.tensor_tensor(
                            gstr[:, ib * 1024:(ib + 1) * 1024],
                            gstr[:, ib * 1024:(ib + 1) * 1024],
                            gb[:, e * 1024:(e + 1) * 1024], op=OP.mult))
                    vdone[ib] = ctx.inc("vector", b.sV, 1)
                ctx.wait("tensor", b.sV, vdone[nblk - 1])

            def w2_pass(nblk, first):
                edone = {}
                for ob in range(16):
                    pr = ob % 2
                    pw = [p4[pr * 4 + 0], p4[pr * 4 + 1]]
                    w2b = ws.use("tensor")
                    if ob >= 2:
                        ctx.wait("tensor", b.sV if not first else b.sS, edone[ob - 2])
                    for half in range(2):
                        for ib in range(nblk):
                            ctx.emit("tensor", lambda E, half=half, ib=ib, w=w2b, p=pw:
                                     E.matmul(p[half][:, :], w[:, ib * 128:(ib + 1) * 128],
                                              gstr[:, ib * 1024 + half * 512:
                                                   ib * 1024 + (half + 1) * 512],
                                              start=(ib == 0), stop=(ib == nblk - 1)))
                    vt = ctx.inc("tensor", b.sT, 1)
                    ws.advance(1, b.sT, vt)
                    if first:
                        ctx.wait("scalar", b.sT, vt)
                        for half in range(2):
                            ctx.emit("scalar", lambda E, ob=ob, half=half, p=pw: E.activation(
                                macc[:, ob * 1024 + half * 512:ob * 1024 + (half + 1) * 512],
                                p[half][:, :], AF.Identity))
                        edone[ob] = ctx.inc("scalar", b.sS, 1)
                    else:
                        ctx.wait("vector", b.sT, vt)
                        for half in range(2):
                            ctx.emit("vector", lambda E, ob=ob, half=half, p=pw: E.tensor_tensor(
                                macc[:, ob * 1024 + half * 512:ob * 1024 + (half + 1) * 512],
                                macc[:, ob * 1024 + half * 512:ob * 1024 + (half + 1) * 512],
                                p[half][:, :], op=OP.add))
                        edone[ob] = ctx.inc("vector", b.sV, 1)
                ctx.wait("tensor", b.sS if first else b.sV,
                         edone[15])

            ffn_block(22, None)
            w2_pass(22, True)
            for e in range(8):
                ffn_block(11, e)
                w2_pass(11, False)
        ctx.barrier("e_acc")

        with ExitStack() as sc:
            sb2 = lambda n, sh, dt: sc.enter_context(nc.sbuf_tensor(n, sh, dt))
            ps1 = lambda n, sh, dt: sc.enter_context(nc.psum_tensor(n, sh, dt))
            x3r = [sb2(f"e_x3r{i}", [128, 2048], F32) for i in range(2)]
            mot = sb2("e_mot", [128, 2048], F32)
            pstr = ps1("e_pstr", [128, 512], F32)
            ldh = ctx.sem("e_ldh")
            sto = ctx.sem("e_sto")
            hdone = []
            for tt in range(8):
                i = tt % 2
                if tt >= 2:
                    ctx.wait("sync", b.sV, hdone[tt - 2])
                ctx.dma("sync", x3r[i][:, :], dram["hx3"][tt * 128:(tt + 1) * 128, :], ldh)
                for ob in range(16):
                    q = ob % 4
                    ctx.emit("tensor", lambda E, ob=ob, tt=tt, q=q: E.matmul(
                        pstr[:, q * 128:(q + 1) * 128],
                        macc[:, ob * 1024 + tt * 128:ob * 1024 + (tt + 1) * 128],
                        b.eye_f[:, :], is_transpose=True, start=True, stop=True))
                    if q == 3:
                        vt = ctx.inc("tensor", b.sT, 1)
                        ctx.wait("scalar", b.sT, vt)
                        for qq in range(4):
                            obq = ob - 3 + qq
                            ctx.emit("scalar", lambda E, obq=obq, qq=qq: E.activation(
                                mot[:, obq * 128:(obq + 1) * 128],
                                pstr[:, qq * 128:(qq + 1) * 128], AF.Identity))
                        ss = ctx.inc("scalar", b.sS, 1)
                        ctx.wait("tensor", b.sS, ss)
                if tt >= 1:
                    ctx.wait("scalar", b.stx, b.stx.v)
                sEnd = emit_ln(ctx, b, mot[:, :], b.scr[:, 0:2048], b.scr, b.stat)
                ctx.wait("vector", b.sS, sEnd)
                ctx.wait("vector", ldh, 16 * (tt + 1))
                ctx.emit("vector", lambda E, i=i: E.tensor_tensor(
                    x3r[i][:, :], x3r[i][:, :], b.scr[:, 0:2048], op=OP.add))
                vv = ctx.inc("vector", b.sV, 1)
                hdone.append(vv)
                ctx.wait("sync", b.sV, vv)
                ctx.dma("sync", dram["out"][tt * 128:(tt + 1) * 128, :], x3r[i][:, :], sto)
            ctx.wait("gpsimd", sto, 16 * 8)


_CACHE = {}


def kernel(**inputs):
    x = inputs["L0_attn_hidden_states"].astype(np.float32, copy=False)
    wp = _prep_weights(inputs)
    stru = _structural()
    if "nc" not in _CACHE:
        _CACHE["nc"] = build()
    nc = _CACHE["nc"]
    in_maps = []
    for c in range(NCORES):
        m = dict(wp)
        m.update(_prep_core(c, x))
        m.update(stru)
        in_maps.append(m)
    res = run_bass_kernel_spmd(nc, in_maps, core_ids=list(range(NCORES)))
    outs = [res.results[c]["out"] for c in range(NCORES)]
    return np.concatenate(outs, axis=0).reshape(1, T, DM)


# version 12 (final)
# speedup vs baseline: 1.3445x; 1.0015x over previous
import numpy as np
from contextlib import ExitStack

import concourse.bass as bass
import concourse.mybir as mybir
from concourse.bass_utils import run_bass_kernel_spmd
from concourse.alu_op_type import AluOpType as OP

F32 = mybir.dt.float32
BF16 = mybir.dt.bfloat16
AF = mybir.ActivationFunctionType
AX = mybir.AxisListType

NCORES = 8
T = 8192
S = T // NCORES
DM = 2048
NEG = -30000.0
ISC = 0.08838834764831845

ENGS = ("tensor", "vector", "scalar", "gpsimd", "sync")


class Sem:
    __slots__ = ("name", "v", "h")

    def __init__(self, name):
        self.name, self.v, self.h = name, 0, None

    def bump(self, n):
        self.v += n
        return self.v


class Ctx:
    def __init__(self):
        self.ops = {e: [] for e in ENGS}
        self.sems = []
        self.dma_sems = {}

    def sem(self, name):
        s = Sem(name)
        self.sems.append(s)
        return s

    def emit(self, eng, fn, fusable=True):
        fn._fusable = fusable
        self.ops[eng].append(fn)

    def wait(self, eng, sem, val=None):
        v = sem.v if val is None else val
        if v <= 0:
            return
        self.emit(eng, lambda E, s=sem, v=v: E.wait_ge(s.h, v), fusable=False)

    def inc(self, eng, sem, n=1):
        v = sem.bump(n)
        ops = self.ops[eng]
        if ops and getattr(ops[-1], "_fusable", False):
            prev = ops[-1]

            def fused(E, prev=prev, s=sem, n=n):
                r = prev(E)
                r.then_inc(s.h, n)
            fused._fusable = False
            ops[-1] = fused
        else:
            def fn(E, s=sem, n=n):
                E.drain()
                E.sem_inc(s.h, n)
            fn._fusable = False
            self.emit(eng, fn, fusable=False)
        return v

    def dma(self, q, out, in_, sem):
        self.dma_sems[sem] = q
        if sem.v > 0:
            self.wait(q, sem, sem.v)
        v = sem.bump(16)
        self.emit(q, lambda E, o=out, i=in_, s=sem: E.dma_start(out=o, in_=i).then_inc(s.h, 16),
                  fusable=False)
        return v

    def barrier(self, name):
        for ds, q in list(self.dma_sems.items()):
            if ds.v > 0:
                self.wait(q, ds, ds.v)
        s = self.sem("bar_" + name)

        def fn(E, s=s):
            E.drain()
            E.sem_inc(s.h, 1)
        for e in ENGS:
            self.emit(e, fn)
        s.v = len(ENGS)
        for e in ENGS:
            self.wait(e, s, len(ENGS))


def replay(nc, ctx):
    with ExitStack() as st:
        for s in ctx.sems:
            s.h = st.enter_context(nc.semaphore(s.name))
        block = st.enter_context(nc.Block())

        def mk(e):
            def body(E):
                for fn in ctx.ops[e]:
                    fn(E)
            return body

        for e in ENGS:
            getattr(block, e)(mk(e))


class WStream:
    def __init__(self, ctx, nc, st_, name, cols_max, nbuf=3):
        self.ctx, self.name, self.nbuf = ctx, name, nbuf
        self.bufs = [st_.enter_context(nc.sbuf_tensor(f"{name}{i}", [128, cols_max], BF16))
                     for i in range(nbuf)]
        self.ld = ctx.sem(name + "_ld")
        self.free = ctx.sem(name + "_free")
        self.n_loaded = 0
        self.n_used = 0
        self.n_emitted = 0
        self.pending = []

    def load(self, dram_ap):
        if self.n_loaded - len(self.pending) < self.nbuf and not self.pending:
            self._emit_load(dram_ap)
        else:
            self.pending.append(dram_ap)
        self.n_loaded += 1

    def _emit_load(self, dram_ap):
        buf = self.bufs[self.n_emitted % self.nbuf]
        cols = dram_ap.shape[-1]
        self.ctx.dma("gpsimd", buf[:, 0:cols], dram_ap, self.ld)
        self.n_emitted += 1

    def advance(self, k, gate_sem, gate_val):
        self.n_used += k
        if self.pending:
            self.ctx.wait("gpsimd", gate_sem, gate_val)
            for _ in range(min(k, len(self.pending))):
                self._emit_load(self.pending.pop(0))

    def use(self, eng):
        i = self.n_used
        self.ctx.wait(eng, self.ld, 16 * (i + 1))
        return self.bufs[i % self.nbuf]

    def done(self, eng):
        self.n_used += 1
        self.ctx.inc(eng, self.free, 1)

    def peek(self, eng, k):
        i = self.n_used + k
        self.ctx.wait(eng, self.ld, 16 * (i + 1))
        return self.bufs[i % self.nbuf]

    def release(self, eng, k):
        self.n_used += k
        self.ctx.inc(eng, self.free, k)

    def free_via(self, eng, k, gate_sem, gate_val):
        self.n_used += k
        self.ctx.wait(eng, gate_sem, gate_val)
        self.ctx.inc(eng, self.free, k)


def _prep_weights(w):
    C = np.ascontiguousarray
    p = {}
    p["inpw"] = C(w["L0_attn_in_proj_w"].reshape(64, 128, 16, 128)
                  .transpose(0, 3, 2, 1)).reshape(64, 128, 2048)
    p["outw"] = C(w["L0_attn_out_proj_w"].reshape(16, 128, 32, 128)
                  .transpose(0, 3, 2, 1)).reshape(16, 128, 4096)
    p["xpw"] = C(w["L0_attn_x_proj_w"].reshape(160, 32, 128)
                 .transpose(2, 1, 0)).reshape(128, 32 * 160)
    p["dtw"] = C(w["L0_attn_dt_proj_w"].reshape(32, 128, 128)
                 .transpose(2, 0, 1)).reshape(128, 32 * 128)
    p["convw"] = C(w["L0_attn_conv_w"].reshape(32, 128, 4)
                   .transpose(1, 0, 2)).reshape(128, 128)
    p["convb"] = C(w["L0_attn_conv_b"].reshape(32, 128).T)
    p["dtb"] = C(w["L0_attn_dt_proj_b"].reshape(32, 128).T)
    p["dvec"] = C(w["L0_attn_D"].reshape(32, 128).T)
    p["alog"] = C(w["L0_attn_A_log"].reshape(32, 128, 16)
                  .transpose(1, 0, 2)).reshape(128, 512)
    p["nsc"] = w["L0_attn_norm_scale"].reshape(1, 2048)
    p["ffnw"] = C(w["L0_ffn_weight"].T.reshape(16, 128, 2048)
                  .transpose(1, 0, 2)).reshape(128, 16 * 2048)
    for nm, key in (("wq", "L1_attn_Wq"), ("wk", "L1_attn_Wk"), ("wv", "L1_attn_Wv")):
        p[nm] = C(w[key].reshape(16, 16, 128, 128).transpose(0, 2, 1, 3)).reshape(16, 128, 2048)
    p["wo"] = C(w["L1_attn_Wo"].reshape(16, 128, 16, 128)
                .transpose(2, 1, 0, 3)).reshape(16, 128, 2048)
    p["bq"] = C(w["L1_attn_bq"].T)
    p["bk"] = C(w["L1_attn_bk"].T)
    p["bv"] = C(w["L1_attn_bv"]).reshape(1, 2048)
    p["bo"] = w["L1_attn_bo"].reshape(1, 2048)
    p["w1"] = C(w["L1_ffn_W1"].reshape(8, 16, 128, 11, 128)
                .transpose(0, 3, 2, 1, 4)).reshape(8, 11, 128, 2048)
    p["w3"] = C(w["L1_ffn_W3"].reshape(8, 16, 128, 11, 128)
                .transpose(0, 3, 2, 1, 4)).reshape(8, 11, 128, 2048)
    p["w2"] = C(w["L1_ffn_W2"].reshape(8, 11, 128, 16, 128)
                .transpose(0, 3, 2, 1, 4)).reshape(8, 16, 128, 1408)
    p["sw1"] = C(w["L1_ffn_sw1"].reshape(22, 128, 16, 128)
                 .transpose(0, 3, 2, 1)).reshape(22, 128, 2048)
    p["sw3"] = C(w["L1_ffn_sw3"].reshape(22, 128, 16, 128)
                 .transpose(0, 3, 2, 1)).reshape(22, 128, 2048)
    p["sw2"] = C(w["L1_ffn_sw2"].reshape(16, 128, 22, 128)
                 .transpose(0, 3, 2, 1)).reshape(16, 128, 2816)
    p["gatew"] = C(w["L1_ffn_gate_w"].T.reshape(16, 128, 8)
                   .transpose(1, 0, 2)).reshape(128, 128)
    p["rbias"] = w["L1_ffn_router_bias"].reshape(1, 8)
    return p


def _prep_core(c, x):
    d = {}
    t0 = c * S
    d["x_sh"] = np.ascontiguousarray(x[0, t0:t0 + S, :])
    d["xprev"] = (np.ascontiguousarray(x[0, t0 - 128:t0, :]) if c > 0
                  else np.zeros((128, DM), np.float32))
    sel = np.zeros((128, 8), np.float32)
    sel[:, c] = 1.0
    d["selmask"] = sel
    bt = np.zeros((128, 64), np.float32)
    for jg in range(64):
        if jg // 8 >= c:
            bt[:, jg] = NEG
    d["bias_tab"] = bt
    return d


def _structural():
    s = {"eye": np.eye(128, dtype=np.float32)}
    rsel = np.zeros((32, 32 * 128), np.float32)
    for n in range(32):
        rsel[n, n * 128:(n + 1) * 128] = 1.0
    s["rsel"] = rsel
    esel = np.zeros((8, 8 * 128), np.float32)
    for e in range(8):
        esel[e, e * 128:(e + 1) * 128] = 1.0
    s["esel"] = esel
    m = np.zeros((128, 8 * 1024), np.float32)
    pp = np.arange(128)[:, None]
    tt = np.arange(1024)[None, :]
    for l in range(8):
        m[:, l * 1024:(l + 1) * 1024] = (l * 128 + pp <= tt).astype(np.float32)
    s["lmask"] = m
    return s


def emit_ln(ctx, b, xin, xout, scr, stat, d=2048, eps=1e-5):
    mean, rstd, msq, bias = stat[:, 0:1], stat[:, 1:2], stat[:, 2:3], stat[:, 3:4]
    ctx.emit("scalar", lambda E: E.activation(scr[:, 0:d], xin, AF.Square))
    vS = ctx.inc("scalar", b.sS, 1)
    ctx.wait("vector", b.sS, vS)
    ctx.emit("vector", lambda E: E.tensor_scalar(scr[:, 0:d], scr[:, 0:d], 1.0, None,
                                                 op0=OP.mult, op1=OP.add, accum_out=rstd))
    ctx.emit("vector", lambda E: E.tensor_copy(stat[:, 4:8], stat[:, 0:4]))
    ctx.emit("vector", lambda E: E.tensor_scalar(scr[:, 0:d], xin, 1.0, None,
                                                 op0=OP.mult, op1=OP.add, accum_out=mean))
    ctx.emit("vector", lambda E: E.tensor_copy(stat[:, 4:8], stat[:, 0:4]))
    ctx.emit("vector", lambda E: E.tensor_scalar(rstd, rstd, 1.0 / d, None, op0=OP.mult))
    ctx.emit("vector", lambda E: E.tensor_scalar(mean, mean, 1.0 / d, None, op0=OP.mult))
    ctx.emit("vector", lambda E: E.tensor_tensor(msq, mean, mean, op=OP.mult))
    ctx.emit("vector", lambda E: E.tensor_tensor(rstd, rstd, msq, op=OP.subtract))
    ctx.emit("vector", lambda E: E.tensor_scalar(rstd, rstd, 1.0, eps, op0=OP.mult, op1=OP.add))
    vV = ctx.inc("vector", b.sV, 1)
    ctx.wait("scalar", b.sV, vV)
    ctx.emit("scalar", lambda E: E.activation(rstd, rstd, AF.Ln))
    ctx.emit("scalar", lambda E: E.activation(rstd, rstd, AF.Exp, scale=-0.5))
    vS2 = ctx.inc("scalar", b.sS, 1)
    ctx.wait("vector", b.sS, vS2)
    ctx.emit("vector", lambda E: E.tensor_copy(stat[:, 4:8], stat[:, 0:4]))
    ctx.emit("vector", lambda E: E.tensor_scalar(bias, mean, rstd[:, 0:1], -1.0,
                                                 op0=OP.mult, op1=OP.mult))
    vV2 = ctx.inc("vector", b.sV, 1)
    ctx.wait("scalar", b.sV, vV2)
    ctx.emit("scalar", lambda E: E.activation(xout, xin, AF.Identity, scale=rstd, bias=bias))
    return ctx.inc("scalar", b.sS, 1)


def emit_tr_strip(ctx, b, src_bf, nblk, col_fn, dst, psum):
    for i in range(nblk):
        q = i % 4
        ctx.emit("tensor", lambda E, i=i, q=q: E.matmul(
            psum[:, q * 128:(q + 1) * 128], src_bf[:, i * 128:(i + 1) * 128],
            b.eye_b[:, :], is_transpose=True, start=True, stop=True))
        if q == 3 or i == nblk - 1:
            vt = ctx.inc("tensor", b.sT, 1)
            ctx.wait("scalar", b.sT, vt)
            for qq in range(q + 1):
                i0 = i - q + qq
                ctx.emit("scalar", lambda E, i0=i0, qq=qq: E.activation(
                    dst[:, col_fn(i0):col_fn(i0) + 128],
                    psum[:, qq * 128:(qq + 1) * 128], AF.Identity))
            ss = ctx.inc("scalar", b.sS, 1)
            ctx.wait("tensor", b.sS, ss)
    return b.sS.v


def build():
    nc = bass.Bass("TRN2", target_bir_lowering=False, debug=False,
                   detect_race_conditions=False)
    ctx = Ctx()
    dram = {}

    def din(n, sh, dt=F32):
        dram[n] = nc.dram_tensor(n, list(sh), dt, kind="ExternalInput").ap()

    def dout(n, sh, dt=F32):
        dram[n] = nc.dram_tensor(n, list(sh), dt, kind="ExternalOutput").ap()

    import os as _os
    _dbg = set((_os.environ.get("MK_DBG") or "").split(",")) - {""}

    def dint(n, sh, dt=F32):
        if n in _dbg:
            dram[n] = nc.dram_tensor(n, list(sh), dt, kind="ExternalOutput").ap()
        else:
            dram[n] = nc.dram_tensor(n, list(sh), dt).ap()

    din("x_sh", (S, DM)); din("xprev", (128, DM))
    din("selmask", (128, 8)); din("bias_tab", (128, 64))
    din("eye", (128, 128)); din("lmask", (128, 8 * 1024))
    din("rsel", (32, 32 * 128)); din("esel", (8, 8 * 128))
    din("inpw", (64, 128, 2048)); din("outw", (16, 128, 4096))
    din("xpw", (128, 32 * 160)); din("dtw", (128, 32 * 128))
    din("convw", (128, 128)); din("convb", (128, 32)); din("dtb", (128, 32))
    din("dvec", (128, 32)); din("alog", (128, 512)); din("nsc", (1, 2048))
    din("ffnw", (128, 16 * 2048))
    din("wq", (16, 128, 2048)); din("wk", (16, 128, 2048)); din("wv", (16, 128, 2048))
    din("wo", (16, 128, 2048))
    din("bq", (128, 16)); din("bk", (128, 16)); din("bv", (1, 2048)); din("bo", (1, 2048))
    din("w1", (8, 11, 128, 2048)); din("w3", (8, 11, 128, 2048)); din("w2", (8, 16, 128, 1408))
    din("sw1", (22, 128, 2048)); din("sw3", (22, 128, 2048)); din("sw2", (16, 128, 2816))
    din("gatew", (128, 128)); din("rbias", (1, 8))
    dout("out", (S, DM))
    dint("hx1", (S, DM)); dint("hx2", (S, DM)); dint("hx3", (S, DM))
    dint("hxn0", (S, DM))
    dint("hsz", (128, 32 * S), BF16); dint("hxc", (128, 32 * S), BF16)
    dint("hy0", (128, 32 * S), BF16); dint("hcumd", (128, 32 * S), BF16)
    dint("hy2", (128, 32 * S), BF16)
    dint("hwqt", (128, 16 * 2048), BF16)
    dint("cc_in", (1, 128 * 1024)); dint("cc_out", (8, 128 * 1024))
    if "hha" in _dbg:
        dram["hha"] = nc.dram_tensor("hha", [128, 1], F32, kind="ExternalOutput").ap()
    if "hxin" in _dbg:
        dram["hxin"] = nc.dram_tensor("hxin", [128, 32 * 1027], BF16, kind="ExternalOutput").ap()
    if "hxq" in _dbg:
        dram["hxq"] = nc.dram_tensor("hxq", [S, DM], BF16, kind="ExternalOutput").ap()
    if "hffn" in _dbg:
        dram["hffn"] = nc.dram_tensor("hffn", [S, DM], F32, kind="ExternalOutput").ap()
    if "hbst" in _dbg:
        dram["hbst"] = nc.dram_tensor("hbst", [128, 8], F32, kind="ExternalOutput").ap()
        dram["hvq"] = nc.dram_tensor("hvq", [128, 2048], F32, kind="ExternalOutput").ap()
        dram["hcur"] = nc.dram_tensor("hcur", [128, 2048], F32, kind="ExternalOutput").ap()
    dint("kv_in", (1, 128 * 32768), BF16); dint("kv_out", (8, 128 * 32768), BF16)

    class B:
        pass
    b = B()
    with ExitStack() as top:
        sb = lambda n, sh, dt: top.enter_context(nc.sbuf_tensor(n, sh, dt))
        b.eye_b = sb("eye_b", [128, 128], BF16)
        b.eye_f = sb("eye_f", [128, 128], F32)
        b.ones_b = sb("ones_b", [1, 128], BF16)
        b.ones_f = sb("ones_f", [1, 128], F32)
        b.onescol_b = sb("onescol_b", [128, 1], BF16)
        b.onescol_f = sb("onescol_f", [128, 1], F32)
        b.ha = sb("ha", [128, 1], F32)
        b.stat = sb("stat", [128, 8], F32)
        b.scr = sb("scr", [128, 2052], F32)
        b.sT = ctx.sem("sT"); b.sV = ctx.sem("sV"); b.sS = ctx.sem("sS")
        b.sG = ctx.sem("sG"); b.stx = ctx.sem("stx")
        ld0 = ctx.sem("ld0")
        ld0g = ctx.sem("ld0g")
        ctx.dma("sync", b.eye_f[:, :], dram["eye"], ld0)
        ctx.dma("gpsimd", b.eye_b[:, :], dram["eye"], ld0g)
        ctx.emit("gpsimd", lambda E: E.memset(b.ones_b[:, :], 1.0))
        ctx.emit("gpsimd", lambda E: E.memset(b.ones_f[:, :], 1.0))
        ctx.emit("gpsimd", lambda E: E.memset(b.onescol_b[:, :], 1.0))
        ctx.emit("gpsimd", lambda E: E.memset(b.onescol_f[:, :], 1.0))
        ctx.emit("gpsimd", lambda E: E.memset(b.stat[:, :], 0.0))
        ctx.emit("gpsimd", lambda E: E.memset(b.scr[:, :], 0.0))
        gi = ctx.inc("gpsimd", b.sG, 1)
        for e in ("tensor", "vector", "scalar"):
            ctx.wait(e, ld0, 16)
            ctx.wait(e, ld0g, 16)
            ctx.wait(e, b.sG, gi)

        import os
        nph = int(os.environ.get("MK_PHASES", "5"))
        emit_p0(ctx, nc, b, dram)
        ctx.barrier("p0")
        if nph >= 2:
            emit_p1(ctx, nc, b, dram)
            ctx.barrier("p1")
        if nph >= 3:
            emit_p2(ctx, nc, b, dram)
            ctx.barrier("p2")
        if nph >= 4:
            emit_p3(ctx, nc, b, dram)
            ctx.barrier("p3")
        if nph >= 5:
            emit_p4(ctx, nc, b, dram)

        replay(nc, ctx)
    return nc


def emit_p0(ctx, nc, b, dram):
    with ExitStack() as st_:
        sbl = lambda n, sh, dt: st_.enter_context(nc.sbuf_tensor(n, sh, dt))
        slab = [sbl(f"q_slab{i}", [128, 2048], F32) for i in range(2)]
        absw = sbl("q_absw", [128, 2048], F32)
        sgn = sbl("q_sgn", [128, 2048], F32)
        ind = sbl("q_ind", [128, 2048], F32)
        wqb = sbl("q_wqb", [128, 2048], BF16)
        junk = sbl("q_junk", [128, 2048], BF16)
        acc = sbl("q_acc", [128, 18], F32)
        ld = ctx.sem("q_ld"); fr = ctx.sem("q_fr")
        for i in range(16):
            if i >= 2:
                ctx.wait("sync", fr, i - 1)
            ctx.dma("sync", slab[i % 2][:, :], dram["ffnw"][:, i * 2048:(i + 1) * 2048], ld)
        for i in range(16):
            ctx.wait("scalar", ld, 16 * (i + 1))
            ctx.emit("scalar", lambda E, i=i: E.activation(absw[:, :], slab[i % 2][:, :],
                                                           AF.Abs))
            ss0 = ctx.inc("scalar", b.sS, 1)
            ctx.inc("scalar", fr, 1)
            ctx.wait("vector", b.sS, ss0)
            ctx.emit("vector", lambda E: E.tensor_copy(junk[:, 0:16], absw[:, 0:16]))
            ctx.emit("vector", lambda E, i=i: E.tensor_scalar(
                junk[:, :], absw[:, :], 1.0, None, op0=OP.mult, op1=OP.add,
                accum_out=acc[:, i:i + 1]))
        ctx.emit("vector", lambda E: E.tensor_copy(junk[:, 0:16], absw[:, 0:16]))
        ctx.emit("vector", lambda E: E.tensor_reduce(acc[:, 16:17], acc[:, 0:16], AX.X, OP.add))
        vv = ctx.inc("vector", b.sV, 1)
        with nc.psum_tensor("q_ps", [128, 8], F32) as ps:
            ctx.wait("tensor", b.sV, vv)
            ctx.emit("tensor", lambda E: E.matmul(ps[0:1, 0:1], b.onescol_f[:, :],
                                                  acc[:, 16:17], start=True, stop=True))
            vt = ctx.inc("tensor", b.sT, 1)
            ctx.wait("vector", b.sT, vt)
            ctx.emit("vector", lambda E: E.tensor_scalar(
                acc[0:1, 17:18], ps[0:1, 0:1], 1.0 / (2048.0 * 2048.0), 1e-8,
                op0=OP.mult, op1=OP.max))
            vv2 = ctx.inc("vector", b.sV, 1)
            ctx.wait("tensor", b.sV, vv2)
            ctx.emit("tensor", lambda E: E.matmul(ps[:, 1:2], b.ones_f[0:1, :],
                                                  acc[0:1, 17:18], start=True, stop=True))
            vt2 = ctx.inc("tensor", b.sT, 1)
            ctx.wait("vector", b.sT, vt2)
            ctx.emit("vector", lambda E: E.tensor_scalar(b.ha[:, :], ps[:, 1:2], 0.5, None,
                                                         op0=OP.mult))
            ctx.inc("vector", b.sV, 1)
        ld2 = ctx.sem("q_ld2"); fr2 = ctx.sem("q_fr2"); stq = ctx.sem("q_st")
        def load2(i):
            if i >= 2:
                ctx.wait("sync", fr2, i - 1)
            ctx.dma("sync", slab[i % 2][:, :], dram["ffnw"][:, i * 2048:(i + 1) * 2048], ld2)
        load2(0)
        load2(1)
        pv_vv = []
        for i in range(16):
            ctx.wait("scalar", ld2, 16 * (i + 1))
            if i >= 1:
                ctx.wait("scalar", b.sV, pv_vv[i - 1])
            ctx.emit("scalar", lambda E, i=i: E.activation(sgn[:, :], slab[i % 2][:, :], AF.Sign))
            ctx.emit("scalar", lambda E, i=i: E.activation(absw[:, :], slab[i % 2][:, :],
                                                           AF.Abs))
            ss = ctx.inc("scalar", b.sS, 1)
            ctx.wait("vector", b.sS, ss)
            ctx.emit("vector", lambda E: E.tensor_scalar(
                ind[:, :], absw[:, :], b.ha[:, 0:1], None, op0=OP.is_ge))
            ctx.emit("vector", lambda E: E.tensor_tensor(wqb[:, :], sgn[:, :], ind[:, :],
                                                         op=OP.mult))
            ctx.inc("vector", fr2, 1)
            vv3 = ctx.inc("vector", b.sV, 1)
            pv_vv.append(vv3)
            ctx.wait("sync", b.sV, vv3)
            ctx.dma("sync", dram["hwqt"][:, i * 2048:(i + 1) * 2048], wqb[:, :], stq)
            if i + 2 < 16:
                load2(i + 2)
            ctx.wait("vector", stq, stq.v)
        ctx.wait("sync", stq, 256)
        if "hha" in dram:
            ctx.wait("sync", b.sV, b.sV.v)
            ctx.dma("sync", dram["hha"], b.ha[:, :], stq)


def emit_p1(ctx, nc, b, dram):
    import os as _os
    p1s = int(_os.environ.get("MK_P1S", "9"))
    hsz, hxc, hy0 = dram["hsz"], dram["hxc"], dram["hy0"]
    hcumd, hy2 = dram["hcumd"], dram["hy2"]
    with ExitStack() as st_:
        sbl = lambda n, sh, dt: st_.enter_context(nc.sbuf_tensor(n, sh, dt))
        ps = lambda n, sh, dt: st_.enter_context(nc.psum_tensor(n, sh, dt))
        convw_s = sbl("m_convw", [128, 128], F32)
        convb_s = sbl("m_convb", [128, 32], F32)
        dtb_s = sbl("m_dtb", [128, 32], F32)
        dvec_s = sbl("m_dvec", [128, 32], F32)
        a_s = sbl("m_a", [128, 512], F32)
        etot = sbl("m_etot", [128, 512], F32)
        hlast = sbl("m_hlast", [128, 512], F32)
        sdelta = sbl("m_sdelta", [128, 32], F32)
        bcrows_bf = sbl("m_bcbf", [32, 1024], BF16)
        hin = sbl("m_hin", [128, 512], F32)
        ldp = ctx.sem("m_ldp")
        ldpg = ctx.sem("m_ldpg")
        ctx.dma("sync", convw_s[:, :], dram["convw"], ldp)
        ctx.dma("sync", convb_s[:, :], dram["convb"], ldp)
        ctx.dma("sync", dtb_s[:, :], dram["dtb"], ldp)
        ctx.dma("sync", dvec_s[:, :], dram["dvec"], ldp)
        ctx.dma("sync", a_s[:, :], dram["alog"], ldp)
        ldp_all = ldp.v
        ctx.wait("scalar", ldp, 80)
        ctx.emit("scalar", lambda E: E.activation(a_s[:, :], a_s[:, :], AF.Exp))
        sA = ctx.inc("scalar", b.sS, 1)
        ctx.wait("vector", b.sS, sA)
        ctx.emit("vector", lambda E: E.tensor_scalar(a_s[:, :], a_s[:, :], -1.0, None,
                                                     op0=OP.mult))
        ctx.inc("vector", b.sV, 1)

        with ExitStack() as sc:
            sb2 = lambda n, sh, dt: sc.enter_context(nc.sbuf_tensor(n, sh, dt))
            ps1 = lambda n, sh, dt: sc.enter_context(nc.psum_tensor(n, sh, dt))
            xt = [sb2(f"m_xt{i}", [128, 2048], F32) for i in range(2)]
            xbf = sb2("m_xbf", [128, 2048], BF16)
            nsc_s = sb2("m_nsc", [128, 2048], BF16)
            nscrow = sb2("m_nscrow", [1, 2048], BF16)
            ldn = ctx.sem("m_ldn")
            ctx.dma("gpsimd", nscrow[:, :], dram["nsc"], ldn)
            xnmT = sb2("m_xnmT", [128, 16 * 1152], BF16)
            xin = sb2("m_xin", [128, 32 * 1027], BF16)
            szt = sb2("m_szt", [128, 1024], BF16)
            trp = ps1("m_trp", [128, 512], BF16)
            pse = [ps1(f"m_pse{i}", [128, 512], F32) for i in range(4)]
            psh = ps1("m_psh", [128, 128], F32)
            ctx.wait("tensor", ldn, 16)
            for j in range(4):
                ctx.emit("tensor", lambda E, j=j: E.matmul(
                    pse[0][:, :], b.ones_b[0:1, :], nscrow[0:1, j * 512:(j + 1) * 512],
                    start=True, stop=True))
                vt = ctx.inc("tensor", b.sT, 1)
                ctx.wait("scalar", b.sT, vt)
                ctx.emit("scalar", lambda E, j=j: E.activation(
                    nsc_s[:, j * 512:(j + 1) * 512], pse[0][:, :], AF.Identity))
                ss = ctx.inc("scalar", b.sS, 1)
                ctx.wait("tensor", b.sS, ss)
            ldx = ctx.sem("m_ldx")
            tdone = []
            for tt in range(9):
                i = tt % 2
                src = dram["xprev"] if tt == 0 else dram["x_sh"][(tt - 1) * 128:tt * 128, :]
                if tt >= 2:
                    ctx.wait("sync", b.sV, tdone[tt - 2])
                ctx.dma("sync", xt[i][:, :], src, ldx)
                ctx.wait("scalar", ldx, 16 * (tt + 1))
                if tt >= 1:
                    ctx.wait("scalar", b.sV, tdone[tt - 1])
                    ctx.wait("scalar", b.stx, b.stx.v)
                sEnd = emit_ln(ctx, b, xt[i][:, :], b.scr[:, 0:2048], b.scr, b.stat)
                if tt >= 1:
                    ctx.wait("sync", b.sS, sEnd)
                    ctx.dma("sync", dram["hxn0"][(tt - 1) * 128:tt * 128, :],
                            b.scr[:, 0:2048], b.stx)
                ctx.emit("scalar", lambda E, i=i: E.activation(
                    xt[i][:, :], b.scr[:, 0:2048], AF.Square))
                s2 = ctx.inc("scalar", b.sS, 1)
                ctx.wait("vector", b.sS, s2)
                ctx.emit("vector", lambda E, i=i: E.tensor_scalar(
                    xt[i][:, :], xt[i][:, :], 1.0, None, op0=OP.mult, op1=OP.add,
                    accum_out=b.stat[:, 2:3]))
                ctx.emit("vector", lambda E: E.tensor_copy(b.stat[:, 4:8], b.stat[:, 0:4]))
                ctx.emit("vector", lambda E: E.tensor_scalar(
                    b.stat[:, 2:3], b.stat[:, 2:3], 1.0 / 2048.0, 1e-5,
                    op0=OP.mult, op1=OP.add))
                v2 = ctx.inc("vector", b.sV, 1)
                ctx.wait("scalar", b.sV, v2)
                ctx.emit("scalar", lambda E: E.activation(b.stat[:, 2:3], b.stat[:, 2:3], AF.Ln))
                ctx.emit("scalar", lambda E: E.activation(b.stat[:, 2:3], b.stat[:, 2:3],
                                                          AF.Exp, scale=-0.5))
                s3 = ctx.inc("scalar", b.sS, 1)
                ctx.wait("vector", b.sS, s3)
                ctx.emit("vector", lambda E: E.tensor_copy(b.stat[:, 4:8], b.stat[:, 0:4]))
                ctx.emit("vector", lambda E, i=i: E.tensor_scalar(
                    xt[i][:, :], b.scr[:, 0:2048], b.stat[:, 2:3], None, op0=OP.mult))
                ctx.emit("vector", lambda E, i=i: E.tensor_tensor(
                    xbf[:, :], xt[i][:, :], nsc_s[:, :], op=OP.mult))
                v3 = ctx.inc("vector", b.sV, 1)
                tdone.append(v3)
                ctx.wait("tensor", b.sV, v3)
                emit_tr_strip(ctx, b, xbf, 16,
                              (lambda tt: (lambda dm: dm * 1152 + tt * 128))(tt), xnmT, trp)
                ctx.wait("vector", b.sS, b.sS.v)
            if int(_os.environ.get("MK_P1SUB", "9")) < 2:
                ctx.barrier("m_sub1")
                return
            ws_in = WStream(ctx, nc, sc, "m_win", 2048, 3)
            for eb in range(64):
                ws_in.load(dram["inpw"][eb])
            g_evac = {}
            g_idx = 0
            for eb in range(64):
                wbuf = ws_in.use("tensor")
                is_x = eb < 32
                if is_x:
                    for dm in range(16):
                        ctx.emit("tensor", lambda E, dm=dm, w=wbuf: E.matmul(
                            psh[:, :], w[:, dm * 128:(dm + 1) * 128],
                            xnmT[:, dm * 1152:dm * 1152 + 128],
                            start=(dm == 0), stop=(dm == 15)))
                    vt = ctx.inc("tensor", b.sT, 1)
                    ctx.wait("scalar", b.sT, vt)
                    ctx.emit("scalar", lambda E, eb=eb: E.activation(
                        xin[:, eb * 1027:eb * 1027 + 3], psh[:, 125:128], AF.Identity))
                    hv = ctx.inc("scalar", b.sS, 1)
                    ctx.wait("tensor", b.sS, hv)
                for half in range(2):
                    pp = pse[g_idx % 4]
                    if g_idx >= 4:
                        ctx.wait("tensor", b.sS, g_evac[g_idx - 4])
                    for dm in range(16):
                        ctx.emit("tensor", lambda E, dm=dm, half=half, w=wbuf, pp=pp: E.matmul(
                            pp[:, :], w[:, dm * 128:(dm + 1) * 128],
                            xnmT[:, dm * 1152 + 128 + half * 512:
                                 dm * 1152 + 128 + (half + 1) * 512],
                            start=(dm == 0), stop=(dm == 15)))
                    vt = ctx.inc("tensor", b.sT, 1)
                    ctx.wait("scalar", b.sT, vt)
                    if is_x:
                        ctx.emit("scalar", lambda E, eb=eb, half=half, pp=pp: E.activation(
                            xin[:, eb * 1027 + 3 + half * 512:
                                eb * 1027 + 3 + (half + 1) * 512], pp[:, :], AF.Identity))
                    else:
                        ctx.emit("scalar", lambda E, half=half, pp=pp: E.activation(
                            szt[:, half * 512:(half + 1) * 512], pp[:, :], AF.Silu))
                    g_evac[g_idx] = ctx.inc("scalar", b.sS, 1)
                    g_idx += 1
                ws_in.advance(1, b.sT, b.sT.v)
                if not is_x:
                    ctx.wait("sync", b.sS, b.sS.v)
                    ctx.dma("sync", hsz[:, (eb - 32) * 1024:(eb - 31) * 1024], szt[:, :], b.stx)
                    ctx.wait("scalar", b.stx, b.stx.v)
            xct = sb2("m_xct", [128, 1024], BF16)
            cacc = sb2("m_cacc", [128, 1024], F32)
            if "hxin" in dram:
                ctx.wait("sync", b.sS, b.sS.v)
                ctx.dma("sync", dram["hxin"], xin[:, :], b.stx)
            ctx.wait("vector", b.sS, b.sS.v)
            ctx.wait("vector", ldp, 32)
            for db in range(32):
                base = db * 1027
                ctx.emit("vector", lambda E, base=base, db=db: E.tensor_scalar(
                    cacc[:, :], xin[:, base:base + 1024], convw_s[:, db * 4:db * 4 + 1],
                    None, op0=OP.mult))
                for k in range(1, 4):
                    ctx.emit("vector", lambda E, base=base, db=db, k=k: E.scalar_tensor_tensor(
                        cacc[:, :], xin[:, base + k:base + k + 1024],
                        convw_s[:, db * 4 + k:db * 4 + k + 1], cacc[:, :],
                        op0=OP.mult, op1=OP.add))
                vv = ctx.inc("vector", b.sV, 1)
                ctx.wait("scalar", b.sV, vv)
                ctx.emit("scalar", lambda E, db=db: E.activation(
                    xct[:, :], cacc[:, :], AF.Silu, bias=convb_s[:, db:db + 1]))
                ss = ctx.inc("scalar", b.sS, 1)
                ctx.wait("sync", b.sS, ss)
                ctx.dma("sync", hxc[:, db * 1024:(db + 1) * 1024], xct[:, :], b.stx)
                ctx.wait("scalar", b.stx, b.stx.v)
                ctx.wait("vector", b.sS, ss)
        ctx.barrier("m_ab")
        if p1s < 2:
            return

        with ExitStack() as sc:
            sb2 = lambda n, sh, dt: sc.enter_context(nc.sbuf_tensor(n, sh, dt))
            cb_t = sb2("m_cb", [128, 16 * 1024], BF16)
            tmb = sb2("m_tmb", [128, 1024], BF16)
            xcr = [sb2(f"m_xcr{i}", [128, 1024], BF16) for i in range(2)]
            with ExitStack() as sd:
                sb3 = lambda n, sh, dt: sd.enter_context(nc.sbuf_tensor(n, sh, dt))
                delta = sb3("m_delta", [128, 32 * 1024], BF16)
                bb = sb3("m_bb", [128, 16 * 1024], BF16)
                du = sb3("m_du", [128, 1024], BF16)
                wt = sb3("m_wt", [128, 1024], BF16)
                ht = sb3("m_ht", [128, 1024], BF16)
                y0 = [sb3(f"m_y0{i}", [128, 1024], BF16) for i in range(2)]
                sdd = sd.enter_context(ExitStack())
                sb4 = lambda n, sh, dt: sdd.enter_context(nc.sbuf_tensor(n, sh, dt))
                xpw_s = sb4("m_xpw", [128, 32 * 160], BF16)
                dtw_s = sb4("m_dtw", [128, 32 * 128], BF16)
                ctx.dma("gpsimd", xpw_s[:, :], dram["xpw"], ldpg)
                ctx.dma("gpsimd", dtw_s[:, :], dram["dtw"], ldpg)
                ldp_all = ldpg.v
                dtlow = sb4("m_dtlow", [128, 1024], BF16)
                sp1 = sb4("m_sp1", [128, 1024], BF16)
                sp2 = sb4("m_sp2", [128, 1024], BF16)
                cumd_t = [sb4(f"m_cumd{i}", [128, 1024], BF16) for i in range(2)]
                onesf_t = sb4("m_ones_t", [128, 1024], BF16)
                ctx.emit("gpsimd", lambda E: E.memset(onesf_t[:, :], 1.0))
                gv = ctx.inc("gpsimd", b.sG, 1)
                ps2 = lambda n, sh, dt: sdd.enter_context(nc.psum_tensor(n, sh, dt))
                psd = [ps2(f"m_psd{i}", [128, 512], F32) for i in range(2)]
                psbc = [ps2(f"m_psbc{i}", [32, 512], F32) for i in range(2)]
                psq = [ps2(f"m_psq{i}", [128, 512], F32) for i in range(2)]
                ldc = ctx.sem("m_ldc")
                db_done = []
                for db in range(32):
                    if db >= 2:
                        ctx.wait("sync", b.sT, db_done[db - 2])
                    ctx.dma("sync", xcr[db % 2][:, :], hxc[:, db * 1024:(db + 1) * 1024], ldc)
                    ctx.wait("tensor", ldc, 16 * (db + 1))
                    ctx.wait("tensor", ldpg, ldp_all)
                    for half in range(2):
                        ctx.emit("tensor", lambda E, db=db, half=half: E.matmul(
                            psd[half][:, :], xpw_s[:, db * 160:db * 160 + 128],
                            xcr[db % 2][:, half * 512:(half + 1) * 512],
                            start=(db == 0), stop=(db == 31)))
                        ctx.emit("tensor", lambda E, db=db, half=half: E.matmul(
                            psbc[half][:, :], xpw_s[:, db * 160 + 128:db * 160 + 160],
                            xcr[db % 2][:, half * 512:(half + 1) * 512],
                            start=(db == 0), stop=(db == 31)))
                    db_done.append(ctx.inc("tensor", b.sT, 1))
                ctx.wait("scalar", b.sT, b.sT.v)
                ctx.emit("scalar", lambda E: E.activation(dtlow[:, 0:512], psd[0][:, :],
                                                          AF.Identity))
                ctx.emit("scalar", lambda E: E.activation(dtlow[:, 512:1024], psd[1][:, :],
                                                          AF.Identity))
                ctx.emit("scalar", lambda E: E.activation(bcrows_bf[:, 0:512], psbc[0][:, :],
                                                          AF.Identity))
                ctx.emit("scalar", lambda E: E.activation(bcrows_bf[:, 512:1024], psbc[1][:, :],
                                                          AF.Identity))
                ssx = ctx.inc("scalar", b.sS, 1)
                ctx.wait("tensor", b.sS, ssx)
                dvv = []
                for db in range(32):
                    for half in range(2):
                        ctx.emit("tensor", lambda E, db=db, half=half: E.matmul(
                            psq[half][:, :], dtw_s[:, db * 128:(db + 1) * 128],
                            dtlow[:, half * 512:(half + 1) * 512], start=True, stop=True))
                    vt = ctx.inc("tensor", b.sT, 1)
                    ctx.wait("scalar", b.sT, vt)
                    if db >= 1:
                        ctx.wait("scalar", b.sV, dvv[db - 1])
                    for half in range(2):
                        ctx.emit("scalar", lambda E, db=db, half=half: E.activation(
                            sp1[:, half * 512:(half + 1) * 512], psq[half][:, :], AF.Abs,
                            bias=dtb_s[:, db:db + 1]))
                        ctx.emit("scalar", lambda E, db=db, half=half: E.activation(
                            sp2[:, half * 512:(half + 1) * 512], psq[half][:, :], AF.Relu,
                            bias=dtb_s[:, db:db + 1]))
                    ctx.emit("scalar", lambda E: E.activation(sp1[:, :], sp1[:, :], AF.Exp,
                                                              scale=-1.0))
                    ctx.emit("scalar", lambda E: E.activation(sp1[:, :], sp1[:, :], AF.Ln,
                                                              bias=1.0))
                    ss = ctx.inc("scalar", b.sS, 1)
                    ctx.wait("tensor", b.sS, ss)
                    ctx.wait("vector", b.sS, ss)
                    ctx.emit("vector", lambda E, db=db: E.scalar_tensor_tensor(
                        delta[:, db * 1024:(db + 1) * 1024], sp1[:, :], 1.0, sp2[:, :],
                        op0=OP.mult, op1=OP.add, accum_out=sdelta[:, db:db + 1]))
                    ctx.wait("vector", b.sG, gv)
                    if db >= 2:
                        ctx.wait("vector", b.stx, b.stx.v)
                    ctx.emit("vector", lambda E, db=db: E.tensor_tensor_scan(
                        cumd_t[db % 2][:, :], onesf_t[:, :],
                        delta[:, db * 1024:(db + 1) * 1024], 0.0, op0=OP.mult, op1=OP.add))
                    vv = ctx.inc("vector", b.sV, 1)
                    dvv.append(vv)
                    ctx.wait("sync", b.sV, vv)
                    ctx.dma("sync", hcumd[:, db * 1024:(db + 1) * 1024],
                            cumd_t[db % 2][:, :], b.stx)
                    ctx.wait("scalar", b.sV, vv)
                    for n in range(16):
                        ctx.emit("scalar", lambda E, db=db, n=n: E.activation(
                            etot[:, db * 16 + n:db * 16 + n + 1], sdelta[:, db:db + 1],
                            AF.Exp, scale=a_s[:, db * 16 + n:db * 16 + n + 1]))
                    ctx.inc("scalar", b.sS, 1)
                ctx.barrier("m_dt")
                sdd.close()
                if p1s < 3:
                    return

                psw = [sd.enter_context(nc.psum_tensor(f"m_psw{i}", [128, 512], F32))
                       for i in range(2)]
                rsel_s = sb3("m_rsel", [32, 32 * 128], BF16)
                dA = [sb3(f"m_dA{i}", [128, 1024], BF16) for i in range(8)]
                ldrs = ctx.sem("m_ldrs")
                ctx.dma("gpsimd", rsel_s[:, :], dram["rsel"], ldrs)
                ctx.wait("tensor", ldrs, 16)
                for n in range(32):
                    dst = bb if n < 16 else cb_t
                    nn = n % 16
                    for half in range(2):
                        ctx.emit("tensor", lambda E, n=n, half=half: E.matmul(
                            psw[half][:, :], rsel_s[:, n * 128:(n + 1) * 128],
                            bcrows_bf[:, half * 512:(half + 1) * 512],
                            start=True, stop=True))
                    vt = ctx.inc("tensor", b.sT, 1)
                    ctx.wait("scalar", b.sT, vt)
                    for half in range(2):
                        ctx.emit("scalar", lambda E, nn=nn, dst=dst, half=half: E.activation(
                            dst[:, nn * 1024 + half * 512:nn * 1024 + (half + 1) * 512],
                            psw[half][:, :], AF.Identity))
                    ss = ctx.inc("scalar", b.sS, 1)
                    ctx.wait("tensor", b.sS, ss)
                ldc2 = ctx.sem("m_ldc2")
                sty = ctx.sem("m_sty")
                sdone = {}
                ctx.wait("vector", b.sS, b.sS.v)
                ctx.wait("vector", ldp, 64)
                for db in range(32):
                    if db >= 2:
                        ctx.wait("sync", b.sV, sdone[(db - 2, 3)])
                    ctx.dma("sync", xcr[db % 2][:, :], hxc[:, db * 1024:(db + 1) * 1024], ldc2)
                    ctx.wait("vector", ldc2, 16 * (db + 1))
                    ctx.emit("vector", lambda E, db=db: E.tensor_tensor(
                        du[:, :], delta[:, db * 1024:(db + 1) * 1024], xcr[db % 2][:, :],
                        op=OP.mult))
                    if db >= 2:
                        ctx.wait("vector", sty, 16 * (db - 1))
                    ctx.emit("vector", lambda E, db=db: E.tensor_scalar(
                        y0[db % 2][:, :], xcr[db % 2][:, :], dvec_s[:, db:db + 1],
                        None, op0=OP.mult))
                    for nb in range(4):
                        if db * 4 + nb >= 2:
                            pb = (db, nb - 2) if nb >= 2 else (db - 1, nb + 2)
                            ctx.wait("scalar", b.sV, sdone[pb])
                        for j in range(4):
                            n = nb * 4 + j
                            ctx.emit("scalar", lambda E, db=db, n=n, nb=nb, j=j: E.activation(
                                dA[(nb % 2) * 4 + j][:, :],
                                delta[:, db * 1024:(db + 1) * 1024], AF.Exp,
                                scale=a_s[:, db * 16 + n:db * 16 + n + 1]))
                        ss = ctx.inc("scalar", b.sS, 1)
                        ctx.wait("vector", b.sS, ss)
                        for j in range(4):
                            n = nb * 4 + j
                            ctx.emit("vector", lambda E, n=n: E.tensor_tensor(
                                wt[:, :], du[:, :], bb[:, n * 1024:(n + 1) * 1024], op=OP.mult))
                            ctx.emit("vector", lambda E, nb=nb, j=j: E.tensor_tensor_scan(
                                ht[:, :], dA[(nb % 2) * 4 + j][:, :], wt[:, :], 0.0,
                                op0=OP.mult, op1=OP.add))
                            ctx.emit("vector", lambda E, db=db, n=n: E.tensor_copy(
                                hlast[:, db * 16 + n:db * 16 + n + 1], ht[:, 1023:1024]))
                            ctx.emit("vector", lambda E, n=n: E.tensor_tensor(
                                tmb[:, :], ht[:, :], cb_t[:, n * 1024:(n + 1) * 1024],
                                op=OP.mult))
                            ctx.emit("vector", lambda E, db=db: E.tensor_tensor(
                                y0[db % 2][:, :], y0[db % 2][:, :], tmb[:, :], op=OP.add))
                        sdone[(db, nb)] = ctx.inc("vector", b.sV, 1)
                    ctx.wait("sync", b.sV, sdone[(db, 3)])
                    ctx.dma("sync", hy0[:, db * 1024:(db + 1) * 1024], y0[db % 2][:, :], sty)
                ctx.barrier("m_scan")
                if p1s < 4:
                    return

            civ = dram["cc_in"].rearrange("o (p f) -> (o p) f", p=128)
            ctx.dma("sync", civ[:, 0:512], etot[:, :], b.stx)
            ctx.dma("sync", civ[:, 512:1024], hlast[:, :], b.stx)
            ctx.wait("gpsimd", b.stx, b.stx.v)
            ccs = ctx.sem("m_ccs")
            ctx.emit("gpsimd", lambda E: E.collective_compute(
                "AllGather", OP.bypass, replica_groups=[list(range(NCORES))],
                ins=[dram["cc_in"].opt()], outs=[dram["cc_out"].opt()]).then_inc(ccs.h, 1),
                fusable=False)
            ccs.v += 1
            ctx.wait("sync", ccs, 1)
            er = [sb2(f"m_er{i}", [128, 512], F32) for i in range(8)]
            hr = [sb2(f"m_hr{i}", [128, 512], F32) for i in range(8)]
            cur = [sb2(f"m_cur{i}", [128, 512], F32) for i in range(2)]
            selm = sb2("m_selm", [128, 8], F32)
            ldcc = ctx.sem("m_ldcc")
            ctx.dma("sync", selm[:, :], dram["selmask"], ldcc)
            for r in range(NCORES):
                view = dram["cc_out"][r:r + 1, :].rearrange("o (p f) -> (o p) f", p=128)
                ctx.dma("sync", er[r][:, :], view[:, 0:512], ldcc)
                ctx.dma("sync", hr[r][:, :], view[:, 512:1024], ldcc)
            ctx.emit("gpsimd", lambda E: E.memset(hin[:, :], 0.0))
            ctx.emit("gpsimd", lambda E: E.memset(cur[0][:, :], 0.0))
            gv2 = ctx.inc("gpsimd", b.sG, 1)
            ctx.wait("vector", b.sG, gv2)
            ctx.wait("vector", ldcc, ldcc.v)
            for r in range(NCORES):
                a, o = cur[r % 2], cur[(r + 1) % 2]
                ctx.emit("vector", lambda E, r=r, a=a: E.scalar_tensor_tensor(
                    hin[:, :], a[:, :], selm[:, r:r + 1], hin[:, :], op0=OP.mult, op1=OP.add))
                ctx.emit("vector", lambda E, r=r, a=a, o=o: E.tensor_tensor(
                    o[:, :], a[:, :], er[r][:, :], op=OP.mult))
                ctx.emit("vector", lambda E, r=r, o=o: E.tensor_tensor(
                    o[:, :], o[:, :], hr[r][:, :], op=OP.add))
            ctx.inc("vector", b.sV, 1)
            ctx.barrier("m_carry")
            if p1s < 5:
                return

            szr = [sb2(f"m_szr{i}", [128, 1024], BF16) for i in range(2)]
            cdr = [sb2(f"m_cdr{i}", [128, 1024], BF16) for i in range(2)]
            y0r = [sb2(f"m_y0r{i}", [128, 1024], BF16) for i in range(2)]
            El = [sb2(f"m_El{i}", [128, 1024], BF16) for i in range(8)]
            y2t = [sb2(f"m_y2t{i}", [128, 1024], BF16) for i in range(2)]
            ldg = ctx.sem("m_ldg")
            sty2 = ctx.sem("m_sty2")
            gdone = {}
            for db in range(32):
                i = db % 2
                if db >= 2:
                    ctx.wait("sync", b.sV, gdone[db - 2])
                ctx.dma("sync", y0r[i][:, :], hy0[:, db * 1024:(db + 1) * 1024], ldg)
                ctx.dma("sync", cdr[i][:, :], hcumd[:, db * 1024:(db + 1) * 1024], ldg)
                ctx.dma("sync", szr[i][:, :], hsz[:, db * 1024:(db + 1) * 1024], ldg)
                ctx.wait("vector", ldg, 48 * (db + 1))
                for nb in range(4):
                    if db * 4 + nb >= 2:
                        pb = (db, nb - 2) if nb >= 2 else (db - 1, nb + 2)
                        ctx.wait("scalar", b.sV, gdone[pb])
                    ctx.wait("scalar", ldg, 48 * (db + 1))
                    for j in range(4):
                        n = nb * 4 + j
                        ctx.emit("scalar", lambda E, db=db, n=n, nb=nb, j=j, i=i: E.activation(
                            El[(nb % 2) * 4 + j][:, :], cdr[i][:, :], AF.Exp,
                            scale=a_s[:, db * 16 + n:db * 16 + n + 1]))
                    ss = ctx.inc("scalar", b.sS, 1)
                    ctx.wait("vector", b.sS, ss)
                    for j in range(4):
                        n = nb * 4 + j
                        sl = (nb % 2) * 4 + j
                        ctx.emit("vector", lambda E, db=db, n=n, sl=sl: E.tensor_scalar(
                            El[sl][:, :], El[sl][:, :],
                            hin[:, db * 16 + n:db * 16 + n + 1], None, op0=OP.mult))
                        ctx.emit("vector", lambda E, n=n, sl=sl: E.tensor_tensor(
                            tmb[:, :], El[sl][:, :], cb_t[:, n * 1024:(n + 1) * 1024],
                            op=OP.mult))
                        ctx.emit("vector", lambda E, db=db, i=i: E.tensor_tensor(
                            y0r[i][:, :], y0r[i][:, :], tmb[:, :], op=OP.add))
                    gdone[(db, nb)] = ctx.inc("vector", b.sV, 1)
                if db >= 2:
                    ctx.wait("vector", sty2, 16 * (db - 1))
                ctx.emit("vector", lambda E, db=db, i=i: E.tensor_tensor(
                    y2t[i][:, :], y0r[i][:, :], szr[i][:, :], op=OP.mult))
                gdone[db] = ctx.inc("vector", b.sV, 1)
                ctx.wait("sync", b.sV, gdone[db])
                ctx.dma("sync", hy2[:, db * 1024:(db + 1) * 1024], y2t[i][:, :], sty2)
        ctx.barrier("m_y2")
        if p1s < 6:
            return

        with ExitStack() as sc:
            sb2 = lambda n, sh, dt: sc.enter_context(nc.sbuf_tensor(n, sh, dt))
            ps1 = lambda n, sh, dt: sc.enter_context(nc.psum_tensor(n, sh, dt))
            y2r = [sb2(f"m_y2r{i}", [128, 1024], BF16) for i in range(2)]
            mo = sb2("m_mo", [128, 16 * 1024], BF16)
            owb = [sb2(f"m_owb{i}", [128, 4096], BF16) for i in range(4)]
            sop = sc.enter_context(ExitStack())
            pso = [sop.enter_context(nc.psum_tensor(f"m_pso{i}", [128, 512], F32))
                   for i in range(8)]
            ldy = ctx.sem("m_ldy")
            ldo = ctx.sem("m_ldo")
            ogdone = ctx.sem("m_ogdone")
            nload = 0
            tdb = []
            for og in range(4):
                for j in range(4):
                    if og >= 1:
                        ctx.wait("gpsimd", ogdone, og)
                    ctx.dma("gpsimd", owb[j][:, :], dram["outw"][og * 4 + j], ldo)
                ctx.wait("tensor", ldo, 16 * 4 * (og + 1))
                for db in range(32):
                    if nload >= 2:
                        ctx.wait("sync", b.sT, tdb[nload - 2])
                    ctx.dma("sync", y2r[db % 2][:, :], dram["hy2"][:, db * 1024:(db + 1) * 1024], ldy)
                    nload += 1
                    ctx.wait("tensor", ldy, 16 * nload)
                    for ob in range(4):
                        for half in range(2):
                            ctx.emit("tensor", lambda E, ob=ob, half=half, db=db, w=owb[ob]:
                                     E.matmul(pso[ob * 2 + half][:, :],
                                              w[:, db * 128:(db + 1) * 128],
                                              y2r[db % 2][:, half * 512:(half + 1) * 512],
                                              start=(db == 0), stop=(db == 31)))
                    tdb.append(ctx.inc("tensor", b.sT, 1))
                vt = b.sT.v
                ctx.wait("scalar", b.sT, vt)
                for ob in range(4):
                    obg = og * 4 + ob
                    for half in range(2):
                        ctx.emit("scalar", lambda E, obg=obg, ob=ob, half=half: E.activation(
                            mo[:, obg * 1024 + half * 512:obg * 1024 + (half + 1) * 512],
                            pso[ob * 2 + half][:, :], AF.Identity))
                ssg = ctx.inc("scalar", b.sS, 1)
                ctx.wait("tensor", b.sS, ssg)
                ctx.inc("scalar", ogdone, 1)
            ctx.barrier("m_op")
            sop.close()

            xn0r = [sb2(f"m_xn0r{i}", [128, 2048], F32) for i in range(2)]
            xr = [sb2(f"m_xr{i}", [128, 2048], F32) for i in range(2)]
            pstr = ps1("m_pstr", [128, 512], BF16)
            ldh = ctx.sem("m_ldh")
            hdone = []
            for tt in range(8):
                i = tt % 2
                if tt >= 2:
                    ctx.wait("sync", b.sV, hdone[tt - 2])
                ctx.dma("sync", xn0r[i][:, :], dram["hxn0"][tt * 128:(tt + 1) * 128, :], ldh)
                ctx.dma("sync", xr[i][:, :], dram["x_sh"][tt * 128:(tt + 1) * 128, :], ldh)
                for ob in range(16):
                    q = ob % 4
                    ctx.emit("tensor", lambda E, ob=ob, tt=tt, q=q: E.matmul(
                        pstr[:, q * 128:(q + 1) * 128],
                        mo[:, ob * 1024 + tt * 128:ob * 1024 + (tt + 1) * 128],
                        b.eye_b[:, :], is_transpose=True, start=True, stop=True))
                    if q == 3:
                        vt = ctx.inc("tensor", b.sT, 1)
                        ctx.wait("vector", b.sT, vt)
                        ctx.wait("vector", ldh, 32 * (tt + 1))
                        for qq in range(4):
                            obq = ob - 3 + qq
                            ctx.emit("vector", lambda E, obq=obq, qq=qq, i=i: E.tensor_tensor(
                                xn0r[i][:, obq * 128:(obq + 1) * 128],
                                xn0r[i][:, obq * 128:(obq + 1) * 128],
                                pstr[:, qq * 128:(qq + 1) * 128], op=OP.add))
                        vv = ctx.inc("vector", b.sV, 1)
                        ctx.wait("tensor", b.sV, vv)
                ctx.wait("scalar", b.sV, b.sV.v)
                if tt >= 1:
                    ctx.wait("scalar", b.stx, b.stx.v)
                sEnd = emit_ln(ctx, b, xn0r[i][:, :], b.scr[:, 0:2048], b.scr, b.stat)
                ctx.wait("vector", b.sS, sEnd)
                ctx.emit("vector", lambda E, i=i: E.tensor_tensor(
                    xr[i][:, :], xr[i][:, :], b.scr[:, 0:2048], op=OP.add))
                vv = ctx.inc("vector", b.sV, 1)
                hdone.append(vv)
                ctx.wait("sync", b.sV, vv)
                ctx.dma("sync", dram["hx1"][tt * 128:(tt + 1) * 128, :], xr[i][:, :], b.stx)


def emit_p2(ctx, nc, b, dram):
    with ExitStack() as st_:
        sbl = lambda n, sh, dt: st_.enter_context(nc.sbuf_tensor(n, sh, dt))
        psl = lambda n, sh, dt: st_.enter_context(nc.psum_tensor(n, sh, dt))
        wqt = sbl("f_wqt", [128, 16 * 2048], BF16)
        x1t = [sbl(f"f_x1t{i}", [128, 2048], F32) for i in range(2)]
        xn2 = sbl("f_xn2", [128, 2048], F32)
        vq = sbl("f_vq", [128, 2048], F32)
        vkeep = sbl("f_vkeep", [128, 2048], F32)
        lad0 = sbl("f_lad0", [128, 2048], F32)
        indt = sbl("f_ind", [128, 2048], F32)
        sgn = sbl("f_sgn", [128, 2048], F32)
        xqb = sbl("f_xqb", [128, 2048], BF16)
        xqT = sbl("f_xqT", [128, 16 * 128], BF16)
        ot = sbl("f_ot", [128, 2048], F32)
        bst = sbl("f_bst", [128, 4], F32)
        ptr = psl("f_ptr", [128, 512], BF16)
        pmm = [psl(f"f_pmm{i}", [128, 512], F32) for i in range(4)]
        ldw = ctx.sem("f_ldw")
        ctx.dma("sync", wqt[:, :], dram["hwqt"], ldw)
        ldx = ctx.sem("f_ldx")
        fdone = []
        for tt in range(8):
            i = tt % 2
            if tt >= 2:
                ctx.wait("sync", b.sV, fdone[tt - 2])
            ctx.dma("sync", x1t[i][:, :], dram["hx1"][tt * 128:(tt + 1) * 128, :], ldx)
            ctx.wait("scalar", ldx, 16 * (tt + 1))
            if tt >= 1:
                ctx.wait("scalar", b.sV, fdone[tt - 1])
                ctx.wait("scalar", b.stx, b.stx.v)
            sEnd = emit_ln(ctx, b, x1t[i][:, :], xn2[:, :], b.scr, b.stat)
            ctx.emit("scalar", lambda E: E.activation(sgn[:, :], xn2[:, :], AF.Sign))
            sSg = ctx.inc("scalar", b.sS, 1)
            ctx.wait("vector", b.sS, sSg)
            ctx.emit("vector", lambda E: E.scalar_tensor_tensor(
                vq[:, :], xn2[:, :], -1.0, xn2[:, :], op0=OP.mult, op1=OP.max))
            ctx.emit("vector", lambda E: E.tensor_scalar(
                lad0[:, :], vq[:, :], 1.0, None, op0=OP.mult, op1=OP.max,
                accum_out=bst[:, 0:1]))
            ctx.emit("vector", lambda E: E.tensor_copy(vkeep[:, 0:16], vq[:, 0:16]))
            ctx.emit("vector", lambda E: E.tensor_scalar(
                bst[:, 1:2], bst[:, 0:1], 1.0 / 127.0, 1e-8, op0=OP.mult, op1=OP.max))
            ctx.emit("vector", lambda E: E.tensor_copy(vkeep[:, 32:48], vq[:, 32:48]))
            ctx.emit("vector", lambda E: E.tensor_copy(vkeep[:, 48:64], vq[:, 48:64]))
            ctx.emit("vector", lambda E: E.reciprocal(bst[:, 2:3], bst[:, 1:2]))
            ctx.emit("vector", lambda E: E.tensor_scalar(
                bst[:, 3:4], bst[:, 1:2], b.ha[:, 0:1], 2.0, op0=OP.mult, op1=OP.mult))
            ctx.emit("vector", lambda E: E.tensor_copy(lad0[:, 0:16], vq[:, 0:16]))
            ctx.emit("vector", lambda E: E.tensor_copy(lad0[:, 16:32], vq[:, 16:32]))
            ctx.emit("vector", lambda E: E.tensor_scalar(
                vq[:, :], vq[:, :], bst[:, 2:3], None, op0=OP.mult))
            ctx.emit("vector", lambda E: E.tensor_copy(vkeep[:, :], vq[:, :]))
            if "hbst" in dram and tt == 0:
                vdmp = ctx.inc("vector", b.sV, 1)
                ctx.wait("sync", b.sV, vdmp)
                ctx.dma("sync", dram["hbst"][:, 0:4], bst[:, 0:4], b.stx)
                ctx.dma("sync", dram["hvq"], vq[:, :], b.stx)
            cur, other = vq, lad0
            for k in (64.0, 32.0, 16.0, 8.0, 4.0, 2.0, 1.0):
                ctx.emit("vector", lambda E, cur=cur, k=k: E.tensor_scalar(
                    indt[:, :], cur[:, :], k, None, op0=OP.is_ge))
                ctx.emit("vector", lambda E, cur=cur, other=other, k=k: E.scalar_tensor_tensor(
                    other[:, :], indt[:, :], -k, cur[:, :], op0=OP.mult, op1=OP.add))
                cur, other = other, cur
            if "hbst" in dram and tt == 0:
                vdmp2 = ctx.inc("vector", b.sV, 1)
                ctx.wait("sync", b.sV, vdmp2)
                ctx.dma("sync", dram["hcur"], cur[:, :], b.stx)
                ctx.wait("vector", b.stx, b.stx.v)
            ctx.emit("vector", lambda E, cur=cur: E.tensor_tensor(
                vkeep[:, :], vkeep[:, :], cur[:, :], op=OP.subtract))
            ctx.emit("vector", lambda E, cur=cur: E.tensor_scalar(
                indt[:, :], cur[:, :], 0.5, None, op0=OP.is_ge))
            ctx.emit("vector", lambda E: E.tensor_tensor(
                vkeep[:, :], vkeep[:, :], indt[:, :], op=OP.add))
            ctx.emit("vector", lambda E: E.tensor_scalar(
                vkeep[:, :], vkeep[:, :], 127.0, None, op0=OP.min))
            ctx.emit("vector", lambda E: E.tensor_tensor(
                xqb[:, :], vkeep[:, :], sgn[:, :], op=OP.mult))
            vq_done = ctx.inc("vector", b.sV, 1)
            if "hxq" in dram:
                ctx.wait("sync", b.sV, vq_done)
                ctx.dma("sync", dram["hxq"][tt * 128:(tt + 1) * 128, :], xqb[:, :], b.stx)
            ctx.wait("tensor", b.sV, vq_done)
            emit_tr_strip(ctx, b, xqb, 16, lambda j: j * 128, xqT, ptr)
            ctx.wait("tensor", ldw, 16)
            for ib in range(16):
                for oc in range(4):
                    ctx.emit("tensor", lambda E, ib=ib, oc=oc: E.matmul(
                        pmm[oc][:, :], xqT[:, ib * 128:(ib + 1) * 128],
                        wqt[:, ib * 2048 + oc * 512:ib * 2048 + (oc + 1) * 512],
                        start=(ib == 0), stop=(ib == 15)))
            vt = ctx.inc("tensor", b.sT, 1)
            ctx.wait("scalar", b.sT, vt)
            for oc in range(4):
                ctx.emit("scalar", lambda E, oc=oc: E.activation(
                    ot[:, oc * 512:(oc + 1) * 512], pmm[oc][:, :], AF.Identity,
                    scale=bst[:, 3:4]))
            sEv = ctx.inc("scalar", b.sS, 1)
            if "hffn" in dram:
                ctx.wait("sync", b.sS, sEv)
                ctx.dma("sync", dram["hffn"][tt * 128:(tt + 1) * 128, :], ot[:, :], b.stx)
                ctx.wait("scalar", b.stx, b.stx.v)
            ctx.wait("tensor", b.sS, sEv)
            sEnd2 = emit_ln(ctx, b, ot[:, :], b.scr[:, 0:2048], b.scr, b.stat)
            ctx.wait("vector", b.sS, sEnd2)
            ctx.emit("vector", lambda E, i=i: E.tensor_tensor(
                x1t[i][:, :], x1t[i][:, :], b.scr[:, 0:2048], op=OP.add))
            vv = ctx.inc("vector", b.sV, 1)
            fdone.append(vv)
            ctx.wait("sync", b.sV, vv)
            ctx.dma("sync", dram["hx2"][tt * 128:(tt + 1) * 128, :], x1t[i][:, :], b.stx)


def emit_p3(ctx, nc, b, dram):
    with ExitStack() as st_:
        sbl = lambda n, sh, dt: st_.enter_context(nc.sbuf_tensor(n, sh, dt))
        psl = lambda n, sh, dt: st_.enter_context(nc.psum_tensor(n, sh, dt))
        yhb = sbl("a_yhb", [128, 16 * 1024], BF16)
        bt_s = sbl("a_bt", [128, 64], F32)
        bq_s = sbl("a_bq", [128, 16], F32)
        bk_s = sbl("a_bk", [128, 16], F32)
        bvrow = sbl("a_bvrow", [1, 2048], BF16)
        bvb = sbl("a_bvb", [128, 16 * 128], BF16)
        borow = sbl("a_borow", [1, 2048], BF16)
        bob = sbl("a_bob", [128, 2048], BF16)
        ldp = ctx.sem("a_ldp")
        ldpg = ctx.sem("a_ldpg")
        ctx.dma("gpsimd", bvrow[:, :], dram["bv"], ldpg)
        ctx.dma("gpsimd", borow[:, :], dram["bo"], ldpg)
        ctx.dma("sync", bt_s[:, :], dram["bias_tab"], ldp)
        ctx.dma("sync", bq_s[:, :], dram["bq"], ldp)
        ctx.dma("sync", bk_s[:, :], dram["bk"], ldp)
        ldp_all = ldp.v
        ctx.wait("scalar", ldp, ldp_all)
        with nc.psum_tensor("a_psb", [128, 512], F32) as psb:
            ctx.wait("tensor", ldp, ldp_all)
            ctx.wait("tensor", ldpg, 32)
            for h in range(16):
                ctx.emit("tensor", lambda E, h=h: E.matmul(
                    psb[:, 0:128], b.ones_b[0:1, :], bvrow[0:1, h * 128:(h + 1) * 128],
                    start=True, stop=True))
                vt = ctx.inc("tensor", b.sT, 1)
                ctx.wait("scalar", b.sT, vt)
                ctx.emit("scalar", lambda E, h=h: E.activation(
                    bvb[:, h * 128:(h + 1) * 128], psb[:, 0:128], AF.Identity))
                ss = ctx.inc("scalar", b.sS, 1)
                ctx.wait("tensor", b.sS, ss)
            for j in range(4):
                ctx.emit("tensor", lambda E, j=j: E.matmul(
                    psb[:, :], b.ones_b[0:1, :], borow[0:1, j * 512:(j + 1) * 512],
                    start=True, stop=True))
                vt = ctx.inc("tensor", b.sT, 1)
                ctx.wait("scalar", b.sT, vt)
                ctx.emit("scalar", lambda E, j=j: E.activation(
                    bob[:, j * 512:(j + 1) * 512], psb[:, :], AF.Identity))
                ss = ctx.inc("scalar", b.sS, 1)
                ctx.wait("tensor", b.sS, ss)

        skv = st_.enter_context(ExitStack())
        qb = skv.enter_context(nc.sbuf_tensor("a_qb", [128, 16 * 1024], BF16))
        kb = skv.enter_context(nc.sbuf_tensor("a_kb", [128, 16 * 1024], BF16))
        vb = skv.enter_context(nc.sbuf_tensor("a_vb", [128, 16 * 1024], BF16))
        with ExitStack() as sc:
            sb2 = lambda n, sh, dt: sc.enter_context(nc.sbuf_tensor(n, sh, dt))
            ps1 = lambda n, sh, dt: sc.enter_context(nc.psum_tensor(n, sh, dt))
            xn3T = sb2("a_xn3T", [128, 16 * 1024], BF16)
            x2t = [sb2("a_x2t0", [128, 2048], F32)]
            xn3 = sb2("a_xn3", [128, 2048], BF16)
            trp = ps1("a_trp", [128, 512], BF16)
            ldx = ctx.sem("a_ldx")
            adone = []
            for tt in range(8):
                i = 0
                if tt >= 1:
                    ctx.wait("sync", b.sS, adone[tt - 1])
                ctx.dma("sync", x2t[i][:, :], dram["hx2"][tt * 128:(tt + 1) * 128, :], ldx)
                ctx.wait("scalar", ldx, 16 * (tt + 1))
                sEnd = emit_ln(ctx, b, x2t[i][:, :], b.scr[:, 0:2048], b.scr, b.stat)
                ctx.emit("scalar", lambda E: E.activation(xn3[:, :], b.scr[:, 0:2048],
                                                          AF.Identity))
                s2 = ctx.inc("scalar", b.sS, 1)
                ctx.wait("tensor", b.sS, s2)
                s3 = emit_tr_strip(ctx, b, xn3, 16,
                                   (lambda tt: (lambda dm: dm * 1024 + tt * 128))(tt),
                                   xn3T, trp)
                adone.append(s3)
            ws_qkv = WStream(ctx, nc, sc, "a_wqkv", 2048, 2)
            for h in range(16):
                ws_qkv.load(dram["wq"][h])
            for h in range(16):
                ws_qkv.load(dram["wk"][h])
            for h in range(16):
                ws_qkv.load(dram["wv"][h])
            psq = [ps1(f"a_psq{i}", [128, 512], F32) for i in range(4)]
            qg_evac = {}
            qg = 0
            for which, (dstb, bias_s) in enumerate(((qb, bq_s), (kb, bk_s))):
                for h in range(16):
                    wb = ws_qkv.use("tensor")
                    for half in range(2):
                        pp = psq[qg % 4]
                        if qg >= 4:
                            ctx.wait("tensor", b.sS, qg_evac[qg - 4])
                        for dm in range(16):
                            ctx.emit("tensor", lambda E, h=h, half=half, dm=dm, w=wb, pp=pp:
                                     E.matmul(
                                         pp[:, :], w[:, dm * 128:(dm + 1) * 128],
                                         xn3T[:, dm * 1024 + half * 512:
                                              dm * 1024 + (half + 1) * 512],
                                         start=(dm == 0), stop=(dm == 15)))
                        vt = ctx.inc("tensor", b.sT, 1)
                        ctx.wait("scalar", b.sT, vt)
                        ctx.emit("scalar", lambda E, h=h, half=half, dstb=dstb,
                                 bias_s=bias_s, pp=pp: E.activation(
                                     dstb[:, h * 1024 + half * 512:h * 1024 + (half + 1) * 512],
                                     pp[:, :], AF.Identity, bias=bias_s[:, h:h + 1]))
                        qg_evac[qg] = ctx.inc("scalar", b.sS, 1)
                        qg += 1
                    ws_qkv.advance(1, b.sT, b.sT.v)
            ctx.wait("tensor", b.sS, b.sS.v)
            psv = [ps1(f"a_psv{i}", [128, 128], F32) for i in range(2)]
            vg_evac = {}
            vgi = 0
            for h in range(16):
                wb = ws_qkv.use("tensor")
                for tt in range(8):
                    pv = psv[vgi % 2]
                    if vgi >= 2:
                        ctx.wait("tensor", b.sV, vg_evac[vgi - 2])
                    for dm in range(16):
                        ctx.emit("tensor", lambda E, h=h, tt=tt, dm=dm, w=wb, pv=pv: E.matmul(
                            pv[:, :], xn3T[:, dm * 1024 + tt * 128:dm * 1024 + (tt + 1) * 128],
                            w[:, dm * 128:(dm + 1) * 128],
                            start=(dm == 0), stop=(dm == 15)))
                    vt = ctx.inc("tensor", b.sT, 1)
                    ctx.wait("vector", b.sT, vt)
                    ctx.emit("vector", lambda E, h=h, tt=tt, pv=pv: E.tensor_tensor(
                        vb[:, h * 1024 + tt * 128:h * 1024 + (tt + 1) * 128],
                        pv[:, :], bvb[:, h * 128:(h + 1) * 128], op=OP.add))
                    vg_evac[vgi] = ctx.inc("vector", b.sV, 1)
                    vgi += 1
                ws_qkv.advance(1, b.sT, b.sT.v)
            ctx.wait("tensor", b.sV, b.sV.v)
        ctx.barrier("a_qkv")

        kvi = dram["kv_in"].rearrange("o (p f) -> (o p) f", p=128)
        ctx.dma("sync", kvi[:, 0:16384], kb[:, :], b.stx)
        ctx.dma("sync", kvi[:, 16384:32768], vb[:, :], b.stx)
        ctx.wait("gpsimd", b.stx, b.stx.v)
        ccs = ctx.sem("a_ccs")
        ctx.emit("gpsimd", lambda E: E.collective_compute(
            "AllGather", OP.bypass, replica_groups=[list(range(NCORES))],
            ins=[dram["kv_in"].opt()], outs=[dram["kv_out"].opt()]).then_inc(ccs.h, 1),
            fusable=False)
        ccs.v += 1
        ctx.wait("sync", ccs, 1)

        with ExitStack() as sc:
            sb2 = lambda n, sh, dt: sc.enter_context(nc.sbuf_tensor(n, sh, dt))
            ps1 = lambda n, sh, dt: sc.enter_context(nc.psum_tensor(n, sh, dt))
            lm_s = sb2("a_lm", [128, 8 * 1024], BF16)
            ldlm = ctx.sem("a_ldlm")
            ctx.dma("gpsimd", lm_s[:, :], dram["lmask"], ldlm)
            ctx.wait("vector", ldlm, 16)
            kg = [sb2(f"a_kg{i}", [128, 1024], BF16) for i in range(2)]
            vg = [sb2(f"a_vg{i}", [128, 1024], BF16) for i in range(2)]
            expst = [sb2(f"a_ex{i}", [128, 1024], BF16) for i in range(4)]
            dnacc = sb2("a_dn", [128, 1024], F32)
            dnb = sb2("a_dnb", [128, 1024], BF16)
            rd = sb2("a_rd", [1, 1024], F32)
            rd2 = sb2("a_rd2", [1, 1024], F32)
            rdb_row = sb2("a_rdbrow", [1, 1024], BF16)
            rdb = sb2("a_rdb", [128, 1024], BF16)
            psc = [ps1(f"a_psc{i}", [128, 1024], F32) for i in range(2)]
            pyh = ps1("a_pyh", [128, 1024], F32)
            pdn = ps1("a_pdn", [1, 1024], F32)
            ldkv = ctx.sem("a_ldkv")
            nkv = 0
            kv_gate = []
            exp_done = {}
            pv_done = {}
            va_done = {}

            def load_rank(g):
                nonlocal nkv
                h, r = g // 8, g % 8
                view = dram["kv_out"][r:r + 1, :].rearrange("o (p f) -> (o p) f", p=128)
                if g >= 2:
                    ctx.wait("sync", b.sT, kv_gate[g - 2])
                ctx.dma("sync", kg[g % 2][:, :], view[:, h * 1024:(h + 1) * 1024], ldkv)
                ctx.dma("sync", vg[g % 2][:, :],
                        view[:, 16384 + h * 1024:16384 + (h + 1) * 1024], ldkv)
                nkv += 1

            load_rank(0)
            load_rank(1)

            def tile_info(h, idx):
                if idx < 8:
                    ksl = kb[:, h * 1024 + idx * 128:h * 1024 + (idx + 1) * 128]
                    vsl = vb[:, h * 1024 + idx * 128:h * 1024 + (idx + 1) * 128]
                    return "L", idx, ksl, vsl, 0.0, None
                g = h * 8 + (idx - 8) // 8
                j2 = (idx - 8) % 8
                ksl = kg[g % 2][:, j2 * 128:(j2 + 1) * 128]
                vsl = vg[g % 2][:, j2 * 128:(j2 + 1) * 128]
                jg = g % 8 * 8 + j2
                return "G", j2, ksl, vsl, bt_s[:, jg:jg + 1], g

            mv_done = {}

            def emit_pv(h, idx):
                kind, j2, ksl, vsl, bias, g = tile_info(h, idx)
                first, last = idx == 0, idx == 71
                if kind == "L":
                    ctx.wait("tensor", b.sV, mv_done[(h, idx)])
                else:
                    ctx.wait("tensor", b.sS, exp_done[(h, idx)])
                tb = idx % 4
                for half in range(2):
                    ctx.emit("tensor", lambda E, tb=tb, half=half, vsl=vsl,
                             first=first, last=last: E.matmul(
                                 pyh[:, half * 512:(half + 1) * 512], vsl,
                                 expst[tb][:, half * 512:(half + 1) * 512],
                                 start=first, stop=last))
                pv_done[(h, idx)] = ctx.inc("tensor", b.sT, 1)
                if kind == "G" and j2 == 7:
                    kv_gate.append(pv_done[(h, idx)])
                    if nkv < 128:
                        load_rank(nkv)

            for h in range(16):
                for idx in range(72):
                    kind, j2, ksl, vsl, bias, g = tile_info(h, idx)
                    tb, pb = idx % 4, idx % 2
                    if kind == "G" and j2 == 0:
                        ctx.wait("tensor", ldkv, 32 * (g + 1))
                    if idx >= 2:
                        ctx.wait("tensor", b.sS, exp_done[(h, idx - 2)])
                    for half in range(2):
                        ctx.emit("tensor", lambda E, pb=pb, half=half, ksl=ksl, h=h: E.matmul(
                            psc[pb][:, half * 512:(half + 1) * 512], ksl,
                            qb[:, h * 1024 + half * 512:h * 1024 + (half + 1) * 512],
                            start=True, stop=True))
                    sct = ctx.inc("tensor", b.sT, 1)
                    ctx.wait("scalar", b.sT, sct)
                    pu = (h, idx - 4) if idx >= 4 else ((h - 1, 68 + idx) if h > 0 else None)
                    if pu is not None:
                        ctx.wait("scalar", b.sT, pv_done[pu])
                        ctx.wait("scalar", b.sV, va_done[pu])
                    ctx.emit("scalar", lambda E, tb=tb, pb=pb, bias=bias: E.activation(
                        expst[tb][:, :], psc[pb][:, :], AF.Exp, scale=ISC, bias=bias))
                    exp_done[(h, idx)] = ctx.inc("scalar", b.sS, 1)
                    if kind == "L":
                        ctx.wait("vector", b.sS, exp_done[(h, idx)])
                        ctx.emit("vector", lambda E, tb=tb, j2=j2,
                                 w=(j2 + 1) * 128: E.tensor_tensor(
                            expst[tb][:, 0:w], expst[tb][:, 0:w],
                            lm_s[:, j2 * 1024:j2 * 1024 + w], op=OP.mult))
                        mv_done[(h, idx)] = ctx.inc("vector", b.sV, 1)
                    else:
                        ctx.wait("vector", b.sS, exp_done[(h, idx)])
                    if idx == 0:
                        ctx.emit("vector", lambda E, tb=tb: E.tensor_copy(
                            dnacc[:, :], expst[tb][:, :]))
                    else:
                        ctx.emit("vector", lambda E, tb=tb: E.tensor_tensor(
                            dnacc[:, :], dnacc[:, :], expst[tb][:, :], op=OP.add))
                    va_done[(h, idx)] = ctx.inc("vector", b.sV, 1)
                    if idx >= 1:
                        emit_pv(h, idx - 1)
                emit_pv(h, 71)
                ctx.emit("vector", lambda E: E.tensor_copy(dnb[:, :], dnacc[:, :]))
                dnv = ctx.inc("vector", b.sV, 1)
                ctx.wait("tensor", b.sV, dnv)
                for half in range(2):
                    ctx.emit("tensor", lambda E, half=half: E.matmul(
                        pdn[0:1, half * 512:(half + 1) * 512], b.onescol_b[:, :],
                        dnb[:, half * 512:(half + 1) * 512], start=True, stop=True))
                vtd = ctx.inc("tensor", b.sT, 1)
                ctx.wait("vector", b.sT, vtd)
                vt = pv_done[(h, 71)]
                ctx.wait("vector", b.sT, vt)
                ctx.emit("vector", lambda E: E.tensor_copy(rd[:, :], pdn[0:1, :]))
                ctx.emit("vector", lambda E: E.tensor_copy(rdb_row[:, 0:16], rd[:, 0:16]))
                ctx.emit("vector", lambda E: E.tensor_copy(rdb_row[:, 16:32], rd[:, 16:32]))
                ctx.emit("vector", lambda E: E.reciprocal(rd2[:, :], rd[:, :]))
                ctx.emit("vector", lambda E: E.tensor_copy(rd[:, 0:16], rd[:, 0:16]))
                ctx.emit("vector", lambda E: E.tensor_copy(rd[:, 16:32], rd[:, 16:32]))
                ctx.emit("vector", lambda E: E.tensor_copy(rdb_row[:, :], rd2[:, :]))
                bcv = ctx.inc("vector", b.sV, 1)
                ctx.wait("tensor", b.sV, bcv)
                for half in range(2):
                    ctx.emit("tensor", lambda E, half=half: E.matmul(
                        psc[0][:, half * 512:(half + 1) * 512], b.ones_b[0:1, :],
                        rdb_row[0:1, half * 512:(half + 1) * 512], start=True, stop=True))
                vt2 = ctx.inc("tensor", b.sT, 1)
                ctx.wait("scalar", b.sT, vt2)
                ctx.emit("scalar", lambda E: E.activation(rdb[:, :], psc[0][:, :], AF.Identity))
                ssn = ctx.inc("scalar", b.sS, 1)
                ctx.wait("vector", b.sS, ssn)
                for half in range(2):
                    ctx.emit("vector", lambda E, h=h, half=half: E.tensor_tensor(
                        yhb[:, h * 1024 + half * 512:h * 1024 + (half + 1) * 512],
                        pyh[:, half * 512:(half + 1) * 512],
                        rdb[:, half * 512:(half + 1) * 512], op=OP.mult))
                yv = ctx.inc("vector", b.sV, 1)
                ctx.wait("tensor", b.sV, yv)
        ctx.barrier("a_attn")
        skv.close()

        with ExitStack() as sc:
            sb2 = lambda n, sh, dt: sc.enter_context(nc.sbuf_tensor(n, sh, dt))
            ps1 = lambda n, sh, dt: sc.enter_context(nc.psum_tensor(n, sh, dt))
            ao = sb2("a_ao", [128, 16 * 1024], BF16)
            ws_wo = WStream(ctx, nc, sc, "a_wwo", 2048, 3)
            for ob in range(16):
                ws_wo.load(dram["wo"][ob])
            pwo = [ps1(f"a_pwo{i}", [128, 512], F32) for i in range(2)]
            for ob in range(16):
                wb = ws_wo.use("tensor")
                for half in range(2):
                    for hh in range(16):
                        ctx.emit("tensor", lambda E, ob=ob, half=half, hh=hh, w=wb: E.matmul(
                            pwo[half][:, :], w[:, hh * 128:(hh + 1) * 128],
                            yhb[:, hh * 1024 + half * 512:hh * 1024 + (half + 1) * 512],
                            start=(hh == 0), stop=(hh == 15)))
                vt = ctx.inc("tensor", b.sT, 1)
                ctx.wait("scalar", b.sT, vt)
                for half in range(2):
                    ctx.emit("scalar", lambda E, ob=ob, half=half: E.activation(
                        ao[:, ob * 1024 + half * 512:ob * 1024 + (half + 1) * 512],
                        pwo[half][:, :], AF.Identity))
                ss = ctx.inc("scalar", b.sS, 1)
                ctx.wait("tensor", b.sS, ss)
                ws_wo.advance(1, b.sT, vt)
            x2r = [sb2(f"a_x2r{i}", [128, 2048], F32) for i in range(2)]
            att = sb2("a_att", [128, 2048], F32)
            prev_ln = [0]
            pstr = ps1("a_pstr", [128, 512], BF16)
            ldh = ctx.sem("a_ldh")
            hdone = []
            for tt in range(8):
                i = tt % 2
                if tt >= 2:
                    ctx.wait("sync", b.sV, hdone[tt - 2])
                ctx.dma("sync", x2r[i][:, :], dram["hx2"][tt * 128:(tt + 1) * 128, :], ldh)
                ctx.wait("vector", b.sS, prev_ln[0])
                for ob in range(16):
                    q = ob % 4
                    ctx.emit("tensor", lambda E, ob=ob, tt=tt, q=q: E.matmul(
                        pstr[:, q * 128:(q + 1) * 128],
                        ao[:, ob * 1024 + tt * 128:ob * 1024 + (tt + 1) * 128],
                        b.eye_b[:, :], is_transpose=True, start=True, stop=True))
                    if q == 3:
                        vt = ctx.inc("tensor", b.sT, 1)
                        ctx.wait("vector", b.sT, vt)
                        for qq in range(4):
                            obq = ob - 3 + qq
                            ctx.emit("vector", lambda E, obq=obq, qq=qq: E.scalar_tensor_tensor(
                                att[:, obq * 128:(obq + 1) * 128],
                                pstr[:, qq * 128:(qq + 1) * 128], 1.0,
                                bob[:, obq * 128:(obq + 1) * 128], op0=OP.mult, op1=OP.add))
                        vv = ctx.inc("vector", b.sV, 1)
                        ctx.wait("tensor", b.sV, vv)
                ctx.wait("scalar", b.sV, b.sV.v)
                if tt >= 1:
                    ctx.wait("scalar", b.stx, b.stx.v)
                sEnd = emit_ln(ctx, b, att[:, :], b.scr[:, 0:2048], b.scr, b.stat)
                prev_ln[0] = sEnd
                ctx.wait("vector", b.sS, sEnd)
                ctx.wait("vector", ldh, 16 * (tt + 1))
                ctx.emit("vector", lambda E, i=i: E.tensor_tensor(
                    x2r[i][:, :], x2r[i][:, :], b.scr[:, 0:2048], op=OP.add))
                vv = ctx.inc("vector", b.sV, 1)
                hdone.append(vv)
                ctx.wait("sync", b.sV, vv)
                ctx.dma("sync", dram["hx3"][tt * 128:(tt + 1) * 128, :], x2r[i][:, :], b.stx)


def emit_p4(ctx, nc, b, dram):
    with ExitStack() as st_:
        sbl = lambda n, sh, dt: st_.enter_context(nc.sbuf_tensor(n, sh, dt))
        psl = lambda n, sh, dt: st_.enter_context(nc.psum_tensor(n, sh, dt))
        xn4T = sbl("e_xn4T", [128, 16 * 1024], BF16)
        gb = sbl("e_gb", [128, 8 * 1024], BF16)
        macc = sbl("e_macc", [128, 16 * 1024], F32)
        gstr = sbl("e_gstr", [128, 22 * 1024], BF16)
        gatew_s = sbl("e_gatew", [128, 128], BF16)
        gates = sbl("e_gates", [128, 64], F32)
        gstat = sbl("e_gstat", [128, 48], F32)
        gatesT = sbl("e_gatesT", [8, 1024], F32)
        gatesTb = sbl("e_gatesTb", [8, 1024], BF16)
        rbb = sbl("e_rbb", [128, 8], F32)
        rbrow = sbl("e_rbrow", [1, 8], F32)
        ldp = ctx.sem("e_ldp")
        ldpg = ctx.sem("e_ldpg")
        ctx.dma("gpsimd", gatew_s[:, :], dram["gatew"], ldpg)
        ctx.dma("sync", rbrow[:, :], dram["rbias"], ldp)
        ldp_all = ldp.v

        with ExitStack() as sc:
            sb2 = lambda n, sh, dt: sc.enter_context(nc.sbuf_tensor(n, sh, dt))
            ps1 = lambda n, sh, dt: sc.enter_context(nc.psum_tensor(n, sh, dt))
            x3t = [sb2(f"e_x3t{i}", [128, 2048], F32) for i in range(2)]
            xn4 = sb2("e_xn4", [128, 2048], BF16)
            trp = ps1("e_trp", [128, 512], BF16)
            psr = ps1("e_psr", [128, 8], F32)
            psg = ps1("e_psg", [8, 128], F32)
            psbc2 = ps1("e_psbc2", [128, 512], F32)
            ldx = ctx.sem("e_ldx")
            edone = []
            ctx.wait("tensor", ldp, ldp_all)
            ctx.wait("tensor", ldpg, 16)
            ctx.emit("tensor", lambda E: E.matmul(psbc2[:, 0:8], b.ones_f[0:1, :],
                                                  rbrow[0:1, :], start=True, stop=True))
            vt = ctx.inc("tensor", b.sT, 1)
            ctx.wait("scalar", b.sT, vt)
            ctx.emit("scalar", lambda E: E.activation(rbb[:, :], psbc2[:, 0:8], AF.Identity))
            ssr = ctx.inc("scalar", b.sS, 1)
            ctx.wait("tensor", b.sS, ssr)
            for tt in range(8):
                i = tt % 2
                if tt >= 2:
                    ctx.wait("sync", b.sS, edone[tt - 2])
                ctx.dma("sync", x3t[i][:, :], dram["hx3"][tt * 128:(tt + 1) * 128, :], ldx)
                ctx.wait("scalar", ldx, 16 * (tt + 1))
                if tt >= 1:
                    ctx.wait("scalar", b.stx, b.stx.v)
                sEnd = emit_ln(ctx, b, x3t[i][:, :], b.scr[:, 0:2048], b.scr, b.stat)
                ctx.emit("scalar", lambda E: E.activation(xn4[:, :], b.scr[:, 0:2048],
                                                          AF.Identity))
                s2 = ctx.inc("scalar", b.sS, 1)
                ctx.wait("tensor", b.sS, s2)
                emit_tr_strip(ctx, b, xn4, 16,
                              (lambda tt: (lambda dm: dm * 1024 + tt * 128))(tt),
                              xn4T, trp)
                for dm in range(16):
                    ctx.emit("tensor", lambda E, tt=tt, dm=dm: E.matmul(
                        psr[:, :], xn4T[:, dm * 1024 + tt * 128:dm * 1024 + (tt + 1) * 128],
                        gatew_s[:, dm * 8:(dm + 1) * 8], start=(dm == 0), stop=(dm == 15)))
                vt = ctx.inc("tensor", b.sT, 1)
                ctx.wait("scalar", b.sT, vt)
                ctx.emit("scalar", lambda E, tt=tt: E.activation(
                    gates[:, tt * 8:(tt + 1) * 8], psr[:, :], AF.Tanh, scale=0.5))
                ss = ctx.inc("scalar", b.sS, 1)
                edone.append(ss)
                ctx.wait("tensor", b.sS, ss)
            sg_all = ctx.inc("scalar", b.sS, 0) if False else b.sS.v
            ctx.wait("vector", b.sS, sg_all)
            G = gstat
            for tt in range(8):
                ctx.emit("vector", lambda E, tt=tt: E.tensor_scalar(
                    gates[:, tt * 8:(tt + 1) * 8], gates[:, tt * 8:(tt + 1) * 8],
                    0.5, 0.5, op0=OP.mult, op1=OP.add))
            for tt in range(8):
                ctx.emit("vector", lambda E, tt=tt: E.tensor_tensor(
                    b.scr[:, tt * 8:tt * 8 + 8], gates[:, tt * 8:(tt + 1) * 8],
                    rbb[:, :], op=OP.add))
            for tt in range(8):
                ctx.emit("vector", lambda E, tt=tt: E.tensor_reduce(
                    G[:, tt:tt + 1], b.scr[:, tt * 8:tt * 8 + 8], AX.X, OP.max))
            for tt in range(8):
                ctx.emit("vector", lambda E, tt=tt: E.tensor_scalar(
                    b.scr[:, 64 + tt * 8:64 + tt * 8 + 8], b.scr[:, tt * 8:tt * 8 + 8],
                    G[:, tt:tt + 1], None, op0=OP.is_ge))
            for tt in range(8):
                ctx.emit("vector", lambda E, tt=tt: E.scalar_tensor_tensor(
                    b.scr[:, 64 + tt * 8:64 + tt * 8 + 8],
                    b.scr[:, 64 + tt * 8:64 + tt * 8 + 8], NEG,
                    b.scr[:, tt * 8:tt * 8 + 8], op0=OP.mult, op1=OP.add))
            for tt in range(8):
                ctx.emit("vector", lambda E, tt=tt: E.tensor_reduce(
                    G[:, 8 + tt:9 + tt], b.scr[:, 64 + tt * 8:64 + tt * 8 + 8], AX.X, OP.max))
            for tt in range(8):
                ctx.emit("vector", lambda E, tt=tt: E.tensor_scalar(
                    b.scr[:, tt * 8:tt * 8 + 8], b.scr[:, tt * 8:tt * 8 + 8],
                    G[:, 8 + tt:9 + tt], None, op0=OP.is_ge))
            for tt in range(8):
                ctx.emit("vector", lambda E, tt=tt: E.tensor_tensor(
                    gates[:, tt * 8:(tt + 1) * 8], gates[:, tt * 8:(tt + 1) * 8],
                    b.scr[:, tt * 8:tt * 8 + 8], op=OP.mult))
            for tt in range(8):
                ctx.emit("vector", lambda E, tt=tt: E.tensor_reduce(
                    G[:, 16 + tt:17 + tt], gates[:, tt * 8:(tt + 1) * 8], AX.X, OP.add))
            for tt in range(8):
                ctx.emit("vector", lambda E, tt=tt: E.tensor_scalar(
                    G[:, 24 + tt:25 + tt], G[:, 16 + tt:17 + tt], 1e-12, None, op0=OP.max))
            for tt in range(8):
                ctx.emit("vector", lambda E, tt=tt: E.reciprocal(
                    G[:, 32 + tt:33 + tt], G[:, 24 + tt:25 + tt]))
            for tt in range(8):
                ctx.emit("vector", lambda E, tt=tt: E.tensor_scalar(
                    gates[:, tt * 8:(tt + 1) * 8], gates[:, tt * 8:(tt + 1) * 8],
                    G[:, 32 + tt:33 + tt], None, op0=OP.mult))
            gv = ctx.inc("vector", b.sV, 1)
            ctx.wait("tensor", b.sV, gv)
            for tt in range(8):
                ctx.emit("tensor", lambda E, tt=tt: E.matmul(
                    psg[:, :], gates[:, tt * 8:(tt + 1) * 8], b.eye_f[:, :],
                    is_transpose=True, start=True, stop=True))
                vt = ctx.inc("tensor", b.sT, 1)
                ctx.wait("scalar", b.sT, vt)
                ctx.emit("scalar", lambda E, tt=tt: E.activation(
                    gatesT[:, tt * 128:(tt + 1) * 128], psg[:, :], AF.Identity))
                ss = ctx.inc("scalar", b.sS, 1)
                ctx.wait("tensor", b.sS, ss)
            ctx.wait("vector", b.sS, b.sS.v)
            ctx.emit("vector", lambda E: E.tensor_copy(gatesTb[:, :], gatesT[:, :]))
            gtv = ctx.inc("vector", b.sV, 1)
            ctx.wait("tensor", b.sV, gtv)
            esel_s = sb2("e_esel", [8, 8 * 128], BF16)
            ldes = ctx.sem("e_ldes")
            ctx.dma("gpsimd", esel_s[:, :], dram["esel"], ldes)
            ctx.wait("tensor", ldes, 16)
            for e in range(8):
                for half in range(2):
                    ctx.emit("tensor", lambda E, e=e, half=half: E.matmul(
                        psbc2[:, :], esel_s[:, e * 128:(e + 1) * 128],
                        gatesTb[:, half * 512:(half + 1) * 512], start=True, stop=True))
                    vt = ctx.inc("tensor", b.sT, 1)
                    ctx.wait("scalar", b.sT, vt)
                    ctx.emit("scalar", lambda E, e=e, half=half: E.activation(
                        gb[:, e * 1024 + half * 512:e * 1024 + (half + 1) * 512],
                        psbc2[:, :], AF.Identity))
                    ss = ctx.inc("scalar", b.sS, 1)
                    ctx.wait("tensor", b.sS, ss)

        ctx.barrier("e_router")
        with ExitStack() as sc:
            sb2 = lambda n, sh, dt: sc.enter_context(nc.sbuf_tensor(n, sh, dt))
            ps1 = lambda n, sh, dt: sc.enter_context(nc.psum_tensor(n, sh, dt))
            s1b = [sb2(f"e_s1b{i}", [128, 1024], BF16) for i in range(2)]
            p4 = [ps1(f"e_p4_{i}", [128, 512], F32) for i in range(8)]
            ws = WStream(ctx, nc, sc, "e_ws", 2816, 4)
            for i2 in range(22):
                ws.load(dram["sw1"][i2])
                ws.load(dram["sw3"][i2])
            for ob in range(16):
                ws.load(dram["sw2"][ob])
            for e in range(8):
                for ib in range(11):
                    ws.load(dram["w1"][e, ib])
                    ws.load(dram["w3"][e, ib])
                for ob in range(16):
                    ws.load(dram["w2"][e, ob])

            def ffn_block(nblk, gate_e):
                vdone = {}
                for ib in range(nblk):
                    pr = ib % 2
                    ph1 = [p4[pr * 4 + 0], p4[pr * 4 + 1]]
                    ph3 = [p4[pr * 4 + 2], p4[pr * 4 + 3]]
                    w1b = ws.peek("tensor", 0)
                    w3b = ws.peek("tensor", 1)
                    if ib >= 2:
                        ctx.wait("tensor", b.sV, vdone[ib - 2])
                    for half in range(2):
                        for dm in range(16):
                            ctx.emit("tensor", lambda E, half=half, dm=dm, w=w1b, p=ph1:
                                     E.matmul(p[half][:, :], w[:, dm * 128:(dm + 1) * 128],
                                              xn4T[:, dm * 1024 + half * 512:
                                                   dm * 1024 + (half + 1) * 512],
                                              start=(dm == 0), stop=(dm == 15)))
                        for dm in range(16):
                            ctx.emit("tensor", lambda E, half=half, dm=dm, w=w3b, p=ph3:
                                     E.matmul(p[half][:, :], w[:, dm * 128:(dm + 1) * 128],
                                              xn4T[:, dm * 1024 + half * 512:
                                                   dm * 1024 + (half + 1) * 512],
                                              start=(dm == 0), stop=(dm == 15)))
                    vt = ctx.inc("tensor", b.sT, 1)
                    ws.advance(2, b.sT, vt)
                    ctx.wait("scalar", b.sT, vt)
                    if ib >= 2:
                        ctx.wait("scalar", b.sV, vdone[ib - 2])
                    for half in range(2):
                        ctx.emit("scalar", lambda E, half=half, pr=pr, p=ph1: E.activation(
                            s1b[pr][:, half * 512:(half + 1) * 512], p[half][:, :], AF.Silu))
                    ss = ctx.inc("scalar", b.sS, 1)
                    ctx.wait("vector", b.sS, ss)
                    for half in range(2):
                        ctx.emit("vector", lambda E, ib=ib, half=half, pr=pr, p=ph3:
                                 E.tensor_tensor(
                                     gstr[:, ib * 1024 + half * 512:ib * 1024 + (half + 1) * 512],
                                     s1b[pr][:, half * 512:(half + 1) * 512],
                                     p[half][:, :], op=OP.mult))
                    if gate_e is not None:
                        ctx.emit("vector", lambda E, ib=ib, e=gate_e: E.tensor_tensor(
                            gstr[:, ib * 1024:(ib + 1) * 1024],
                            gstr[:, ib * 1024:(ib + 1) * 1024],
                            gb[:, e * 1024:(e + 1) * 1024], op=OP.mult))
                    vdone[ib] = ctx.inc("vector", b.sV, 1)
                ctx.wait("tensor", b.sV, vdone[nblk - 1])

            def w2_pass(nblk, first):
                edone = {}
                for ob in range(16):
                    pr = ob % 2
                    pw = [p4[pr * 4 + 0], p4[pr * 4 + 1]]
                    w2b = ws.use("tensor")
                    if ob >= 2:
                        ctx.wait("tensor", b.sV if not first else b.sS, edone[ob - 2])
                    for half in range(2):
                        for ib in range(nblk):
                            ctx.emit("tensor", lambda E, half=half, ib=ib, w=w2b, p=pw:
                                     E.matmul(p[half][:, :], w[:, ib * 128:(ib + 1) * 128],
                                              gstr[:, ib * 1024 + half * 512:
                                                   ib * 1024 + (half + 1) * 512],
                                              start=(ib == 0), stop=(ib == nblk - 1)))
                    vt = ctx.inc("tensor", b.sT, 1)
                    ws.advance(1, b.sT, vt)
                    if first:
                        ctx.wait("scalar", b.sT, vt)
                        for half in range(2):
                            ctx.emit("scalar", lambda E, ob=ob, half=half, p=pw: E.activation(
                                macc[:, ob * 1024 + half * 512:ob * 1024 + (half + 1) * 512],
                                p[half][:, :], AF.Identity))
                        edone[ob] = ctx.inc("scalar", b.sS, 1)
                    else:
                        ctx.wait("vector", b.sT, vt)
                        for half in range(2):
                            ctx.emit("vector", lambda E, ob=ob, half=half, p=pw: E.tensor_tensor(
                                macc[:, ob * 1024 + half * 512:ob * 1024 + (half + 1) * 512],
                                macc[:, ob * 1024 + half * 512:ob * 1024 + (half + 1) * 512],
                                p[half][:, :], op=OP.add))
                        edone[ob] = ctx.inc("vector", b.sV, 1)
                ctx.wait("tensor", b.sS if first else b.sV,
                         edone[15])

            ffn_block(22, None)
            w2_pass(22, True)
            for e in range(8):
                ffn_block(11, e)
                w2_pass(11, False)
        ctx.barrier("e_acc")

        with ExitStack() as sc:
            sb2 = lambda n, sh, dt: sc.enter_context(nc.sbuf_tensor(n, sh, dt))
            ps1 = lambda n, sh, dt: sc.enter_context(nc.psum_tensor(n, sh, dt))
            x3r = [sb2(f"e_x3r{i}", [128, 2048], F32) for i in range(2)]
            mot = sb2("e_mot", [128, 2048], F32)
            pstr = ps1("e_pstr", [128, 512], F32)
            ldh = ctx.sem("e_ldh")
            sto = ctx.sem("e_sto")
            hdone = []
            for tt in range(8):
                i = tt % 2
                if tt >= 2:
                    ctx.wait("sync", b.sV, hdone[tt - 2])
                ctx.dma("sync", x3r[i][:, :], dram["hx3"][tt * 128:(tt + 1) * 128, :], ldh)
                for ob in range(16):
                    q = ob % 4
                    ctx.emit("tensor", lambda E, ob=ob, tt=tt, q=q: E.matmul(
                        pstr[:, q * 128:(q + 1) * 128],
                        macc[:, ob * 1024 + tt * 128:ob * 1024 + (tt + 1) * 128],
                        b.eye_f[:, :], is_transpose=True, start=True, stop=True))
                    if q == 3:
                        vt = ctx.inc("tensor", b.sT, 1)
                        ctx.wait("scalar", b.sT, vt)
                        for qq in range(4):
                            obq = ob - 3 + qq
                            ctx.emit("scalar", lambda E, obq=obq, qq=qq: E.activation(
                                mot[:, obq * 128:(obq + 1) * 128],
                                pstr[:, qq * 128:(qq + 1) * 128], AF.Identity))
                        ss = ctx.inc("scalar", b.sS, 1)
                        ctx.wait("tensor", b.sS, ss)
                if tt >= 1:
                    ctx.wait("scalar", b.stx, b.stx.v)
                sEnd = emit_ln(ctx, b, mot[:, :], b.scr[:, 0:2048], b.scr, b.stat)
                ctx.wait("vector", b.sS, sEnd)
                ctx.wait("vector", ldh, 16 * (tt + 1))
                ctx.emit("vector", lambda E, i=i: E.tensor_tensor(
                    x3r[i][:, :], x3r[i][:, :], b.scr[:, 0:2048], op=OP.add))
                vv = ctx.inc("vector", b.sV, 1)
                hdone.append(vv)
                ctx.wait("sync", b.sV, vv)
                ctx.dma("sync", dram["out"][tt * 128:(tt + 1) * 128, :], x3r[i][:, :], sto)
            ctx.wait("gpsimd", sto, 16 * 8)


_CACHE = {}


def kernel(**inputs):
    x = inputs["L0_attn_hidden_states"].astype(np.float32, copy=False)
    wp = _prep_weights(inputs)
    stru = _structural()
    if "nc" not in _CACHE:
        _CACHE["nc"] = build()
    nc = _CACHE["nc"]
    in_maps = []
    for c in range(NCORES):
        m = dict(wp)
        m.update(_prep_core(c, x))
        m.update(stru)
        in_maps.append(m)
    res = run_bass_kernel_spmd(nc, in_maps, core_ids=list(range(NCORES)))
    outs = [res.results[c]["out"] for c in range(NCORES)]
    return np.concatenate(outs, axis=0).reshape(1, T, DM)


# version 13
# speedup vs baseline: 1.3446x; 1.0000x over previous
import numpy as np
from contextlib import ExitStack

import concourse.bass as bass
import concourse.mybir as mybir
from concourse.bass_utils import run_bass_kernel_spmd
from concourse.alu_op_type import AluOpType as OP

F32 = mybir.dt.float32
BF16 = mybir.dt.bfloat16
AF = mybir.ActivationFunctionType
AX = mybir.AxisListType

NCORES = 8
T = 8192
S = T // NCORES
DM = 2048
NEG = -30000.0
ISC = 0.08838834764831845

ENGS = ("tensor", "vector", "scalar", "gpsimd", "sync")


class Sem:
    __slots__ = ("name", "v", "h")

    def __init__(self, name):
        self.name, self.v, self.h = name, 0, None

    def bump(self, n):
        self.v += n
        return self.v


class Ctx:
    def __init__(self):
        self.ops = {e: [] for e in ENGS}
        self.sems = []
        self.dma_sems = {}

    def sem(self, name):
        s = Sem(name)
        self.sems.append(s)
        return s

    def emit(self, eng, fn, fusable=True):
        fn._fusable = fusable
        self.ops[eng].append(fn)

    def wait(self, eng, sem, val=None):
        v = sem.v if val is None else val
        if v <= 0:
            return
        self.emit(eng, lambda E, s=sem, v=v: E.wait_ge(s.h, v), fusable=False)

    def inc(self, eng, sem, n=1):
        v = sem.bump(n)
        ops = self.ops[eng]
        if ops and getattr(ops[-1], "_fusable", False):
            prev = ops[-1]

            def fused(E, prev=prev, s=sem, n=n):
                r = prev(E)
                r.then_inc(s.h, n)
            fused._fusable = False
            ops[-1] = fused
        else:
            def fn(E, s=sem, n=n):
                E.drain()
                E.sem_inc(s.h, n)
            fn._fusable = False
            self.emit(eng, fn, fusable=False)
        return v

    def dma(self, q, out, in_, sem):
        self.dma_sems[sem] = q
        if sem.v > 0:
            self.wait(q, sem, sem.v)
        v = sem.bump(16)
        self.emit(q, lambda E, o=out, i=in_, s=sem: E.dma_start(out=o, in_=i).then_inc(s.h, 16),
                  fusable=False)
        return v

    def barrier(self, name):
        for ds, q in list(self.dma_sems.items()):
            if ds.v > 0:
                self.wait(q, ds, ds.v)
        s = self.sem("bar_" + name)

        def fn(E, s=s):
            E.drain()
            E.sem_inc(s.h, 1)
        for e in ENGS:
            self.emit(e, fn)
        s.v = len(ENGS)
        for e in ENGS:
            self.wait(e, s, len(ENGS))


def replay(nc, ctx):
    with ExitStack() as st:
        for s in ctx.sems:
            s.h = st.enter_context(nc.semaphore(s.name))
        block = st.enter_context(nc.Block())

        def mk(e):
            def body(E):
                for fn in ctx.ops[e]:
                    fn(E)
            return body

        for e in ENGS:
            getattr(block, e)(mk(e))


class WStream:
    def __init__(self, ctx, nc, st_, name, cols_max, nbuf=3):
        self.ctx, self.name, self.nbuf = ctx, name, nbuf
        self.bufs = [st_.enter_context(nc.sbuf_tensor(f"{name}{i}", [128, cols_max], BF16))
                     for i in range(nbuf)]
        self.ld = ctx.sem(name + "_ld")
        self.free = ctx.sem(name + "_free")
        self.n_loaded = 0
        self.n_used = 0
        self.n_emitted = 0
        self.pending = []

    def load(self, dram_ap):
        if self.n_loaded - len(self.pending) < self.nbuf and not self.pending:
            self._emit_load(dram_ap)
        else:
            self.pending.append(dram_ap)
        self.n_loaded += 1

    def _emit_load(self, dram_ap):
        buf = self.bufs[self.n_emitted % self.nbuf]
        cols = dram_ap.shape[-1]
        self.ctx.dma("gpsimd", buf[:, 0:cols], dram_ap, self.ld)
        self.n_emitted += 1

    def advance(self, k, gate_sem, gate_val):
        self.n_used += k
        if self.pending:
            self.ctx.wait("gpsimd", gate_sem, gate_val)
            for _ in range(min(k, len(self.pending))):
                self._emit_load(self.pending.pop(0))

    def use(self, eng):
        i = self.n_used
        self.ctx.wait(eng, self.ld, 16 * (i + 1))
        return self.bufs[i % self.nbuf]

    def done(self, eng):
        self.n_used += 1
        self.ctx.inc(eng, self.free, 1)

    def peek(self, eng, k):
        i = self.n_used + k
        self.ctx.wait(eng, self.ld, 16 * (i + 1))
        return self.bufs[i % self.nbuf]

    def release(self, eng, k):
        self.n_used += k
        self.ctx.inc(eng, self.free, k)

    def free_via(self, eng, k, gate_sem, gate_val):
        self.n_used += k
        self.ctx.wait(eng, gate_sem, gate_val)
        self.ctx.inc(eng, self.free, k)


def _prep_weights(w):
    C = np.ascontiguousarray
    p = {}
    p["inpw"] = C(w["L0_attn_in_proj_w"].reshape(64, 128, 16, 128)
                  .transpose(0, 3, 2, 1)).reshape(64, 128, 2048)
    p["outw"] = C(w["L0_attn_out_proj_w"].reshape(16, 128, 32, 128)
                  .transpose(0, 3, 2, 1)).reshape(16, 128, 4096)
    p["xpw"] = C(w["L0_attn_x_proj_w"].reshape(160, 32, 128)
                 .transpose(2, 1, 0)).reshape(128, 32 * 160)
    p["dtw"] = C(w["L0_attn_dt_proj_w"].reshape(32, 128, 128)
                 .transpose(2, 0, 1)).reshape(128, 32 * 128)
    p["convw"] = C(w["L0_attn_conv_w"].reshape(32, 128, 4)
                   .transpose(1, 0, 2)).reshape(128, 128)
    p["convb"] = C(w["L0_attn_conv_b"].reshape(32, 128).T)
    p["dtb"] = C(w["L0_attn_dt_proj_b"].reshape(32, 128).T)
    p["dvec"] = C(w["L0_attn_D"].reshape(32, 128).T)
    p["alog"] = C(w["L0_attn_A_log"].reshape(32, 128, 16)
                  .transpose(1, 0, 2)).reshape(128, 512)
    p["nsc"] = w["L0_attn_norm_scale"].reshape(1, 2048)
    p["ffnw"] = C(w["L0_ffn_weight"].T.reshape(16, 128, 2048)
                  .transpose(1, 0, 2)).reshape(128, 16 * 2048)
    for nm, key in (("wq", "L1_attn_Wq"), ("wk", "L1_attn_Wk"), ("wv", "L1_attn_Wv")):
        p[nm] = C(w[key].reshape(16, 16, 128, 128).transpose(0, 2, 1, 3)).reshape(16, 128, 2048)
    p["wo"] = C(w["L1_attn_Wo"].reshape(16, 128, 16, 128)
                .transpose(2, 1, 0, 3)).reshape(16, 128, 2048)
    p["bq"] = C(w["L1_attn_bq"].T)
    p["bk"] = C(w["L1_attn_bk"].T)
    p["bv"] = C(w["L1_attn_bv"]).reshape(1, 2048)
    p["bo"] = w["L1_attn_bo"].reshape(1, 2048)
    p["w1"] = C(w["L1_ffn_W1"].reshape(8, 16, 128, 11, 128)
                .transpose(0, 3, 2, 1, 4)).reshape(8, 11, 128, 2048)
    p["w3"] = C(w["L1_ffn_W3"].reshape(8, 16, 128, 11, 128)
                .transpose(0, 3, 2, 1, 4)).reshape(8, 11, 128, 2048)
    p["w2"] = C(w["L1_ffn_W2"].reshape(8, 11, 128, 16, 128)
                .transpose(0, 3, 2, 1, 4)).reshape(8, 16, 128, 1408)
    p["sw1"] = C(w["L1_ffn_sw1"].reshape(22, 128, 16, 128)
                 .transpose(0, 3, 2, 1)).reshape(22, 128, 2048)
    p["sw3"] = C(w["L1_ffn_sw3"].reshape(22, 128, 16, 128)
                 .transpose(0, 3, 2, 1)).reshape(22, 128, 2048)
    p["sw2"] = C(w["L1_ffn_sw2"].reshape(16, 128, 22, 128)
                 .transpose(0, 3, 2, 1)).reshape(16, 128, 2816)
    p["gatew"] = C(w["L1_ffn_gate_w"].T.reshape(16, 128, 8)
                   .transpose(1, 0, 2)).reshape(128, 128)
    p["rbias"] = w["L1_ffn_router_bias"].reshape(1, 8)
    return p


def _prep_core(c, x):
    d = {}
    t0 = c * S
    d["x_sh"] = np.ascontiguousarray(x[0, t0:t0 + S, :])
    d["xprev"] = (np.ascontiguousarray(x[0, t0 - 128:t0, :]) if c > 0
                  else np.zeros((128, DM), np.float32))
    sel = np.zeros((128, 8), np.float32)
    sel[:, c] = 1.0
    d["selmask"] = sel
    bt = np.zeros((128, 64), np.float32)
    for jg in range(64):
        if jg // 8 >= c:
            bt[:, jg] = NEG
    d["bias_tab"] = bt
    return d


def _structural():
    s = {"eye": np.eye(128, dtype=np.float32)}
    rsel = np.zeros((32, 32 * 128), np.float32)
    for n in range(32):
        rsel[n, n * 128:(n + 1) * 128] = 1.0
    s["rsel"] = rsel
    esel = np.zeros((8, 8 * 128), np.float32)
    for e in range(8):
        esel[e, e * 128:(e + 1) * 128] = 1.0
    s["esel"] = esel
    m = np.zeros((128, 8 * 1024), np.float32)
    pp = np.arange(128)[:, None]
    tt = np.arange(1024)[None, :]
    for l in range(8):
        m[:, l * 1024:(l + 1) * 1024] = (l * 128 + pp <= tt).astype(np.float32)
    s["lmask"] = m
    return s


def emit_ln(ctx, b, xin, xout, scr, stat, d=2048, eps=1e-5):
    mean, rstd, msq, bias = stat[:, 0:1], stat[:, 1:2], stat[:, 2:3], stat[:, 3:4]
    ctx.emit("scalar", lambda E: E.activation(scr[:, 0:d], xin, AF.Square))
    vS = ctx.inc("scalar", b.sS, 1)
    ctx.wait("vector", b.sS, vS)
    ctx.emit("vector", lambda E: E.tensor_scalar(scr[:, 0:d], scr[:, 0:d], 1.0, None,
                                                 op0=OP.mult, op1=OP.add, accum_out=rstd))
    ctx.emit("vector", lambda E: E.tensor_copy(stat[:, 4:8], stat[:, 0:4]))
    ctx.emit("vector", lambda E: E.tensor_scalar(scr[:, 0:d], xin, 1.0, None,
                                                 op0=OP.mult, op1=OP.add, accum_out=mean))
    ctx.emit("vector", lambda E: E.tensor_copy(stat[:, 4:8], stat[:, 0:4]))
    ctx.emit("vector", lambda E: E.tensor_scalar(rstd, rstd, 1.0 / d, None, op0=OP.mult))
    ctx.emit("vector", lambda E: E.tensor_scalar(mean, mean, 1.0 / d, None, op0=OP.mult))
    ctx.emit("vector", lambda E: E.tensor_tensor(msq, mean, mean, op=OP.mult))
    ctx.emit("vector", lambda E: E.tensor_tensor(rstd, rstd, msq, op=OP.subtract))
    ctx.emit("vector", lambda E: E.tensor_scalar(rstd, rstd, 1.0, eps, op0=OP.mult, op1=OP.add))
    vV = ctx.inc("vector", b.sV, 1)
    ctx.wait("scalar", b.sV, vV)
    ctx.emit("scalar", lambda E: E.activation(rstd, rstd, AF.Ln))
    ctx.emit("scalar", lambda E: E.activation(rstd, rstd, AF.Exp, scale=-0.5))
    vS2 = ctx.inc("scalar", b.sS, 1)
    ctx.wait("vector", b.sS, vS2)
    ctx.emit("vector", lambda E: E.tensor_copy(stat[:, 4:8], stat[:, 0:4]))
    ctx.emit("vector", lambda E: E.tensor_scalar(bias, mean, rstd[:, 0:1], -1.0,
                                                 op0=OP.mult, op1=OP.mult))
    vV2 = ctx.inc("vector", b.sV, 1)
    ctx.wait("scalar", b.sV, vV2)
    ctx.emit("scalar", lambda E: E.activation(xout, xin, AF.Identity, scale=rstd, bias=bias))
    return ctx.inc("scalar", b.sS, 1)


def emit_tr_strip(ctx, b, src_bf, nblk, col_fn, dst, psum):
    for i in range(nblk):
        q = i % 4
        ctx.emit("tensor", lambda E, i=i, q=q: E.matmul(
            psum[:, q * 128:(q + 1) * 128], src_bf[:, i * 128:(i + 1) * 128],
            b.eye_b[:, :], is_transpose=True, start=True, stop=True))
        if q == 3 or i == nblk - 1:
            vt = ctx.inc("tensor", b.sT, 1)
            ctx.wait("scalar", b.sT, vt)
            for qq in range(q + 1):
                i0 = i - q + qq
                ctx.emit("scalar", lambda E, i0=i0, qq=qq: E.activation(
                    dst[:, col_fn(i0):col_fn(i0) + 128],
                    psum[:, qq * 128:(qq + 1) * 128], AF.Identity))
            ss = ctx.inc("scalar", b.sS, 1)
            ctx.wait("tensor", b.sS, ss)
    return b.sS.v


def build():
    nc = bass.Bass("TRN2", target_bir_lowering=False, debug=False,
                   detect_race_conditions=False)
    ctx = Ctx()
    dram = {}

    def din(n, sh, dt=F32):
        dram[n] = nc.dram_tensor(n, list(sh), dt, kind="ExternalInput").ap()

    def dout(n, sh, dt=F32):
        dram[n] = nc.dram_tensor(n, list(sh), dt, kind="ExternalOutput").ap()

    import os as _os
    _dbg = set((_os.environ.get("MK_DBG") or "").split(",")) - {""}

    def dint(n, sh, dt=F32):
        if n in _dbg:
            dram[n] = nc.dram_tensor(n, list(sh), dt, kind="ExternalOutput").ap()
        else:
            dram[n] = nc.dram_tensor(n, list(sh), dt).ap()

    din("x_sh", (S, DM)); din("xprev", (128, DM))
    din("selmask", (128, 8)); din("bias_tab", (128, 64))
    din("eye", (128, 128)); din("lmask", (128, 8 * 1024))
    din("rsel", (32, 32 * 128)); din("esel", (8, 8 * 128))
    din("inpw", (64, 128, 2048)); din("outw", (16, 128, 4096))
    din("xpw", (128, 32 * 160)); din("dtw", (128, 32 * 128))
    din("convw", (128, 128)); din("convb", (128, 32)); din("dtb", (128, 32))
    din("dvec", (128, 32)); din("alog", (128, 512)); din("nsc", (1, 2048))
    din("ffnw", (128, 16 * 2048))
    din("wq", (16, 128, 2048)); din("wk", (16, 128, 2048)); din("wv", (16, 128, 2048))
    din("wo", (16, 128, 2048))
    din("bq", (128, 16)); din("bk", (128, 16)); din("bv", (1, 2048)); din("bo", (1, 2048))
    din("w1", (8, 11, 128, 2048)); din("w3", (8, 11, 128, 2048)); din("w2", (8, 16, 128, 1408))
    din("sw1", (22, 128, 2048)); din("sw3", (22, 128, 2048)); din("sw2", (16, 128, 2816))
    din("gatew", (128, 128)); din("rbias", (1, 8))
    dout("out", (S, DM))
    dint("hx1", (S, DM)); dint("hx2", (S, DM)); dint("hx3", (S, DM))
    dint("hxn0", (S, DM))
    dint("hsz", (128, 32 * S), BF16); dint("hxc", (128, 32 * S), BF16)
    dint("hy0", (128, 32 * S), BF16); dint("hcumd", (128, 32 * S), BF16)
    dint("hy2", (128, 32 * S), BF16)
    dint("hwqt", (128, 16 * 2048), BF16)
    dint("cc_in", (1, 128 * 1024)); dint("cc_out", (8, 128 * 1024))
    if "hha" in _dbg:
        dram["hha"] = nc.dram_tensor("hha", [128, 1], F32, kind="ExternalOutput").ap()
    if "hxin" in _dbg:
        dram["hxin"] = nc.dram_tensor("hxin", [128, 32 * 1027], BF16, kind="ExternalOutput").ap()
    if "hxq" in _dbg:
        dram["hxq"] = nc.dram_tensor("hxq", [S, DM], BF16, kind="ExternalOutput").ap()
    if "hffn" in _dbg:
        dram["hffn"] = nc.dram_tensor("hffn", [S, DM], F32, kind="ExternalOutput").ap()
    if "hbst" in _dbg:
        dram["hbst"] = nc.dram_tensor("hbst", [128, 8], F32, kind="ExternalOutput").ap()
        dram["hvq"] = nc.dram_tensor("hvq", [128, 2048], F32, kind="ExternalOutput").ap()
        dram["hcur"] = nc.dram_tensor("hcur", [128, 2048], F32, kind="ExternalOutput").ap()
    dint("kv_in", (1, 128 * 32768), BF16); dint("kv_out", (8, 128 * 32768), BF16)

    class B:
        pass
    b = B()
    with ExitStack() as top:
        sb = lambda n, sh, dt: top.enter_context(nc.sbuf_tensor(n, sh, dt))
        b.eye_b = sb("eye_b", [128, 128], BF16)
        b.eye_f = sb("eye_f", [128, 128], F32)
        b.ones_b = sb("ones_b", [1, 128], BF16)
        b.ones_f = sb("ones_f", [1, 128], F32)
        b.onescol_b = sb("onescol_b", [128, 1], BF16)
        b.onescol_f = sb("onescol_f", [128, 1], F32)
        b.ha = sb("ha", [128, 1], F32)
        b.stat = sb("stat", [128, 8], F32)
        b.scr = sb("scr", [128, 2052], F32)
        b.sT = ctx.sem("sT"); b.sV = ctx.sem("sV"); b.sS = ctx.sem("sS")
        b.sG = ctx.sem("sG"); b.stx = ctx.sem("stx")
        ld0 = ctx.sem("ld0")
        ld0g = ctx.sem("ld0g")
        ctx.dma("sync", b.eye_f[:, :], dram["eye"], ld0)
        ctx.dma("gpsimd", b.eye_b[:, :], dram["eye"], ld0g)
        ctx.emit("gpsimd", lambda E: E.memset(b.ones_b[:, :], 1.0))
        ctx.emit("gpsimd", lambda E: E.memset(b.ones_f[:, :], 1.0))
        ctx.emit("gpsimd", lambda E: E.memset(b.onescol_b[:, :], 1.0))
        ctx.emit("gpsimd", lambda E: E.memset(b.onescol_f[:, :], 1.0))
        ctx.emit("gpsimd", lambda E: E.memset(b.stat[:, :], 0.0))
        ctx.emit("gpsimd", lambda E: E.memset(b.scr[:, :], 0.0))
        gi = ctx.inc("gpsimd", b.sG, 1)
        for e in ("tensor", "vector", "scalar"):
            ctx.wait(e, ld0, 16)
            ctx.wait(e, ld0g, 16)
            ctx.wait(e, b.sG, gi)

        import os
        nph = int(os.environ.get("MK_PHASES", "5"))
        emit_p0(ctx, nc, b, dram)
        ctx.barrier("p0")
        if nph >= 2:
            emit_p1(ctx, nc, b, dram)
            ctx.barrier("p1")
        if nph >= 3:
            emit_p2(ctx, nc, b, dram)
            ctx.barrier("p2")
        if nph >= 4:
            emit_p3(ctx, nc, b, dram)
            ctx.barrier("p3")
        if nph >= 5:
            emit_p4(ctx, nc, b, dram)

        replay(nc, ctx)
    return nc


def emit_p0(ctx, nc, b, dram):
    with ExitStack() as st_:
        sbl = lambda n, sh, dt: st_.enter_context(nc.sbuf_tensor(n, sh, dt))
        slab = [sbl(f"q_slab{i}", [128, 2048], F32) for i in range(2)]
        absw = sbl("q_absw", [128, 2048], F32)
        sgn = sbl("q_sgn", [128, 2048], F32)
        ind = sbl("q_ind", [128, 2048], F32)
        wqb = sbl("q_wqb", [128, 2048], BF16)
        junk = sbl("q_junk", [128, 2048], BF16)
        acc = sbl("q_acc", [128, 18], F32)
        ld = ctx.sem("q_ld"); fr = ctx.sem("q_fr")
        for i in range(16):
            if i >= 2:
                ctx.wait("sync", fr, i - 1)
            ctx.dma("sync", slab[i % 2][:, :], dram["ffnw"][:, i * 2048:(i + 1) * 2048], ld)
        for i in range(16):
            ctx.wait("scalar", ld, 16 * (i + 1))
            ctx.emit("scalar", lambda E, i=i: E.activation(absw[:, :], slab[i % 2][:, :],
                                                           AF.Abs))
            ss0 = ctx.inc("scalar", b.sS, 1)
            ctx.inc("scalar", fr, 1)
            ctx.wait("vector", b.sS, ss0)
            ctx.emit("vector", lambda E: E.tensor_copy(junk[:, 0:16], absw[:, 0:16]))
            ctx.emit("vector", lambda E, i=i: E.tensor_scalar(
                junk[:, :], absw[:, :], 1.0, None, op0=OP.mult, op1=OP.add,
                accum_out=acc[:, i:i + 1]))
        ctx.emit("vector", lambda E: E.tensor_copy(junk[:, 0:16], absw[:, 0:16]))
        ctx.emit("vector", lambda E: E.tensor_reduce(acc[:, 16:17], acc[:, 0:16], AX.X, OP.add))
        vv = ctx.inc("vector", b.sV, 1)
        with nc.psum_tensor("q_ps", [128, 8], F32) as ps:
            ctx.wait("tensor", b.sV, vv)
            ctx.emit("tensor", lambda E: E.matmul(ps[0:1, 0:1], b.onescol_f[:, :],
                                                  acc[:, 16:17], start=True, stop=True))
            vt = ctx.inc("tensor", b.sT, 1)
            ctx.wait("vector", b.sT, vt)
            ctx.emit("vector", lambda E: E.tensor_scalar(
                acc[0:1, 17:18], ps[0:1, 0:1], 1.0 / (2048.0 * 2048.0), 1e-8,
                op0=OP.mult, op1=OP.max))
            vv2 = ctx.inc("vector", b.sV, 1)
            ctx.wait("tensor", b.sV, vv2)
            ctx.emit("tensor", lambda E: E.matmul(ps[:, 1:2], b.ones_f[0:1, :],
                                                  acc[0:1, 17:18], start=True, stop=True))
            vt2 = ctx.inc("tensor", b.sT, 1)
            ctx.wait("vector", b.sT, vt2)
            ctx.emit("vector", lambda E: E.tensor_scalar(b.ha[:, :], ps[:, 1:2], 0.5, None,
                                                         op0=OP.mult))
            ctx.inc("vector", b.sV, 1)
        ld2 = ctx.sem("q_ld2"); fr2 = ctx.sem("q_fr2"); stq = ctx.sem("q_st")
        def load2(i):
            if i >= 2:
                ctx.wait("sync", fr2, i - 1)
            ctx.dma("sync", slab[i % 2][:, :], dram["ffnw"][:, i * 2048:(i + 1) * 2048], ld2)
        load2(0)
        load2(1)
        pv_vv = []
        for i in range(16):
            ctx.wait("scalar", ld2, 16 * (i + 1))
            if i >= 1:
                ctx.wait("scalar", b.sV, pv_vv[i - 1])
            ctx.emit("scalar", lambda E, i=i: E.activation(sgn[:, :], slab[i % 2][:, :], AF.Sign))
            ctx.emit("scalar", lambda E, i=i: E.activation(absw[:, :], slab[i % 2][:, :],
                                                           AF.Abs))
            ss = ctx.inc("scalar", b.sS, 1)
            ctx.wait("vector", b.sS, ss)
            ctx.emit("vector", lambda E: E.tensor_scalar(
                ind[:, :], absw[:, :], b.ha[:, 0:1], None, op0=OP.is_ge))
            ctx.emit("vector", lambda E: E.tensor_tensor(wqb[:, :], sgn[:, :], ind[:, :],
                                                         op=OP.mult))
            ctx.inc("vector", fr2, 1)
            vv3 = ctx.inc("vector", b.sV, 1)
            pv_vv.append(vv3)
            ctx.wait("sync", b.sV, vv3)
            ctx.dma("sync", dram["hwqt"][:, i * 2048:(i + 1) * 2048], wqb[:, :], stq)
            if i + 2 < 16:
                load2(i + 2)
            ctx.wait("vector", stq, stq.v)
        ctx.wait("sync", stq, 256)
        if "hha" in dram:
            ctx.wait("sync", b.sV, b.sV.v)
            ctx.dma("sync", dram["hha"], b.ha[:, :], stq)


def emit_p1(ctx, nc, b, dram):
    import os as _os
    p1s = int(_os.environ.get("MK_P1S", "9"))
    hsz, hxc, hy0 = dram["hsz"], dram["hxc"], dram["hy0"]
    hcumd, hy2 = dram["hcumd"], dram["hy2"]
    with ExitStack() as st_:
        sbl = lambda n, sh, dt: st_.enter_context(nc.sbuf_tensor(n, sh, dt))
        ps = lambda n, sh, dt: st_.enter_context(nc.psum_tensor(n, sh, dt))
        convw_s = sbl("m_convw", [128, 128], F32)
        convb_s = sbl("m_convb", [128, 32], F32)
        dtb_s = sbl("m_dtb", [128, 32], F32)
        dvec_s = sbl("m_dvec", [128, 32], F32)
        a_s = sbl("m_a", [128, 512], F32)
        etot = sbl("m_etot", [128, 512], F32)
        hlast = sbl("m_hlast", [128, 512], F32)
        sdelta = sbl("m_sdelta", [128, 32], F32)
        bcrows_bf = sbl("m_bcbf", [32, 1024], BF16)
        hin = sbl("m_hin", [128, 512], F32)
        ldp = ctx.sem("m_ldp")
        ldpg = ctx.sem("m_ldpg")
        ctx.dma("sync", convw_s[:, :], dram["convw"], ldp)
        ctx.dma("sync", convb_s[:, :], dram["convb"], ldp)
        ctx.dma("sync", dtb_s[:, :], dram["dtb"], ldp)
        ctx.dma("sync", dvec_s[:, :], dram["dvec"], ldp)
        ctx.dma("sync", a_s[:, :], dram["alog"], ldp)
        ldp_all = ldp.v
        ctx.wait("scalar", ldp, 80)
        ctx.emit("scalar", lambda E: E.activation(a_s[:, :], a_s[:, :], AF.Exp))
        sA = ctx.inc("scalar", b.sS, 1)
        ctx.wait("vector", b.sS, sA)
        ctx.emit("vector", lambda E: E.tensor_scalar(a_s[:, :], a_s[:, :], -1.0, None,
                                                     op0=OP.mult))
        ctx.inc("vector", b.sV, 1)

        with ExitStack() as sc:
            sb2 = lambda n, sh, dt: sc.enter_context(nc.sbuf_tensor(n, sh, dt))
            ps1 = lambda n, sh, dt: sc.enter_context(nc.psum_tensor(n, sh, dt))
            xt = [sb2(f"m_xt{i}", [128, 2048], F32) for i in range(2)]
            xbf = sb2("m_xbf", [128, 2048], BF16)
            nsc_s = sb2("m_nsc", [128, 2048], BF16)
            nscrow = sb2("m_nscrow", [1, 2048], BF16)
            ldn = ctx.sem("m_ldn")
            ctx.dma("gpsimd", nscrow[:, :], dram["nsc"], ldn)
            xnmT = sb2("m_xnmT", [128, 16 * 1152], BF16)
            xin = sb2("m_xin", [128, 32 * 1027], BF16)
            szt = sb2("m_szt", [128, 1024], BF16)
            trp = ps1("m_trp", [128, 512], BF16)
            pse = [ps1(f"m_pse{i}", [128, 512], F32) for i in range(4)]
            psh = ps1("m_psh", [128, 128], F32)
            ctx.wait("tensor", ldn, 16)
            for j in range(4):
                ctx.emit("tensor", lambda E, j=j: E.matmul(
                    pse[0][:, :], b.ones_b[0:1, :], nscrow[0:1, j * 512:(j + 1) * 512],
                    start=True, stop=True))
                vt = ctx.inc("tensor", b.sT, 1)
                ctx.wait("scalar", b.sT, vt)
                ctx.emit("scalar", lambda E, j=j: E.activation(
                    nsc_s[:, j * 512:(j + 1) * 512], pse[0][:, :], AF.Identity))
                ss = ctx.inc("scalar", b.sS, 1)
                ctx.wait("tensor", b.sS, ss)
            ldx = ctx.sem("m_ldx")
            tdone = []
            for tt in range(9):
                i = tt % 2
                src = dram["xprev"] if tt == 0 else dram["x_sh"][(tt - 1) * 128:tt * 128, :]
                if tt >= 2:
                    ctx.wait("sync", b.sV, tdone[tt - 2])
                ctx.dma("sync", xt[i][:, :], src, ldx)
                ctx.wait("scalar", ldx, 16 * (tt + 1))
                if tt >= 1:
                    ctx.wait("scalar", b.sV, tdone[tt - 1])
                    ctx.wait("scalar", b.stx, b.stx.v)
                sEnd = emit_ln(ctx, b, xt[i][:, :], b.scr[:, 0:2048], b.scr, b.stat)
                if tt >= 1:
                    ctx.wait("sync", b.sS, sEnd)
                    ctx.dma("sync", dram["hxn0"][(tt - 1) * 128:tt * 128, :],
                            b.scr[:, 0:2048], b.stx)
                ctx.emit("scalar", lambda E, i=i: E.activation(
                    xt[i][:, :], b.scr[:, 0:2048], AF.Square))
                s2 = ctx.inc("scalar", b.sS, 1)
                ctx.wait("vector", b.sS, s2)
                ctx.emit("vector", lambda E, i=i: E.tensor_scalar(
                    xt[i][:, :], xt[i][:, :], 1.0, None, op0=OP.mult, op1=OP.add,
                    accum_out=b.stat[:, 2:3]))
                ctx.emit("vector", lambda E: E.tensor_copy(b.stat[:, 4:8], b.stat[:, 0:4]))
                ctx.emit("vector", lambda E: E.tensor_scalar(
                    b.stat[:, 2:3], b.stat[:, 2:3], 1.0 / 2048.0, 1e-5,
                    op0=OP.mult, op1=OP.add))
                v2 = ctx.inc("vector", b.sV, 1)
                ctx.wait("scalar", b.sV, v2)
                ctx.emit("scalar", lambda E: E.activation(b.stat[:, 2:3], b.stat[:, 2:3], AF.Ln))
                ctx.emit("scalar", lambda E: E.activation(b.stat[:, 2:3], b.stat[:, 2:3],
                                                          AF.Exp, scale=-0.5))
                s3 = ctx.inc("scalar", b.sS, 1)
                ctx.wait("vector", b.sS, s3)
                ctx.emit("vector", lambda E: E.tensor_copy(b.stat[:, 4:8], b.stat[:, 0:4]))
                ctx.emit("vector", lambda E, i=i: E.tensor_scalar(
                    xt[i][:, :], b.scr[:, 0:2048], b.stat[:, 2:3], None, op0=OP.mult))
                ctx.emit("vector", lambda E, i=i: E.tensor_tensor(
                    xbf[:, :], xt[i][:, :], nsc_s[:, :], op=OP.mult))
                v3 = ctx.inc("vector", b.sV, 1)
                tdone.append(v3)
                ctx.wait("tensor", b.sV, v3)
                emit_tr_strip(ctx, b, xbf, 16,
                              (lambda tt: (lambda dm: dm * 1152 + tt * 128))(tt), xnmT, trp)
                ctx.wait("vector", b.sS, b.sS.v)
            if int(_os.environ.get("MK_P1SUB", "9")) < 2:
                ctx.barrier("m_sub1")
                return
            ws_in = WStream(ctx, nc, sc, "m_win", 2048, 3)
            for eb in range(64):
                ws_in.load(dram["inpw"][eb])
            g_evac = {}
            g_idx = 0
            for eb in range(64):
                wbuf = ws_in.use("tensor")
                is_x = eb < 32
                if is_x:
                    for dm in range(16):
                        ctx.emit("tensor", lambda E, dm=dm, w=wbuf: E.matmul(
                            psh[:, :], w[:, dm * 128:(dm + 1) * 128],
                            xnmT[:, dm * 1152:dm * 1152 + 128],
                            start=(dm == 0), stop=(dm == 15)))
                    vt = ctx.inc("tensor", b.sT, 1)
                    ctx.wait("scalar", b.sT, vt)
                    ctx.emit("scalar", lambda E, eb=eb: E.activation(
                        xin[:, eb * 1027:eb * 1027 + 3], psh[:, 125:128], AF.Identity))
                    hv = ctx.inc("scalar", b.sS, 1)
                    ctx.wait("tensor", b.sS, hv)
                for half in range(2):
                    pp = pse[g_idx % 4]
                    if g_idx >= 4:
                        ctx.wait("tensor", b.sS, g_evac[g_idx - 4])
                    for dm in range(16):
                        ctx.emit("tensor", lambda E, dm=dm, half=half, w=wbuf, pp=pp: E.matmul(
                            pp[:, :], w[:, dm * 128:(dm + 1) * 128],
                            xnmT[:, dm * 1152 + 128 + half * 512:
                                 dm * 1152 + 128 + (half + 1) * 512],
                            start=(dm == 0), stop=(dm == 15)))
                    vt = ctx.inc("tensor", b.sT, 1)
                    ctx.wait("scalar", b.sT, vt)
                    if is_x:
                        ctx.emit("scalar", lambda E, eb=eb, half=half, pp=pp: E.activation(
                            xin[:, eb * 1027 + 3 + half * 512:
                                eb * 1027 + 3 + (half + 1) * 512], pp[:, :], AF.Identity))
                    else:
                        ctx.emit("scalar", lambda E, half=half, pp=pp: E.activation(
                            szt[:, half * 512:(half + 1) * 512], pp[:, :], AF.Silu))
                    g_evac[g_idx] = ctx.inc("scalar", b.sS, 1)
                    g_idx += 1
                ws_in.advance(1, b.sT, b.sT.v)
                if not is_x:
                    ctx.wait("sync", b.sS, b.sS.v)
                    ctx.dma("sync", hsz[:, (eb - 32) * 1024:(eb - 31) * 1024], szt[:, :], b.stx)
                    ctx.wait("scalar", b.stx, b.stx.v)
            xct = sb2("m_xct", [128, 1024], BF16)
            cacc = sb2("m_cacc", [128, 1024], F32)
            if "hxin" in dram:
                ctx.wait("sync", b.sS, b.sS.v)
                ctx.dma("sync", dram["hxin"], xin[:, :], b.stx)
            ctx.wait("vector", b.sS, b.sS.v)
            ctx.wait("vector", ldp, 32)
            for db in range(32):
                base = db * 1027
                ctx.emit("vector", lambda E, base=base, db=db: E.tensor_scalar(
                    cacc[:, :], xin[:, base:base + 1024], convw_s[:, db * 4:db * 4 + 1],
                    None, op0=OP.mult))
                for k in range(1, 4):
                    ctx.emit("vector", lambda E, base=base, db=db, k=k: E.scalar_tensor_tensor(
                        cacc[:, :], xin[:, base + k:base + k + 1024],
                        convw_s[:, db * 4 + k:db * 4 + k + 1], cacc[:, :],
                        op0=OP.mult, op1=OP.add))
                vv = ctx.inc("vector", b.sV, 1)
                ctx.wait("scalar", b.sV, vv)
                ctx.emit("scalar", lambda E, db=db: E.activation(
                    xct[:, :], cacc[:, :], AF.Silu, bias=convb_s[:, db:db + 1]))
                ss = ctx.inc("scalar", b.sS, 1)
                ctx.wait("sync", b.sS, ss)
                ctx.dma("sync", hxc[:, db * 1024:(db + 1) * 1024], xct[:, :], b.stx)
                ctx.wait("scalar", b.stx, b.stx.v)
                ctx.wait("vector", b.sS, ss)
        ctx.barrier("m_ab")
        if p1s < 2:
            return

        with ExitStack() as sc:
            sb2 = lambda n, sh, dt: sc.enter_context(nc.sbuf_tensor(n, sh, dt))
            cb_t = sb2("m_cb", [128, 16 * 1024], BF16)
            tmb = sb2("m_tmb", [128, 1024], BF16)
            xcr = [sb2(f"m_xcr{i}", [128, 1024], BF16) for i in range(2)]
            with ExitStack() as sd:
                sb3 = lambda n, sh, dt: sd.enter_context(nc.sbuf_tensor(n, sh, dt))
                delta = sb3("m_delta", [128, 32 * 1024], BF16)
                bb = sb3("m_bb", [128, 16 * 1024], BF16)
                du = sb3("m_du", [128, 1024], BF16)
                wt = sb3("m_wt", [128, 1024], BF16)
                ht = sb3("m_ht", [128, 1024], BF16)
                y0 = [sb3(f"m_y0{i}", [128, 1024], BF16) for i in range(2)]
                sdd = sd.enter_context(ExitStack())
                sb4 = lambda n, sh, dt: sdd.enter_context(nc.sbuf_tensor(n, sh, dt))
                xpw_s = sb4("m_xpw", [128, 32 * 160], BF16)
                dtw_s = sb4("m_dtw", [128, 32 * 128], BF16)
                ctx.dma("gpsimd", xpw_s[:, :], dram["xpw"], ldpg)
                ctx.dma("gpsimd", dtw_s[:, :], dram["dtw"], ldpg)
                ldp_all = ldpg.v
                dtlow = sb4("m_dtlow", [128, 1024], BF16)
                sp1 = sb4("m_sp1", [128, 1024], BF16)
                sp2 = sb4("m_sp2", [128, 1024], BF16)
                cumd_t = [sb4(f"m_cumd{i}", [128, 1024], BF16) for i in range(2)]
                onesf_t = sb4("m_ones_t", [128, 1024], BF16)
                ctx.emit("gpsimd", lambda E: E.memset(onesf_t[:, :], 1.0))
                gv = ctx.inc("gpsimd", b.sG, 1)
                ps2 = lambda n, sh, dt: sdd.enter_context(nc.psum_tensor(n, sh, dt))
                psd = [ps2(f"m_psd{i}", [128, 512], F32) for i in range(2)]
                psbc = [ps2(f"m_psbc{i}", [32, 512], F32) for i in range(2)]
                psq = [ps2(f"m_psq{i}", [128, 512], F32) for i in range(2)]
                ldc = ctx.sem("m_ldc")
                db_done = []
                for db in range(32):
                    if db >= 2:
                        ctx.wait("sync", b.sT, db_done[db - 2])
                    ctx.dma("sync", xcr[db % 2][:, :], hxc[:, db * 1024:(db + 1) * 1024], ldc)
                    ctx.wait("tensor", ldc, 16 * (db + 1))
                    ctx.wait("tensor", ldpg, ldp_all)
                    for half in range(2):
                        ctx.emit("tensor", lambda E, db=db, half=half: E.matmul(
                            psd[half][:, :], xpw_s[:, db * 160:db * 160 + 128],
                            xcr[db % 2][:, half * 512:(half + 1) * 512],
                            start=(db == 0), stop=(db == 31)))
                        ctx.emit("tensor", lambda E, db=db, half=half: E.matmul(
                            psbc[half][:, :], xpw_s[:, db * 160 + 128:db * 160 + 160],
                            xcr[db % 2][:, half * 512:(half + 1) * 512],
                            start=(db == 0), stop=(db == 31)))
                    db_done.append(ctx.inc("tensor", b.sT, 1))
                ctx.wait("scalar", b.sT, b.sT.v)
                ctx.emit("scalar", lambda E: E.activation(dtlow[:, 0:512], psd[0][:, :],
                                                          AF.Identity))
                ctx.emit("scalar", lambda E: E.activation(dtlow[:, 512:1024], psd[1][:, :],
                                                          AF.Identity))
                ctx.emit("scalar", lambda E: E.activation(bcrows_bf[:, 0:512], psbc[0][:, :],
                                                          AF.Identity))
                ctx.emit("scalar", lambda E: E.activation(bcrows_bf[:, 512:1024], psbc[1][:, :],
                                                          AF.Identity))
                ssx = ctx.inc("scalar", b.sS, 1)
                ctx.wait("tensor", b.sS, ssx)
                dvv = []
                for db in range(32):
                    for half in range(2):
                        ctx.emit("tensor", lambda E, db=db, half=half: E.matmul(
                            psq[half][:, :], dtw_s[:, db * 128:(db + 1) * 128],
                            dtlow[:, half * 512:(half + 1) * 512], start=True, stop=True))
                    vt = ctx.inc("tensor", b.sT, 1)
                    ctx.wait("scalar", b.sT, vt)
                    if db >= 1:
                        ctx.wait("scalar", b.sV, dvv[db - 1])
                    for half in range(2):
                        ctx.emit("scalar", lambda E, db=db, half=half: E.activation(
                            sp1[:, half * 512:(half + 1) * 512], psq[half][:, :], AF.Abs,
                            bias=dtb_s[:, db:db + 1]))
                        ctx.emit("scalar", lambda E, db=db, half=half: E.activation(
                            sp2[:, half * 512:(half + 1) * 512], psq[half][:, :], AF.Relu,
                            bias=dtb_s[:, db:db + 1]))
                    ctx.emit("scalar", lambda E: E.activation(sp1[:, :], sp1[:, :], AF.Exp,
                                                              scale=-1.0))
                    ctx.emit("scalar", lambda E: E.activation(sp1[:, :], sp1[:, :], AF.Ln,
                                                              bias=1.0))
                    ss = ctx.inc("scalar", b.sS, 1)
                    ctx.wait("tensor", b.sS, ss)
                    ctx.wait("vector", b.sS, ss)
                    ctx.emit("vector", lambda E, db=db: E.scalar_tensor_tensor(
                        delta[:, db * 1024:(db + 1) * 1024], sp1[:, :], 1.0, sp2[:, :],
                        op0=OP.mult, op1=OP.add, accum_out=sdelta[:, db:db + 1]))
                    ctx.wait("vector", b.sG, gv)
                    if db >= 2:
                        ctx.wait("vector", b.stx, b.stx.v)
                    ctx.emit("vector", lambda E, db=db: E.tensor_tensor_scan(
                        cumd_t[db % 2][:, :], onesf_t[:, :],
                        delta[:, db * 1024:(db + 1) * 1024], 0.0, op0=OP.mult, op1=OP.add))
                    vv = ctx.inc("vector", b.sV, 1)
                    dvv.append(vv)
                    ctx.wait("sync", b.sV, vv)
                    ctx.dma("sync", hcumd[:, db * 1024:(db + 1) * 1024],
                            cumd_t[db % 2][:, :], b.stx)
                    ctx.wait("scalar", b.sV, vv)
                    for n in range(16):
                        ctx.emit("scalar", lambda E, db=db, n=n: E.activation(
                            etot[:, db * 16 + n:db * 16 + n + 1], sdelta[:, db:db + 1],
                            AF.Exp, scale=a_s[:, db * 16 + n:db * 16 + n + 1]))
                    ctx.inc("scalar", b.sS, 1)
                ctx.barrier("m_dt")
                sdd.close()
                if p1s < 3:
                    return

                psw = [sd.enter_context(nc.psum_tensor(f"m_psw{i}", [128, 512], F32))
                       for i in range(2)]
                rsel_s = sb3("m_rsel", [32, 32 * 128], BF16)
                dA = [sb3(f"m_dA{i}", [128, 1024], BF16) for i in range(8)]
                ldrs = ctx.sem("m_ldrs")
                ctx.dma("gpsimd", rsel_s[:, :], dram["rsel"], ldrs)
                ctx.wait("tensor", ldrs, 16)
                for n in range(32):
                    dst = bb if n < 16 else cb_t
                    nn = n % 16
                    for half in range(2):
                        ctx.emit("tensor", lambda E, n=n, half=half: E.matmul(
                            psw[half][:, :], rsel_s[:, n * 128:(n + 1) * 128],
                            bcrows_bf[:, half * 512:(half + 1) * 512],
                            start=True, stop=True))
                    vt = ctx.inc("tensor", b.sT, 1)
                    ctx.wait("scalar", b.sT, vt)
                    for half in range(2):
                        ctx.emit("scalar", lambda E, nn=nn, dst=dst, half=half: E.activation(
                            dst[:, nn * 1024 + half * 512:nn * 1024 + (half + 1) * 512],
                            psw[half][:, :], AF.Identity))
                    ss = ctx.inc("scalar", b.sS, 1)
                    ctx.wait("tensor", b.sS, ss)
                ldc2 = ctx.sem("m_ldc2")
                sty = ctx.sem("m_sty")
                sdone = {}
                ctx.wait("vector", b.sS, b.sS.v)
                ctx.wait("vector", ldp, 64)
                for db in range(32):
                    if db >= 2:
                        ctx.wait("sync", b.sV, sdone[(db - 2, 3)])
                    ctx.dma("sync", xcr[db % 2][:, :], hxc[:, db * 1024:(db + 1) * 1024], ldc2)
                    ctx.wait("vector", ldc2, 16 * (db + 1))
                    ctx.emit("vector", lambda E, db=db: E.tensor_tensor(
                        du[:, :], delta[:, db * 1024:(db + 1) * 1024], xcr[db % 2][:, :],
                        op=OP.mult))
                    if db >= 2:
                        ctx.wait("vector", sty, 16 * (db - 1))
                    ctx.emit("vector", lambda E, db=db: E.tensor_scalar(
                        y0[db % 2][:, :], xcr[db % 2][:, :], dvec_s[:, db:db + 1],
                        None, op0=OP.mult))
                    for nb in range(4):
                        if db * 4 + nb >= 2:
                            pb = (db, nb - 2) if nb >= 2 else (db - 1, nb + 2)
                            ctx.wait("scalar", b.sV, sdone[pb])
                        for j in range(4):
                            n = nb * 4 + j
                            ctx.emit("scalar", lambda E, db=db, n=n, nb=nb, j=j: E.activation(
                                dA[(nb % 2) * 4 + j][:, :],
                                delta[:, db * 1024:(db + 1) * 1024], AF.Exp,
                                scale=a_s[:, db * 16 + n:db * 16 + n + 1]))
                        ss = ctx.inc("scalar", b.sS, 1)
                        ctx.wait("vector", b.sS, ss)
                        for j in range(4):
                            n = nb * 4 + j
                            ctx.emit("vector", lambda E, n=n: E.tensor_tensor(
                                wt[:, :], du[:, :], bb[:, n * 1024:(n + 1) * 1024], op=OP.mult))
                            ctx.emit("vector", lambda E, nb=nb, j=j: E.tensor_tensor_scan(
                                ht[:, :], dA[(nb % 2) * 4 + j][:, :], wt[:, :], 0.0,
                                op0=OP.mult, op1=OP.add))
                            ctx.emit("vector", lambda E, db=db, n=n: E.tensor_copy(
                                hlast[:, db * 16 + n:db * 16 + n + 1], ht[:, 1023:1024]))
                            ctx.emit("vector", lambda E, n=n: E.tensor_tensor(
                                tmb[:, :], ht[:, :], cb_t[:, n * 1024:(n + 1) * 1024],
                                op=OP.mult))
                            ctx.emit("vector", lambda E, db=db: E.tensor_tensor(
                                y0[db % 2][:, :], y0[db % 2][:, :], tmb[:, :], op=OP.add))
                        sdone[(db, nb)] = ctx.inc("vector", b.sV, 1)
                    ctx.wait("sync", b.sV, sdone[(db, 3)])
                    ctx.dma("sync", hy0[:, db * 1024:(db + 1) * 1024], y0[db % 2][:, :], sty)
                ctx.barrier("m_scan")
                if p1s < 4:
                    return

            civ = dram["cc_in"].rearrange("o (p f) -> (o p) f", p=128)
            ctx.dma("sync", civ[:, 0:512], etot[:, :], b.stx)
            ctx.dma("sync", civ[:, 512:1024], hlast[:, :], b.stx)
            ctx.wait("gpsimd", b.stx, b.stx.v)
            ccs = ctx.sem("m_ccs")
            ctx.emit("gpsimd", lambda E: E.collective_compute(
                "AllGather", OP.bypass, replica_groups=[list(range(NCORES))],
                ins=[dram["cc_in"].opt()], outs=[dram["cc_out"].opt()]).then_inc(ccs.h, 1),
                fusable=False)
            ccs.v += 1
            ctx.wait("sync", ccs, 1)
            er = [sb2(f"m_er{i}", [128, 512], F32) for i in range(8)]
            hr = [sb2(f"m_hr{i}", [128, 512], F32) for i in range(8)]
            cur = [sb2(f"m_cur{i}", [128, 512], F32) for i in range(2)]
            selm = sb2("m_selm", [128, 8], F32)
            ldcc = ctx.sem("m_ldcc")
            ctx.dma("sync", selm[:, :], dram["selmask"], ldcc)
            for r in range(NCORES):
                view = dram["cc_out"][r:r + 1, :].rearrange("o (p f) -> (o p) f", p=128)
                ctx.dma("sync", er[r][:, :], view[:, 0:512], ldcc)
                ctx.dma("sync", hr[r][:, :], view[:, 512:1024], ldcc)
            ctx.emit("gpsimd", lambda E: E.memset(hin[:, :], 0.0))
            ctx.emit("gpsimd", lambda E: E.memset(cur[0][:, :], 0.0))
            gv2 = ctx.inc("gpsimd", b.sG, 1)
            ctx.wait("vector", b.sG, gv2)
            ctx.wait("vector", ldcc, ldcc.v)
            for r in range(NCORES):
                a, o = cur[r % 2], cur[(r + 1) % 2]
                ctx.emit("vector", lambda E, r=r, a=a: E.scalar_tensor_tensor(
                    hin[:, :], a[:, :], selm[:, r:r + 1], hin[:, :], op0=OP.mult, op1=OP.add))
                ctx.emit("vector", lambda E, r=r, a=a, o=o: E.tensor_tensor(
                    o[:, :], a[:, :], er[r][:, :], op=OP.mult))
                ctx.emit("vector", lambda E, r=r, o=o: E.tensor_tensor(
                    o[:, :], o[:, :], hr[r][:, :], op=OP.add))
            ctx.inc("vector", b.sV, 1)
            ctx.barrier("m_carry")
            if p1s < 5:
                return

            szr = [sb2(f"m_szr{i}", [128, 1024], BF16) for i in range(2)]
            cdr = [sb2(f"m_cdr{i}", [128, 1024], BF16) for i in range(2)]
            y0r = [sb2(f"m_y0r{i}", [128, 1024], BF16) for i in range(2)]
            El = [sb2(f"m_El{i}", [128, 1024], BF16) for i in range(8)]
            y2t = [sb2(f"m_y2t{i}", [128, 1024], BF16) for i in range(2)]
            ldg = ctx.sem("m_ldg")
            sty2 = ctx.sem("m_sty2")
            gdone = {}
            for db in range(32):
                i = db % 2
                if db >= 2:
                    ctx.wait("sync", b.sV, gdone[db - 2])
                ctx.dma("sync", y0r[i][:, :], hy0[:, db * 1024:(db + 1) * 1024], ldg)
                ctx.dma("sync", cdr[i][:, :], hcumd[:, db * 1024:(db + 1) * 1024], ldg)
                ctx.dma("sync", szr[i][:, :], hsz[:, db * 1024:(db + 1) * 1024], ldg)
                ctx.wait("vector", ldg, 48 * (db + 1))
                for nb in range(4):
                    if db * 4 + nb >= 2:
                        pb = (db, nb - 2) if nb >= 2 else (db - 1, nb + 2)
                        ctx.wait("scalar", b.sV, gdone[pb])
                    ctx.wait("scalar", ldg, 48 * (db + 1))
                    for j in range(4):
                        n = nb * 4 + j
                        ctx.emit("scalar", lambda E, db=db, n=n, nb=nb, j=j, i=i: E.activation(
                            El[(nb % 2) * 4 + j][:, :], cdr[i][:, :], AF.Exp,
                            scale=a_s[:, db * 16 + n:db * 16 + n + 1]))
                    ss = ctx.inc("scalar", b.sS, 1)
                    ctx.wait("vector", b.sS, ss)
                    for j in range(4):
                        n = nb * 4 + j
                        sl = (nb % 2) * 4 + j
                        ctx.emit("vector", lambda E, db=db, n=n, sl=sl: E.tensor_scalar(
                            El[sl][:, :], El[sl][:, :],
                            hin[:, db * 16 + n:db * 16 + n + 1], None, op0=OP.mult))
                        ctx.emit("vector", lambda E, n=n, sl=sl: E.tensor_tensor(
                            tmb[:, :], El[sl][:, :], cb_t[:, n * 1024:(n + 1) * 1024],
                            op=OP.mult))
                        ctx.emit("vector", lambda E, db=db, i=i: E.tensor_tensor(
                            y0r[i][:, :], y0r[i][:, :], tmb[:, :], op=OP.add))
                    gdone[(db, nb)] = ctx.inc("vector", b.sV, 1)
                if db >= 2:
                    ctx.wait("vector", sty2, 16 * (db - 1))
                ctx.emit("vector", lambda E, db=db, i=i: E.tensor_tensor(
                    y2t[i][:, :], y0r[i][:, :], szr[i][:, :], op=OP.mult))
                gdone[db] = ctx.inc("vector", b.sV, 1)
                ctx.wait("sync", b.sV, gdone[db])
                ctx.dma("sync", hy2[:, db * 1024:(db + 1) * 1024], y2t[i][:, :], sty2)
        ctx.barrier("m_y2")
        if p1s < 6:
            return

        with ExitStack() as sc:
            sb2 = lambda n, sh, dt: sc.enter_context(nc.sbuf_tensor(n, sh, dt))
            ps1 = lambda n, sh, dt: sc.enter_context(nc.psum_tensor(n, sh, dt))
            y2r = [sb2(f"m_y2r{i}", [128, 1024], BF16) for i in range(2)]
            mo = sb2("m_mo", [128, 16 * 1024], BF16)
            owb = [sb2(f"m_owb{i}", [128, 4096], BF16) for i in range(4)]
            sop = sc.enter_context(ExitStack())
            pso = [sop.enter_context(nc.psum_tensor(f"m_pso{i}", [128, 512], F32))
                   for i in range(8)]
            ldy = ctx.sem("m_ldy")
            ldo = ctx.sem("m_ldo")
            ogdone = ctx.sem("m_ogdone")
            nload = 0
            tdb = []
            for og in range(4):
                for j in range(4):
                    if og >= 1:
                        ctx.wait("gpsimd", ogdone, og)
                    ctx.dma("gpsimd", owb[j][:, :], dram["outw"][og * 4 + j], ldo)
                ctx.wait("tensor", ldo, 16 * 4 * (og + 1))
                for db in range(32):
                    if nload >= 2:
                        ctx.wait("sync", b.sT, tdb[nload - 2])
                    ctx.dma("sync", y2r[db % 2][:, :], dram["hy2"][:, db * 1024:(db + 1) * 1024], ldy)
                    nload += 1
                    ctx.wait("tensor", ldy, 16 * nload)
                    for ob in range(4):
                        for half in range(2):
                            ctx.emit("tensor", lambda E, ob=ob, half=half, db=db, w=owb[ob]:
                                     E.matmul(pso[ob * 2 + half][:, :],
                                              w[:, db * 128:(db + 1) * 128],
                                              y2r[db % 2][:, half * 512:(half + 1) * 512],
                                              start=(db == 0), stop=(db == 31)))
                    tdb.append(ctx.inc("tensor", b.sT, 1))
                vt = b.sT.v
                ctx.wait("scalar", b.sT, vt)
                for ob in range(4):
                    obg = og * 4 + ob
                    for half in range(2):
                        ctx.emit("scalar", lambda E, obg=obg, ob=ob, half=half: E.activation(
                            mo[:, obg * 1024 + half * 512:obg * 1024 + (half + 1) * 512],
                            pso[ob * 2 + half][:, :], AF.Identity))
                ssg = ctx.inc("scalar", b.sS, 1)
                ctx.wait("tensor", b.sS, ssg)
                ctx.inc("scalar", ogdone, 1)
            ctx.barrier("m_op")
            sop.close()

            xn0r = [sb2(f"m_xn0r{i}", [128, 2048], F32) for i in range(2)]
            xr = [sb2(f"m_xr{i}", [128, 2048], F32) for i in range(2)]
            pstr = ps1("m_pstr", [128, 512], BF16)
            ldh = ctx.sem("m_ldh")
            hdone = []
            for tt in range(8):
                i = tt % 2
                if tt >= 2:
                    ctx.wait("sync", b.sV, hdone[tt - 2])
                ctx.dma("sync", xn0r[i][:, :], dram["hxn0"][tt * 128:(tt + 1) * 128, :], ldh)
                ctx.dma("sync", xr[i][:, :], dram["x_sh"][tt * 128:(tt + 1) * 128, :], ldh)
                for ob in range(16):
                    q = ob % 4
                    ctx.emit("tensor", lambda E, ob=ob, tt=tt, q=q: E.matmul(
                        pstr[:, q * 128:(q + 1) * 128],
                        mo[:, ob * 1024 + tt * 128:ob * 1024 + (tt + 1) * 128],
                        b.eye_b[:, :], is_transpose=True, start=True, stop=True))
                    if q == 3:
                        vt = ctx.inc("tensor", b.sT, 1)
                        ctx.wait("vector", b.sT, vt)
                        ctx.wait("vector", ldh, 32 * (tt + 1))
                        for qq in range(4):
                            obq = ob - 3 + qq
                            ctx.emit("vector", lambda E, obq=obq, qq=qq, i=i: E.tensor_tensor(
                                xn0r[i][:, obq * 128:(obq + 1) * 128],
                                xn0r[i][:, obq * 128:(obq + 1) * 128],
                                pstr[:, qq * 128:(qq + 1) * 128], op=OP.add))
                        vv = ctx.inc("vector", b.sV, 1)
                        ctx.wait("tensor", b.sV, vv)
                ctx.wait("scalar", b.sV, b.sV.v)
                if tt >= 1:
                    ctx.wait("scalar", b.stx, b.stx.v)
                sEnd = emit_ln(ctx, b, xn0r[i][:, :], b.scr[:, 0:2048], b.scr, b.stat)
                ctx.wait("vector", b.sS, sEnd)
                ctx.emit("vector", lambda E, i=i: E.tensor_tensor(
                    xr[i][:, :], xr[i][:, :], b.scr[:, 0:2048], op=OP.add))
                vv = ctx.inc("vector", b.sV, 1)
                hdone.append(vv)
                ctx.wait("sync", b.sV, vv)
                ctx.dma("sync", dram["hx1"][tt * 128:(tt + 1) * 128, :], xr[i][:, :], b.stx)


def emit_p2(ctx, nc, b, dram):
    with ExitStack() as st_:
        sbl = lambda n, sh, dt: st_.enter_context(nc.sbuf_tensor(n, sh, dt))
        psl = lambda n, sh, dt: st_.enter_context(nc.psum_tensor(n, sh, dt))
        wqt = sbl("f_wqt", [128, 16 * 2048], BF16)
        x1t = [sbl(f"f_x1t{i}", [128, 2048], F32) for i in range(2)]
        xn2 = sbl("f_xn2", [128, 2048], F32)
        vq = sbl("f_vq", [128, 2048], F32)
        vkeep = sbl("f_vkeep", [128, 2048], F32)
        lad0 = sbl("f_lad0", [128, 2048], F32)
        indt = sbl("f_ind", [128, 2048], F32)
        sgn = sbl("f_sgn", [128, 2048], F32)
        xqb = sbl("f_xqb", [128, 2048], BF16)
        xqT = sbl("f_xqT", [128, 16 * 128], BF16)
        ot = sbl("f_ot", [128, 2048], F32)
        bst = sbl("f_bst", [128, 4], F32)
        ptr = psl("f_ptr", [128, 512], BF16)
        pmm = [psl(f"f_pmm{i}", [128, 512], F32) for i in range(4)]
        ldw = ctx.sem("f_ldw")
        ctx.dma("sync", wqt[:, :], dram["hwqt"], ldw)
        ldx = ctx.sem("f_ldx")
        fdone = []
        for tt in range(8):
            i = tt % 2
            if tt >= 2:
                ctx.wait("sync", b.sV, fdone[tt - 2])
            ctx.dma("sync", x1t[i][:, :], dram["hx1"][tt * 128:(tt + 1) * 128, :], ldx)
            ctx.wait("scalar", ldx, 16 * (tt + 1))
            if tt >= 1:
                ctx.wait("scalar", b.sV, fdone[tt - 1])
                ctx.wait("scalar", b.stx, b.stx.v)
            sEnd = emit_ln(ctx, b, x1t[i][:, :], xn2[:, :], b.scr, b.stat)
            ctx.emit("scalar", lambda E: E.activation(sgn[:, :], xn2[:, :], AF.Sign))
            sSg = ctx.inc("scalar", b.sS, 1)
            ctx.wait("vector", b.sS, sSg)
            ctx.emit("vector", lambda E: E.scalar_tensor_tensor(
                vq[:, :], xn2[:, :], -1.0, xn2[:, :], op0=OP.mult, op1=OP.max))
            ctx.emit("vector", lambda E: E.tensor_scalar(
                lad0[:, :], vq[:, :], 1.0, None, op0=OP.mult, op1=OP.max,
                accum_out=bst[:, 0:1]))
            ctx.emit("vector", lambda E: E.tensor_copy(vkeep[:, 0:16], vq[:, 0:16]))
            ctx.emit("vector", lambda E: E.tensor_scalar(
                bst[:, 1:2], bst[:, 0:1], 1.0 / 127.0, 1e-8, op0=OP.mult, op1=OP.max))
            ctx.emit("vector", lambda E: E.tensor_copy(vkeep[:, 32:48], vq[:, 32:48]))
            ctx.emit("vector", lambda E: E.tensor_copy(vkeep[:, 48:64], vq[:, 48:64]))
            ctx.emit("vector", lambda E: E.reciprocal(bst[:, 2:3], bst[:, 1:2]))
            ctx.emit("vector", lambda E: E.tensor_scalar(
                bst[:, 3:4], bst[:, 1:2], b.ha[:, 0:1], 2.0, op0=OP.mult, op1=OP.mult))
            ctx.emit("vector", lambda E: E.tensor_copy(lad0[:, 0:16], vq[:, 0:16]))
            ctx.emit("vector", lambda E: E.tensor_copy(lad0[:, 16:32], vq[:, 16:32]))
            ctx.emit("vector", lambda E: E.tensor_scalar(
                vq[:, :], vq[:, :], bst[:, 2:3], None, op0=OP.mult))
            ctx.emit("vector", lambda E: E.tensor_copy(vkeep[:, :], vq[:, :]))
            if "hbst" in dram and tt == 0:
                vdmp = ctx.inc("vector", b.sV, 1)
                ctx.wait("sync", b.sV, vdmp)
                ctx.dma("sync", dram["hbst"][:, 0:4], bst[:, 0:4], b.stx)
                ctx.dma("sync", dram["hvq"], vq[:, :], b.stx)
            cur, other = vq, lad0
            for k in (64.0, 32.0, 16.0, 8.0, 4.0, 2.0, 1.0):
                ctx.emit("vector", lambda E, cur=cur, k=k: E.tensor_scalar(
                    indt[:, :], cur[:, :], k, None, op0=OP.is_ge))
                ctx.emit("vector", lambda E, cur=cur, other=other, k=k: E.scalar_tensor_tensor(
                    other[:, :], indt[:, :], -k, cur[:, :], op0=OP.mult, op1=OP.add))
                cur, other = other, cur
            if "hbst" in dram and tt == 0:
                vdmp2 = ctx.inc("vector", b.sV, 1)
                ctx.wait("sync", b.sV, vdmp2)
                ctx.dma("sync", dram["hcur"], cur[:, :], b.stx)
                ctx.wait("vector", b.stx, b.stx.v)
            ctx.emit("vector", lambda E, cur=cur: E.tensor_tensor(
                vkeep[:, :], vkeep[:, :], cur[:, :], op=OP.subtract))
            ctx.emit("vector", lambda E, cur=cur: E.tensor_scalar(
                indt[:, :], cur[:, :], 0.5, None, op0=OP.is_ge))
            ctx.emit("vector", lambda E: E.tensor_tensor(
                vkeep[:, :], vkeep[:, :], indt[:, :], op=OP.add))
            ctx.emit("vector", lambda E: E.tensor_scalar(
                vkeep[:, :], vkeep[:, :], 127.0, None, op0=OP.min))
            ctx.emit("vector", lambda E: E.tensor_tensor(
                xqb[:, :], vkeep[:, :], sgn[:, :], op=OP.mult))
            vq_done = ctx.inc("vector", b.sV, 1)
            if "hxq" in dram:
                ctx.wait("sync", b.sV, vq_done)
                ctx.dma("sync", dram["hxq"][tt * 128:(tt + 1) * 128, :], xqb[:, :], b.stx)
            ctx.wait("tensor", b.sV, vq_done)
            emit_tr_strip(ctx, b, xqb, 16, lambda j: j * 128, xqT, ptr)
            ctx.wait("tensor", ldw, 16)
            for ib in range(16):
                for oc in range(4):
                    ctx.emit("tensor", lambda E, ib=ib, oc=oc: E.matmul(
                        pmm[oc][:, :], xqT[:, ib * 128:(ib + 1) * 128],
                        wqt[:, ib * 2048 + oc * 512:ib * 2048 + (oc + 1) * 512],
                        start=(ib == 0), stop=(ib == 15)))
            vt = ctx.inc("tensor", b.sT, 1)
            ctx.wait("scalar", b.sT, vt)
            for oc in range(4):
                ctx.emit("scalar", lambda E, oc=oc: E.activation(
                    ot[:, oc * 512:(oc + 1) * 512], pmm[oc][:, :], AF.Identity,
                    scale=bst[:, 3:4]))
            sEv = ctx.inc("scalar", b.sS, 1)
            if "hffn" in dram:
                ctx.wait("sync", b.sS, sEv)
                ctx.dma("sync", dram["hffn"][tt * 128:(tt + 1) * 128, :], ot[:, :], b.stx)
                ctx.wait("scalar", b.stx, b.stx.v)
            ctx.wait("tensor", b.sS, sEv)
            sEnd2 = emit_ln(ctx, b, ot[:, :], b.scr[:, 0:2048], b.scr, b.stat)
            ctx.wait("vector", b.sS, sEnd2)
            ctx.emit("vector", lambda E, i=i: E.tensor_tensor(
                x1t[i][:, :], x1t[i][:, :], b.scr[:, 0:2048], op=OP.add))
            vv = ctx.inc("vector", b.sV, 1)
            fdone.append(vv)
            ctx.wait("sync", b.sV, vv)
            ctx.dma("sync", dram["hx2"][tt * 128:(tt + 1) * 128, :], x1t[i][:, :], b.stx)


def emit_p3(ctx, nc, b, dram):
    with ExitStack() as st_:
        sbl = lambda n, sh, dt: st_.enter_context(nc.sbuf_tensor(n, sh, dt))
        psl = lambda n, sh, dt: st_.enter_context(nc.psum_tensor(n, sh, dt))
        yhb = sbl("a_yhb", [128, 16 * 1024], BF16)
        bt_s = sbl("a_bt", [128, 64], F32)
        bq_s = sbl("a_bq", [128, 16], F32)
        bk_s = sbl("a_bk", [128, 16], F32)
        bvrow = sbl("a_bvrow", [1, 2048], BF16)
        bvb = sbl("a_bvb", [128, 16 * 128], BF16)
        borow = sbl("a_borow", [1, 2048], BF16)
        bob = sbl("a_bob", [128, 2048], BF16)
        ldp = ctx.sem("a_ldp")
        ldpg = ctx.sem("a_ldpg")
        ctx.dma("gpsimd", bvrow[:, :], dram["bv"], ldpg)
        ctx.dma("gpsimd", borow[:, :], dram["bo"], ldpg)
        ctx.dma("sync", bt_s[:, :], dram["bias_tab"], ldp)
        ctx.dma("sync", bq_s[:, :], dram["bq"], ldp)
        ctx.dma("sync", bk_s[:, :], dram["bk"], ldp)
        ldp_all = ldp.v
        ctx.wait("scalar", ldp, ldp_all)
        with nc.psum_tensor("a_psb", [128, 512], F32) as psb:
            ctx.wait("tensor", ldp, ldp_all)
            ctx.wait("tensor", ldpg, 32)
            for h in range(16):
                ctx.emit("tensor", lambda E, h=h: E.matmul(
                    psb[:, 0:128], b.ones_b[0:1, :], bvrow[0:1, h * 128:(h + 1) * 128],
                    start=True, stop=True))
                vt = ctx.inc("tensor", b.sT, 1)
                ctx.wait("scalar", b.sT, vt)
                ctx.emit("scalar", lambda E, h=h: E.activation(
                    bvb[:, h * 128:(h + 1) * 128], psb[:, 0:128], AF.Identity))
                ss = ctx.inc("scalar", b.sS, 1)
                ctx.wait("tensor", b.sS, ss)
            for j in range(4):
                ctx.emit("tensor", lambda E, j=j: E.matmul(
                    psb[:, :], b.ones_b[0:1, :], borow[0:1, j * 512:(j + 1) * 512],
                    start=True, stop=True))
                vt = ctx.inc("tensor", b.sT, 1)
                ctx.wait("scalar", b.sT, vt)
                ctx.emit("scalar", lambda E, j=j: E.activation(
                    bob[:, j * 512:(j + 1) * 512], psb[:, :], AF.Identity))
                ss = ctx.inc("scalar", b.sS, 1)
                ctx.wait("tensor", b.sS, ss)

        skv = st_.enter_context(ExitStack())
        qb = skv.enter_context(nc.sbuf_tensor("a_qb", [128, 16 * 1024], BF16))
        kb = skv.enter_context(nc.sbuf_tensor("a_kb", [128, 16 * 1024], BF16))
        vb = skv.enter_context(nc.sbuf_tensor("a_vb", [128, 16 * 1024], BF16))
        with ExitStack() as sc:
            sb2 = lambda n, sh, dt: sc.enter_context(nc.sbuf_tensor(n, sh, dt))
            ps1 = lambda n, sh, dt: sc.enter_context(nc.psum_tensor(n, sh, dt))
            xn3T = sb2("a_xn3T", [128, 16 * 1024], BF16)
            x2t = [sb2("a_x2t0", [128, 2048], F32)]
            xn3 = sb2("a_xn3", [128, 2048], BF16)
            trp = ps1("a_trp", [128, 512], BF16)
            ldx = ctx.sem("a_ldx")
            adone = []
            for tt in range(8):
                i = 0
                if tt >= 1:
                    ctx.wait("sync", b.sS, adone[tt - 1])
                ctx.dma("sync", x2t[i][:, :], dram["hx2"][tt * 128:(tt + 1) * 128, :], ldx)
                ctx.wait("scalar", ldx, 16 * (tt + 1))
                sEnd = emit_ln(ctx, b, x2t[i][:, :], b.scr[:, 0:2048], b.scr, b.stat)
                ctx.emit("scalar", lambda E: E.activation(xn3[:, :], b.scr[:, 0:2048],
                                                          AF.Identity))
                s2 = ctx.inc("scalar", b.sS, 1)
                ctx.wait("tensor", b.sS, s2)
                s3 = emit_tr_strip(ctx, b, xn3, 16,
                                   (lambda tt: (lambda dm: dm * 1024 + tt * 128))(tt),
                                   xn3T, trp)
                adone.append(s3)
            ws_qkv = WStream(ctx, nc, sc, "a_wqkv", 2048, 2)
            for h in range(16):
                ws_qkv.load(dram["wq"][h])
            for h in range(16):
                ws_qkv.load(dram["wk"][h])
            for h in range(16):
                ws_qkv.load(dram["wv"][h])
            psq = [ps1(f"a_psq{i}", [128, 512], F32) for i in range(4)]
            qg_evac = {}
            qg = 0
            for which, (dstb, bias_s) in enumerate(((qb, bq_s), (kb, bk_s))):
                for h in range(16):
                    wb = ws_qkv.use("tensor")
                    for half in range(2):
                        pp = psq[qg % 4]
                        if qg >= 4:
                            ctx.wait("tensor", b.sS, qg_evac[qg - 4])
                        for dm in range(16):
                            ctx.emit("tensor", lambda E, h=h, half=half, dm=dm, w=wb, pp=pp:
                                     E.matmul(
                                         pp[:, :], w[:, dm * 128:(dm + 1) * 128],
                                         xn3T[:, dm * 1024 + half * 512:
                                              dm * 1024 + (half + 1) * 512],
                                         start=(dm == 0), stop=(dm == 15)))
                        vt = ctx.inc("tensor", b.sT, 1)
                        ctx.wait("scalar", b.sT, vt)
                        ctx.emit("scalar", lambda E, h=h, half=half, dstb=dstb,
                                 bias_s=bias_s, pp=pp: E.activation(
                                     dstb[:, h * 1024 + half * 512:h * 1024 + (half + 1) * 512],
                                     pp[:, :], AF.Identity, bias=bias_s[:, h:h + 1]))
                        qg_evac[qg] = ctx.inc("scalar", b.sS, 1)
                        qg += 1
                    ws_qkv.advance(1, b.sT, b.sT.v)
            ctx.wait("tensor", b.sS, b.sS.v)
            psv = [ps1(f"a_psv{i}", [128, 128], F32) for i in range(2)]
            vg_evac = {}
            vgi = 0
            for h in range(16):
                wb = ws_qkv.use("tensor")
                for tt in range(8):
                    pv = psv[vgi % 2]
                    if vgi >= 2:
                        ctx.wait("tensor", b.sV, vg_evac[vgi - 2])
                    for dm in range(16):
                        ctx.emit("tensor", lambda E, h=h, tt=tt, dm=dm, w=wb, pv=pv: E.matmul(
                            pv[:, :], xn3T[:, dm * 1024 + tt * 128:dm * 1024 + (tt + 1) * 128],
                            w[:, dm * 128:(dm + 1) * 128],
                            start=(dm == 0), stop=(dm == 15)))
                    vt = ctx.inc("tensor", b.sT, 1)
                    ctx.wait("vector", b.sT, vt)
                    ctx.emit("vector", lambda E, h=h, tt=tt, pv=pv: E.tensor_tensor(
                        vb[:, h * 1024 + tt * 128:h * 1024 + (tt + 1) * 128],
                        pv[:, :], bvb[:, h * 128:(h + 1) * 128], op=OP.add))
                    vg_evac[vgi] = ctx.inc("vector", b.sV, 1)
                    vgi += 1
                ws_qkv.advance(1, b.sT, b.sT.v)
            ctx.wait("tensor", b.sV, b.sV.v)
        ctx.barrier("a_qkv")

        kvi = dram["kv_in"].rearrange("o (p f) -> (o p) f", p=128)
        ctx.dma("sync", kvi[:, 0:16384], kb[:, :], b.stx)
        ctx.dma("sync", kvi[:, 16384:32768], vb[:, :], b.stx)
        ctx.wait("gpsimd", b.stx, b.stx.v)
        ccs = ctx.sem("a_ccs")
        ctx.emit("gpsimd", lambda E: E.collective_compute(
            "AllGather", OP.bypass, replica_groups=[list(range(NCORES))],
            ins=[dram["kv_in"].opt()], outs=[dram["kv_out"].opt()]).then_inc(ccs.h, 1),
            fusable=False)
        ccs.v += 1
        ctx.wait("sync", ccs, 1)

        with ExitStack() as sc:
            sb2 = lambda n, sh, dt: sc.enter_context(nc.sbuf_tensor(n, sh, dt))
            ps1 = lambda n, sh, dt: sc.enter_context(nc.psum_tensor(n, sh, dt))
            lm_s = sb2("a_lm", [128, 8 * 1024], BF16)
            ldlm = ctx.sem("a_ldlm")
            ctx.dma("gpsimd", lm_s[:, :], dram["lmask"], ldlm)
            ctx.wait("vector", ldlm, 16)
            kg = [sb2(f"a_kg{i}", [128, 1024], BF16) for i in range(2)]
            vg = [sb2(f"a_vg{i}", [128, 1024], BF16) for i in range(2)]
            expst = [sb2(f"a_ex{i}", [128, 1024], BF16) for i in range(4)]
            dnacc = sb2("a_dn", [128, 1024], F32)
            dnb = sb2("a_dnb", [128, 1024], BF16)
            rd = sb2("a_rd", [1, 1024], F32)
            rd2 = sb2("a_rd2", [1, 1024], F32)
            rdb_row = sb2("a_rdbrow", [1, 1024], BF16)
            rdb = sb2("a_rdb", [128, 1024], BF16)
            psc = [ps1(f"a_psc{i}", [128, 1024], F32) for i in range(2)]
            pyh = ps1("a_pyh", [128, 1024], F32)
            pdn = ps1("a_pdn", [1, 1024], F32)
            ldkv = ctx.sem("a_ldkv")
            nkv = 0
            kv_gate = []
            exp_done = {}
            pv_done = {}
            va_done = {}

            def load_rank(g):
                nonlocal nkv
                h, r = g // 8, g % 8
                view = dram["kv_out"][r:r + 1, :].rearrange("o (p f) -> (o p) f", p=128)
                if g >= 2:
                    ctx.wait("sync", b.sT, kv_gate[g - 2])
                ctx.dma("sync", kg[g % 2][:, :], view[:, h * 1024:(h + 1) * 1024], ldkv)
                ctx.dma("sync", vg[g % 2][:, :],
                        view[:, 16384 + h * 1024:16384 + (h + 1) * 1024], ldkv)
                nkv += 1

            load_rank(0)
            load_rank(1)

            def tile_info(h, idx):
                if idx < 8:
                    ksl = kb[:, h * 1024 + idx * 128:h * 1024 + (idx + 1) * 128]
                    vsl = vb[:, h * 1024 + idx * 128:h * 1024 + (idx + 1) * 128]
                    return "L", idx, ksl, vsl, 0.0, None
                g = h * 8 + (idx - 8) // 8
                j2 = (idx - 8) % 8
                ksl = kg[g % 2][:, j2 * 128:(j2 + 1) * 128]
                vsl = vg[g % 2][:, j2 * 128:(j2 + 1) * 128]
                jg = g % 8 * 8 + j2
                return "G", j2, ksl, vsl, bt_s[:, jg:jg + 1], g

            mv_done = {}

            def emit_pv(h, idx):
                kind, j2, ksl, vsl, bias, g = tile_info(h, idx)
                first, last = idx == 0, idx == 71
                if kind == "L":
                    ctx.wait("tensor", b.sV, mv_done[(h, idx)])
                else:
                    ctx.wait("tensor", b.sS, exp_done[(h, idx)])
                tb = idx % 4
                for half in range(2):
                    h0, h1 = half * 512, (half + 1) * 512
                    lo = max(h0, j2 * 128) if kind == "L" else h0
                    if lo >= h1:
                        continue
                    ctx.emit("tensor", lambda E, tb=tb, lo=lo, h1=h1, vsl=vsl,
                             first=first, last=last: E.matmul(
                                 pyh[:, lo:h1], vsl,
                                 expst[tb][:, lo:h1],
                                 start=first, stop=last))
                pv_done[(h, idx)] = ctx.inc("tensor", b.sT, 1)
                if kind == "G" and j2 == 7:
                    kv_gate.append(pv_done[(h, idx)])
                    if nkv < 128:
                        load_rank(nkv)

            for h in range(16):
                for idx in range(72):
                    kind, j2, ksl, vsl, bias, g = tile_info(h, idx)
                    tb, pb = idx % 4, idx % 2
                    if kind == "G" and j2 == 0:
                        ctx.wait("tensor", ldkv, 32 * (g + 1))
                    if idx >= 2:
                        ctx.wait("tensor", b.sS, exp_done[(h, idx - 2)])
                    for half in range(2):
                        h0, h1 = half * 512, (half + 1) * 512
                        lo = max(h0, j2 * 128) if kind == "L" else h0
                        if lo >= h1:
                            continue
                        ctx.emit("tensor", lambda E, pb=pb, lo=lo, h1=h1, ksl=ksl, h=h:
                                 E.matmul(
                            psc[pb][:, lo:h1], ksl,
                            qb[:, h * 1024 + lo:h * 1024 + h1],
                            start=True, stop=True))
                    sct = ctx.inc("tensor", b.sT, 1)
                    ctx.wait("scalar", b.sT, sct)
                    pu = (h, idx - 4) if idx >= 4 else ((h - 1, 68 + idx) if h > 0 else None)
                    if pu is not None:
                        ctx.wait("scalar", b.sT, pv_done[pu])
                        ctx.wait("scalar", b.sV, va_done[pu])
                    ctx.emit("scalar", lambda E, tb=tb, pb=pb, bias=bias: E.activation(
                        expst[tb][:, :], psc[pb][:, :], AF.Exp, scale=ISC, bias=bias))
                    exp_done[(h, idx)] = ctx.inc("scalar", b.sS, 1)
                    if kind == "L":
                        ctx.wait("vector", b.sS, exp_done[(h, idx)])
                        ctx.emit("vector", lambda E, tb=tb, j2=j2,
                                 w=(j2 + 1) * 128: E.tensor_tensor(
                            expst[tb][:, 0:w], expst[tb][:, 0:w],
                            lm_s[:, j2 * 1024:j2 * 1024 + w], op=OP.mult))
                        mv_done[(h, idx)] = ctx.inc("vector", b.sV, 1)
                    else:
                        ctx.wait("vector", b.sS, exp_done[(h, idx)])
                    if idx == 0:
                        ctx.emit("vector", lambda E, tb=tb: E.tensor_copy(
                            dnacc[:, :], expst[tb][:, :]))
                    else:
                        alo = j2 * 128 if kind == "L" else 0
                        ctx.emit("vector", lambda E, tb=tb, alo=alo: E.tensor_tensor(
                            dnacc[:, alo:], dnacc[:, alo:], expst[tb][:, alo:], op=OP.add))
                    va_done[(h, idx)] = ctx.inc("vector", b.sV, 1)
                    if idx >= 1:
                        emit_pv(h, idx - 1)
                emit_pv(h, 71)
                ctx.emit("vector", lambda E: E.tensor_copy(dnb[:, :], dnacc[:, :]))
                dnv = ctx.inc("vector", b.sV, 1)
                ctx.wait("tensor", b.sV, dnv)
                for half in range(2):
                    ctx.emit("tensor", lambda E, half=half: E.matmul(
                        pdn[0:1, half * 512:(half + 1) * 512], b.onescol_b[:, :],
                        dnb[:, half * 512:(half + 1) * 512], start=True, stop=True))
                vtd = ctx.inc("tensor", b.sT, 1)
                ctx.wait("vector", b.sT, vtd)
                vt = pv_done[(h, 71)]
                ctx.wait("vector", b.sT, vt)
                ctx.emit("vector", lambda E: E.tensor_copy(rd[:, :], pdn[0:1, :]))
                ctx.emit("vector", lambda E: E.tensor_copy(rdb_row[:, 0:16], rd[:, 0:16]))
                ctx.emit("vector", lambda E: E.tensor_copy(rdb_row[:, 16:32], rd[:, 16:32]))
                ctx.emit("vector", lambda E: E.reciprocal(rd2[:, :], rd[:, :]))
                ctx.emit("vector", lambda E: E.tensor_copy(rd[:, 0:16], rd[:, 0:16]))
                ctx.emit("vector", lambda E: E.tensor_copy(rd[:, 16:32], rd[:, 16:32]))
                ctx.emit("vector", lambda E: E.tensor_copy(rdb_row[:, :], rd2[:, :]))
                bcv = ctx.inc("vector", b.sV, 1)
                ctx.wait("tensor", b.sV, bcv)
                for half in range(2):
                    ctx.emit("tensor", lambda E, half=half: E.matmul(
                        psc[0][:, half * 512:(half + 1) * 512], b.ones_b[0:1, :],
                        rdb_row[0:1, half * 512:(half + 1) * 512], start=True, stop=True))
                vt2 = ctx.inc("tensor", b.sT, 1)
                ctx.wait("scalar", b.sT, vt2)
                ctx.emit("scalar", lambda E: E.activation(rdb[:, :], psc[0][:, :], AF.Identity))
                ssn = ctx.inc("scalar", b.sS, 1)
                ctx.wait("vector", b.sS, ssn)
                for half in range(2):
                    ctx.emit("vector", lambda E, h=h, half=half: E.tensor_tensor(
                        yhb[:, h * 1024 + half * 512:h * 1024 + (half + 1) * 512],
                        pyh[:, half * 512:(half + 1) * 512],
                        rdb[:, half * 512:(half + 1) * 512], op=OP.mult))
                yv = ctx.inc("vector", b.sV, 1)
                ctx.wait("tensor", b.sV, yv)
        ctx.barrier("a_attn")
        skv.close()

        with ExitStack() as sc:
            sb2 = lambda n, sh, dt: sc.enter_context(nc.sbuf_tensor(n, sh, dt))
            ps1 = lambda n, sh, dt: sc.enter_context(nc.psum_tensor(n, sh, dt))
            ao = sb2("a_ao", [128, 16 * 1024], BF16)
            ws_wo = WStream(ctx, nc, sc, "a_wwo", 2048, 3)
            for ob in range(16):
                ws_wo.load(dram["wo"][ob])
            pwo = [ps1(f"a_pwo{i}", [128, 512], F32) for i in range(2)]
            for ob in range(16):
                wb = ws_wo.use("tensor")
                for half in range(2):
                    for hh in range(16):
                        ctx.emit("tensor", lambda E, ob=ob, half=half, hh=hh, w=wb: E.matmul(
                            pwo[half][:, :], w[:, hh * 128:(hh + 1) * 128],
                            yhb[:, hh * 1024 + half * 512:hh * 1024 + (half + 1) * 512],
                            start=(hh == 0), stop=(hh == 15)))
                vt = ctx.inc("tensor", b.sT, 1)
                ctx.wait("scalar", b.sT, vt)
                for half in range(2):
                    ctx.emit("scalar", lambda E, ob=ob, half=half: E.activation(
                        ao[:, ob * 1024 + half * 512:ob * 1024 + (half + 1) * 512],
                        pwo[half][:, :], AF.Identity))
                ss = ctx.inc("scalar", b.sS, 1)
                ctx.wait("tensor", b.sS, ss)
                ws_wo.advance(1, b.sT, vt)
            x2r = [sb2(f"a_x2r{i}", [128, 2048], F32) for i in range(2)]
            att = sb2("a_att", [128, 2048], F32)
            prev_ln = [0]
            pstr = ps1("a_pstr", [128, 512], BF16)
            ldh = ctx.sem("a_ldh")
            hdone = []
            for tt in range(8):
                i = tt % 2
                if tt >= 2:
                    ctx.wait("sync", b.sV, hdone[tt - 2])
                ctx.dma("sync", x2r[i][:, :], dram["hx2"][tt * 128:(tt + 1) * 128, :], ldh)
                ctx.wait("vector", b.sS, prev_ln[0])
                for ob in range(16):
                    q = ob % 4
                    ctx.emit("tensor", lambda E, ob=ob, tt=tt, q=q: E.matmul(
                        pstr[:, q * 128:(q + 1) * 128],
                        ao[:, ob * 1024 + tt * 128:ob * 1024 + (tt + 1) * 128],
                        b.eye_b[:, :], is_transpose=True, start=True, stop=True))
                    if q == 3:
                        vt = ctx.inc("tensor", b.sT, 1)
                        ctx.wait("vector", b.sT, vt)
                        for qq in range(4):
                            obq = ob - 3 + qq
                            ctx.emit("vector", lambda E, obq=obq, qq=qq: E.scalar_tensor_tensor(
                                att[:, obq * 128:(obq + 1) * 128],
                                pstr[:, qq * 128:(qq + 1) * 128], 1.0,
                                bob[:, obq * 128:(obq + 1) * 128], op0=OP.mult, op1=OP.add))
                        vv = ctx.inc("vector", b.sV, 1)
                        ctx.wait("tensor", b.sV, vv)
                ctx.wait("scalar", b.sV, b.sV.v)
                if tt >= 1:
                    ctx.wait("scalar", b.stx, b.stx.v)
                sEnd = emit_ln(ctx, b, att[:, :], b.scr[:, 0:2048], b.scr, b.stat)
                prev_ln[0] = sEnd
                ctx.wait("vector", b.sS, sEnd)
                ctx.wait("vector", ldh, 16 * (tt + 1))
                ctx.emit("vector", lambda E, i=i: E.tensor_tensor(
                    x2r[i][:, :], x2r[i][:, :], b.scr[:, 0:2048], op=OP.add))
                vv = ctx.inc("vector", b.sV, 1)
                hdone.append(vv)
                ctx.wait("sync", b.sV, vv)
                ctx.dma("sync", dram["hx3"][tt * 128:(tt + 1) * 128, :], x2r[i][:, :], b.stx)


def emit_p4(ctx, nc, b, dram):
    with ExitStack() as st_:
        sbl = lambda n, sh, dt: st_.enter_context(nc.sbuf_tensor(n, sh, dt))
        psl = lambda n, sh, dt: st_.enter_context(nc.psum_tensor(n, sh, dt))
        xn4T = sbl("e_xn4T", [128, 16 * 1024], BF16)
        gb = sbl("e_gb", [128, 8 * 1024], BF16)
        macc = sbl("e_macc", [128, 16 * 1024], F32)
        gstr = sbl("e_gstr", [128, 22 * 1024], BF16)
        gatew_s = sbl("e_gatew", [128, 128], BF16)
        gates = sbl("e_gates", [128, 64], F32)
        gstat = sbl("e_gstat", [128, 48], F32)
        gatesT = sbl("e_gatesT", [8, 1024], F32)
        gatesTb = sbl("e_gatesTb", [8, 1024], BF16)
        rbb = sbl("e_rbb", [128, 8], F32)
        rbrow = sbl("e_rbrow", [1, 8], F32)
        ldp = ctx.sem("e_ldp")
        ldpg = ctx.sem("e_ldpg")
        ctx.dma("gpsimd", gatew_s[:, :], dram["gatew"], ldpg)
        ctx.dma("sync", rbrow[:, :], dram["rbias"], ldp)
        ldp_all = ldp.v

        with ExitStack() as sc:
            sb2 = lambda n, sh, dt: sc.enter_context(nc.sbuf_tensor(n, sh, dt))
            ps1 = lambda n, sh, dt: sc.enter_context(nc.psum_tensor(n, sh, dt))
            x3t = [sb2(f"e_x3t{i}", [128, 2048], F32) for i in range(2)]
            xn4 = sb2("e_xn4", [128, 2048], BF16)
            trp = ps1("e_trp", [128, 512], BF16)
            psr = ps1("e_psr", [128, 8], F32)
            psg = ps1("e_psg", [8, 128], F32)
            psbc2 = ps1("e_psbc2", [128, 512], F32)
            ldx = ctx.sem("e_ldx")
            edone = []
            ctx.wait("tensor", ldp, ldp_all)
            ctx.wait("tensor", ldpg, 16)
            ctx.emit("tensor", lambda E: E.matmul(psbc2[:, 0:8], b.ones_f[0:1, :],
                                                  rbrow[0:1, :], start=True, stop=True))
            vt = ctx.inc("tensor", b.sT, 1)
            ctx.wait("scalar", b.sT, vt)
            ctx.emit("scalar", lambda E: E.activation(rbb[:, :], psbc2[:, 0:8], AF.Identity))
            ssr = ctx.inc("scalar", b.sS, 1)
            ctx.wait("tensor", b.sS, ssr)
            for tt in range(8):
                i = tt % 2
                if tt >= 2:
                    ctx.wait("sync", b.sS, edone[tt - 2])
                ctx.dma("sync", x3t[i][:, :], dram["hx3"][tt * 128:(tt + 1) * 128, :], ldx)
                ctx.wait("scalar", ldx, 16 * (tt + 1))
                if tt >= 1:
                    ctx.wait("scalar", b.stx, b.stx.v)
                sEnd = emit_ln(ctx, b, x3t[i][:, :], b.scr[:, 0:2048], b.scr, b.stat)
                ctx.emit("scalar", lambda E: E.activation(xn4[:, :], b.scr[:, 0:2048],
                                                          AF.Identity))
                s2 = ctx.inc("scalar", b.sS, 1)
                ctx.wait("tensor", b.sS, s2)
                emit_tr_strip(ctx, b, xn4, 16,
                              (lambda tt: (lambda dm: dm * 1024 + tt * 128))(tt),
                              xn4T, trp)
                for dm in range(16):
                    ctx.emit("tensor", lambda E, tt=tt, dm=dm: E.matmul(
                        psr[:, :], xn4T[:, dm * 1024 + tt * 128:dm * 1024 + (tt + 1) * 128],
                        gatew_s[:, dm * 8:(dm + 1) * 8], start=(dm == 0), stop=(dm == 15)))
                vt = ctx.inc("tensor", b.sT, 1)
                ctx.wait("scalar", b.sT, vt)
                ctx.emit("scalar", lambda E, tt=tt: E.activation(
                    gates[:, tt * 8:(tt + 1) * 8], psr[:, :], AF.Tanh, scale=0.5))
                ss = ctx.inc("scalar", b.sS, 1)
                edone.append(ss)
                ctx.wait("tensor", b.sS, ss)
            sg_all = ctx.inc("scalar", b.sS, 0) if False else b.sS.v
            ctx.wait("vector", b.sS, sg_all)
            G = gstat
            for tt in range(8):
                ctx.emit("vector", lambda E, tt=tt: E.tensor_scalar(
                    gates[:, tt * 8:(tt + 1) * 8], gates[:, tt * 8:(tt + 1) * 8],
                    0.5, 0.5, op0=OP.mult, op1=OP.add))
            for tt in range(8):
                ctx.emit("vector", lambda E, tt=tt: E.tensor_tensor(
                    b.scr[:, tt * 8:tt * 8 + 8], gates[:, tt * 8:(tt + 1) * 8],
                    rbb[:, :], op=OP.add))
            for tt in range(8):
                ctx.emit("vector", lambda E, tt=tt: E.tensor_reduce(
                    G[:, tt:tt + 1], b.scr[:, tt * 8:tt * 8 + 8], AX.X, OP.max))
            for tt in range(8):
                ctx.emit("vector", lambda E, tt=tt: E.tensor_scalar(
                    b.scr[:, 64 + tt * 8:64 + tt * 8 + 8], b.scr[:, tt * 8:tt * 8 + 8],
                    G[:, tt:tt + 1], None, op0=OP.is_ge))
            for tt in range(8):
                ctx.emit("vector", lambda E, tt=tt: E.scalar_tensor_tensor(
                    b.scr[:, 64 + tt * 8:64 + tt * 8 + 8],
                    b.scr[:, 64 + tt * 8:64 + tt * 8 + 8], NEG,
                    b.scr[:, tt * 8:tt * 8 + 8], op0=OP.mult, op1=OP.add))
            for tt in range(8):
                ctx.emit("vector", lambda E, tt=tt: E.tensor_reduce(
                    G[:, 8 + tt:9 + tt], b.scr[:, 64 + tt * 8:64 + tt * 8 + 8], AX.X, OP.max))
            for tt in range(8):
                ctx.emit("vector", lambda E, tt=tt: E.tensor_scalar(
                    b.scr[:, tt * 8:tt * 8 + 8], b.scr[:, tt * 8:tt * 8 + 8],
                    G[:, 8 + tt:9 + tt], None, op0=OP.is_ge))
            for tt in range(8):
                ctx.emit("vector", lambda E, tt=tt: E.tensor_tensor(
                    gates[:, tt * 8:(tt + 1) * 8], gates[:, tt * 8:(tt + 1) * 8],
                    b.scr[:, tt * 8:tt * 8 + 8], op=OP.mult))
            for tt in range(8):
                ctx.emit("vector", lambda E, tt=tt: E.tensor_reduce(
                    G[:, 16 + tt:17 + tt], gates[:, tt * 8:(tt + 1) * 8], AX.X, OP.add))
            for tt in range(8):
                ctx.emit("vector", lambda E, tt=tt: E.tensor_scalar(
                    G[:, 24 + tt:25 + tt], G[:, 16 + tt:17 + tt], 1e-12, None, op0=OP.max))
            for tt in range(8):
                ctx.emit("vector", lambda E, tt=tt: E.reciprocal(
                    G[:, 32 + tt:33 + tt], G[:, 24 + tt:25 + tt]))
            for tt in range(8):
                ctx.emit("vector", lambda E, tt=tt: E.tensor_scalar(
                    gates[:, tt * 8:(tt + 1) * 8], gates[:, tt * 8:(tt + 1) * 8],
                    G[:, 32 + tt:33 + tt], None, op0=OP.mult))
            gv = ctx.inc("vector", b.sV, 1)
            ctx.wait("tensor", b.sV, gv)
            for tt in range(8):
                ctx.emit("tensor", lambda E, tt=tt: E.matmul(
                    psg[:, :], gates[:, tt * 8:(tt + 1) * 8], b.eye_f[:, :],
                    is_transpose=True, start=True, stop=True))
                vt = ctx.inc("tensor", b.sT, 1)
                ctx.wait("scalar", b.sT, vt)
                ctx.emit("scalar", lambda E, tt=tt: E.activation(
                    gatesT[:, tt * 128:(tt + 1) * 128], psg[:, :], AF.Identity))
                ss = ctx.inc("scalar", b.sS, 1)
                ctx.wait("tensor", b.sS, ss)
            ctx.wait("vector", b.sS, b.sS.v)
            ctx.emit("vector", lambda E: E.tensor_copy(gatesTb[:, :], gatesT[:, :]))
            gtv = ctx.inc("vector", b.sV, 1)
            ctx.wait("tensor", b.sV, gtv)
            esel_s = sb2("e_esel", [8, 8 * 128], BF16)
            ldes = ctx.sem("e_ldes")
            ctx.dma("gpsimd", esel_s[:, :], dram["esel"], ldes)
            ctx.wait("tensor", ldes, 16)
            for e in range(8):
                for half in range(2):
                    ctx.emit("tensor", lambda E, e=e, half=half: E.matmul(
                        psbc2[:, :], esel_s[:, e * 128:(e + 1) * 128],
                        gatesTb[:, half * 512:(half + 1) * 512], start=True, stop=True))
                    vt = ctx.inc("tensor", b.sT, 1)
                    ctx.wait("scalar", b.sT, vt)
                    ctx.emit("scalar", lambda E, e=e, half=half: E.activation(
                        gb[:, e * 1024 + half * 512:e * 1024 + (half + 1) * 512],
                        psbc2[:, :], AF.Identity))
                    ss = ctx.inc("scalar", b.sS, 1)
                    ctx.wait("tensor", b.sS, ss)

        ctx.barrier("e_router")
        with ExitStack() as sc:
            sb2 = lambda n, sh, dt: sc.enter_context(nc.sbuf_tensor(n, sh, dt))
            ps1 = lambda n, sh, dt: sc.enter_context(nc.psum_tensor(n, sh, dt))
            s1b = [sb2(f"e_s1b{i}", [128, 1024], BF16) for i in range(2)]
            p4 = [ps1(f"e_p4_{i}", [128, 512], F32) for i in range(8)]
            ws = WStream(ctx, nc, sc, "e_ws", 2816, 4)
            for i2 in range(22):
                ws.load(dram["sw1"][i2])
                ws.load(dram["sw3"][i2])
            for ob in range(16):
                ws.load(dram["sw2"][ob])
            for e in range(8):
                for ib in range(11):
                    ws.load(dram["w1"][e, ib])
                    ws.load(dram["w3"][e, ib])
                for ob in range(16):
                    ws.load(dram["w2"][e, ob])

            def ffn_block(nblk, gate_e):
                vdone = {}
                for ib in range(nblk):
                    pr = ib % 2
                    ph1 = [p4[pr * 4 + 0], p4[pr * 4 + 1]]
                    ph3 = [p4[pr * 4 + 2], p4[pr * 4 + 3]]
                    w1b = ws.peek("tensor", 0)
                    w3b = ws.peek("tensor", 1)
                    if ib >= 2:
                        ctx.wait("tensor", b.sV, vdone[ib - 2])
                    for half in range(2):
                        for dm in range(16):
                            ctx.emit("tensor", lambda E, half=half, dm=dm, w=w1b, p=ph1:
                                     E.matmul(p[half][:, :], w[:, dm * 128:(dm + 1) * 128],
                                              xn4T[:, dm * 1024 + half * 512:
                                                   dm * 1024 + (half + 1) * 512],
                                              start=(dm == 0), stop=(dm == 15)))
                        for dm in range(16):
                            ctx.emit("tensor", lambda E, half=half, dm=dm, w=w3b, p=ph3:
                                     E.matmul(p[half][:, :], w[:, dm * 128:(dm + 1) * 128],
                                              xn4T[:, dm * 1024 + half * 512:
                                                   dm * 1024 + (half + 1) * 512],
                                              start=(dm == 0), stop=(dm == 15)))
                    vt = ctx.inc("tensor", b.sT, 1)
                    ws.advance(2, b.sT, vt)
                    ctx.wait("scalar", b.sT, vt)
                    if ib >= 2:
                        ctx.wait("scalar", b.sV, vdone[ib - 2])
                    for half in range(2):
                        ctx.emit("scalar", lambda E, half=half, pr=pr, p=ph1: E.activation(
                            s1b[pr][:, half * 512:(half + 1) * 512], p[half][:, :], AF.Silu))
                    ss = ctx.inc("scalar", b.sS, 1)
                    ctx.wait("vector", b.sS, ss)
                    for half in range(2):
                        ctx.emit("vector", lambda E, ib=ib, half=half, pr=pr, p=ph3:
                                 E.tensor_tensor(
                                     gstr[:, ib * 1024 + half * 512:ib * 1024 + (half + 1) * 512],
                                     s1b[pr][:, half * 512:(half + 1) * 512],
                                     p[half][:, :], op=OP.mult))
                    if gate_e is not None:
                        ctx.emit("vector", lambda E, ib=ib, e=gate_e: E.tensor_tensor(
                            gstr[:, ib * 1024:(ib + 1) * 1024],
                            gstr[:, ib * 1024:(ib + 1) * 1024],
                            gb[:, e * 1024:(e + 1) * 1024], op=OP.mult))
                    vdone[ib] = ctx.inc("vector", b.sV, 1)
                ctx.wait("tensor", b.sV, vdone[nblk - 1])

            def w2_pass(nblk, first):
                edone = {}
                for ob in range(16):
                    pr = ob % 2
                    pw = [p4[pr * 4 + 0], p4[pr * 4 + 1]]
                    w2b = ws.use("tensor")
                    if ob >= 2:
                        ctx.wait("tensor", b.sV if not first else b.sS, edone[ob - 2])
                    for half in range(2):
                        for ib in range(nblk):
                            ctx.emit("tensor", lambda E, half=half, ib=ib, w=w2b, p=pw:
                                     E.matmul(p[half][:, :], w[:, ib * 128:(ib + 1) * 128],
                                              gstr[:, ib * 1024 + half * 512:
                                                   ib * 1024 + (half + 1) * 512],
                                              start=(ib == 0), stop=(ib == nblk - 1)))
                    vt = ctx.inc("tensor", b.sT, 1)
                    ws.advance(1, b.sT, vt)
                    if first:
                        ctx.wait("scalar", b.sT, vt)
                        for half in range(2):
                            ctx.emit("scalar", lambda E, ob=ob, half=half, p=pw: E.activation(
                                macc[:, ob * 1024 + half * 512:ob * 1024 + (half + 1) * 512],
                                p[half][:, :], AF.Identity))
                        edone[ob] = ctx.inc("scalar", b.sS, 1)
                    else:
                        ctx.wait("vector", b.sT, vt)
                        for half in range(2):
                            ctx.emit("vector", lambda E, ob=ob, half=half, p=pw: E.tensor_tensor(
                                macc[:, ob * 1024 + half * 512:ob * 1024 + (half + 1) * 512],
                                macc[:, ob * 1024 + half * 512:ob * 1024 + (half + 1) * 512],
                                p[half][:, :], op=OP.add))
                        edone[ob] = ctx.inc("vector", b.sV, 1)
                ctx.wait("tensor", b.sS if first else b.sV,
                         edone[15])

            ffn_block(22, None)
            w2_pass(22, True)
            for e in range(8):
                ffn_block(11, e)
                w2_pass(11, False)
        ctx.barrier("e_acc")

        with ExitStack() as sc:
            sb2 = lambda n, sh, dt: sc.enter_context(nc.sbuf_tensor(n, sh, dt))
            ps1 = lambda n, sh, dt: sc.enter_context(nc.psum_tensor(n, sh, dt))
            x3r = [sb2(f"e_x3r{i}", [128, 2048], F32) for i in range(2)]
            mot = sb2("e_mot", [128, 2048], F32)
            pstr = ps1("e_pstr", [128, 512], F32)
            ldh = ctx.sem("e_ldh")
            sto = ctx.sem("e_sto")
            hdone = []
            for tt in range(8):
                i = tt % 2
                if tt >= 2:
                    ctx.wait("sync", b.sV, hdone[tt - 2])
                ctx.dma("sync", x3r[i][:, :], dram["hx3"][tt * 128:(tt + 1) * 128, :], ldh)
                for ob in range(16):
                    q = ob % 4
                    ctx.emit("tensor", lambda E, ob=ob, tt=tt, q=q: E.matmul(
                        pstr[:, q * 128:(q + 1) * 128],
                        macc[:, ob * 1024 + tt * 128:ob * 1024 + (tt + 1) * 128],
                        b.eye_f[:, :], is_transpose=True, start=True, stop=True))
                    if q == 3:
                        vt = ctx.inc("tensor", b.sT, 1)
                        ctx.wait("scalar", b.sT, vt)
                        for qq in range(4):
                            obq = ob - 3 + qq
                            ctx.emit("scalar", lambda E, obq=obq, qq=qq: E.activation(
                                mot[:, obq * 128:(obq + 1) * 128],
                                pstr[:, qq * 128:(qq + 1) * 128], AF.Identity))
                        ss = ctx.inc("scalar", b.sS, 1)
                        ctx.wait("tensor", b.sS, ss)
                if tt >= 1:
                    ctx.wait("scalar", b.stx, b.stx.v)
                sEnd = emit_ln(ctx, b, mot[:, :], b.scr[:, 0:2048], b.scr, b.stat)
                ctx.wait("vector", b.sS, sEnd)
                ctx.wait("vector", ldh, 16 * (tt + 1))
                ctx.emit("vector", lambda E, i=i: E.tensor_tensor(
                    x3r[i][:, :], x3r[i][:, :], b.scr[:, 0:2048], op=OP.add))
                vv = ctx.inc("vector", b.sV, 1)
                hdone.append(vv)
                ctx.wait("sync", b.sV, vv)
                ctx.dma("sync", dram["out"][tt * 128:(tt + 1) * 128, :], x3r[i][:, :], sto)
            ctx.wait("gpsimd", sto, 16 * 8)


_CACHE = {}


def kernel(**inputs):
    x = inputs["L0_attn_hidden_states"].astype(np.float32, copy=False)
    wp = _prep_weights(inputs)
    stru = _structural()
    if "nc" not in _CACHE:
        _CACHE["nc"] = build()
    nc = _CACHE["nc"]
    in_maps = []
    for c in range(NCORES):
        m = dict(wp)
        m.update(_prep_core(c, x))
        m.update(stru)
        in_maps.append(m)
    res = run_bass_kernel_spmd(nc, in_maps, core_ids=list(range(NCORES)))
    outs = [res.results[c]["out"] for c in range(NCORES)]
    return np.concatenate(outs, axis=0).reshape(1, T, DM)
